# version 09
# baseline (speedup 1.0000x reference)
import numpy as np
import concourse.bass as bass
import concourse.mybir as mybir
from concourse.bass_utils import run_bass_kernel_spmd

F32 = mybir.dt.float32
F32R = mybir.dt.float32r
BF16 = mybir.dt.bfloat16
FP8 = mybir.dt.float8e4
AF = mybir.ActivationFunctionType
AL = mybir.AluOpType
AX = mybir.AxisListType.X

NCORES = 8
T = 1024
D = 2048
KD = 16
NT = 512
NTB = T // NT
DKH = 512
NH = 3
CH = 128
NCHK = T // CH
DI = 6144
MI = DI // 128
EPS = 1e-5
BIG = 1.0e30

KB = 1024


def _retile(w):
    din, dout = w.shape
    K, M = din // 128, dout // 128
    return np.ascontiguousarray(
        w.reshape(K, 128, M, 128).transpose(2, 1, 0, 3)).reshape(M, 128, K * 128)


def _cols(v):
    return np.ascontiguousarray(v.reshape(-1, 128).T)



class PB:
    ENGS = ("t", "v", "s", "g", "y")
    DQ = {"dg": ("dg", "dg2"), "dy": ("dy", "dy2")}

    def __init__(self):
        self.ops = {e: [] for e in self.ENGS}
        self.sem = {}
        self.n = {"dg": 0, "dy": 0, "st": 0, "sv": 0, "ss": 0, "sg": 0, "cc": 0}
        self.di = {"dg": 0, "dy": 0}

    def raw(self, e, fn):
        self.ops[e].append(fn)

    def _wait1(self, e, q, v):
        if v <= 0:
            return
        self.raw(e, lambda eng, q=q, v=v: eng.wait_ge(self.sem[q], v))

    def wait(self, e, q, v):
        if v <= 0:
            return
        if q in self.DQ:
            i = v // 16
            a, b = self.DQ[q]
            self._wait1(e, a, 16 * ((i + 1) // 2))
            self._wait1(e, b, 16 * (i // 2))
        else:
            self._wait1(e, q, v)

    def run(self, e, fn, inc=None):
        if inc is None:
            self.raw(e, fn)
            return self.n.get(None, 0)
        self.n[inc] += 1
        v = self.n[inc]
        self.raw(e, lambda eng, fn=fn, q=inc: fn(eng).then_inc(self.sem[q], 1))
        return v

    def dma(self, e, q, out, in_):
        self.di[q] += 1
        i = self.di[q]
        a, b = self.DQ[q]
        sem = a if i % 2 == 1 else b
        self._wait1(e, sem, 16 * ((i + 1) // 2 - 1))
        self.raw(e, lambda eng, out=out, in_=in_, s=sem:
                 eng.dma_start(out=out, in_=in_).then_inc(self.sem[s], 16))
        self.n[q] = 16 * i
        return 16 * i


def build_program(dbg=()):
    nc = bass.Bass("TRN2", target_bir_lowering=False)
    p = PB()

    dx = nc.dram_tensor("x", [KD, 128, T], F32R, kind="ExternalInput").ap()
    dout = nc.dram_tensor("out", [KD, 128, T], F32R, kind="ExternalOutput")

    dWq = nc.dram_tensor("Wq", [4, 128, D], F32R, kind="ExternalInput")
    dWk = nc.dram_tensor("Wk", [NH, 4, 128, D], F32R, kind="ExternalInput")
    dWv = nc.dram_tensor("Wv", [NH, 4, 128, D], F32R, kind="ExternalInput")
    dWo = nc.dram_tensor("Wo", [16, 128, DKH], F32, kind="ExternalInput")
    dWg = nc.dram_tensor("Wg", [128, KD * 4], F32R, kind="ExternalInput")
    dDL = nc.dram_tensor("decay_logits", [1, 4], F32, kind="ExternalInput").ap()
    dOS = nc.dram_tensor("out_scale", [1, 1], F32R, kind="ExternalInput").ap()
    dWb = nc.dram_tensor("Wbase", [16, 128, D], F32R, kind="ExternalInput")
    dA = nc.dram_tensor("loraA", [128, D], F32R, kind="ExternalInput")
    dB = nc.dram_tensor("loraB", [128, 16 * 128], F32R, kind="ExternalInput")
    dRt = nc.dram_tensor("router", [128, KD * 8], F32R, kind="ExternalInput")
    dW1 = nc.dram_tensor("W1", [MI, 128, D], F32, kind="ExternalInput")
    dB1 = nc.dram_tensor("b1", [128, MI], F32, kind="ExternalInput").ap()
    dGW = nc.dram_tensor("gatew", [MI, 128, D], F32, kind="ExternalInput")
    dUW = nc.dram_tensor("upw", [MI, 128, D], F32, kind="ExternalInput")
    dDW = nc.dram_tensor("downw", [16, 128, DI], F32, kind="ExternalInput")
    dGS = nc.dram_tensor("gates", [128, MI], F32, kind="ExternalInput").ap()
    dUS = nc.dram_tensor("ups", [128, MI], F32, kind="ExternalInput").ap()
    dDS = nc.dram_tensor("downs", [128, 16], F32, kind="ExternalInput").ap()

    dID = nc.dram_tensor("identb", [128, 128], BF16, kind="ExternalInput").ap()
    dIDr = nc.dram_tensor("identr", [128, 128], F32R, kind="ExternalInput").ap()
    dEXP = nc.dram_tensor("expnT", [128, 128], F32, kind="ExternalInput").ap()
    dCAU = nc.dram_tensor("causT", [128, 128], F32, kind="ExternalInput").ap()
    dE1 = nc.dram_tensor("e1row", [1, T], F32R, kind="ExternalInput").ap()
    dE2 = nc.dram_tensor("e2row", [1, T], F32R, kind="ExternalInput").ap()
    dIOT = nc.dram_tensor("iotac", [128, 2], F32, kind="ExternalInput").ap()
    dONE = nc.dram_tensor("onesc", [128, 2], F32R, kind="ExternalInput").ap()
    dONE1 = nc.dram_tensor("ones1", [1, 128], F32R, kind="ExternalInput").ap()
    dONE1b = nc.dram_tensor("ones1b", [1, 128], BF16, kind="ExternalInput").ap()
    dPM = nc.dram_tensor("pmat", [8, 128], BF16, kind="ExternalInput").ap()
    dSH = nc.dram_tensor("selh", [3, NH * 128], BF16, kind="ExternalInput").ap()
    dCN = nc.dram_tensor("cnsts", [128, 2], F32, kind="ExternalInput").ap()
    dBEX = nc.dram_tensor("bexpo", [1, 8], F32, kind="ExternalInput").ap()
    dBMK = nc.dram_tensor("bmask", [1, 8], F32, kind="ExternalInput").ap()

    dQ8g = nc.dram_tensor("q8g", [MI, 128, D], FP8)
    dQ8u = nc.dram_tensor("q8u", [MI, 128, D], FP8)
    dQ8d = nc.dram_tensor("q8d", [16, 128, DI], FP8)
    dSpill = nc.dram_tensor("spill", [128, KD * T], F32R)
    dBin = nc.dram_tensor("bstates_in", [NH * 128, 4 * DKH], BF16)
    dBout = nc.dram_tensor("bstates_out", [NCORES * NH * 128, 4 * DKH], BF16)
    dRin = nc.dram_tensor("rstats_in", [128, 48], F32)
    dRout = nc.dram_tensor("rstats_out", [128, 48], F32)

    dbg_t = {}
    _f32r_taps = {"ln1", "x1", "x2", "x3", "axt"}
    for name in dbg:
        shp = dict(ln1=[2, T], qT=[128, 4 * T], kT=[128, 4 * T], vTM=[128, 8 * DKH],
                   gate=[3, T], blend=[128, 4 * T], ao=[128, KD * T], x1=[128, KD * T],
                   x2=[128, KD * T], x3=[128, KD * T], wd=[8, T], axt=[128, T],
                   xn5=[128, KD * T], qL1=[128, KD * T], cfac=[128, 16],
                   xn6=[128, KD * T], fuse=[128, 4 * T], ffo=[128, KD * T],
                   Sst=[128, 4 * DKH], Ebn=[128, 4 * DKH],
                   qg=[128, 4 * T], qgd=[128, 4 * T], kdTM=[128, 8 * DKH],
                   blend0=[128, 4 * T], blendi=[128, 4 * T],
                   logd=[128, 4], djc=[128, 3], dCc=[128, 3],
                   gateb=[128, T])[name]
        dt_ = F32R if name in _f32r_taps else F32
        dbg_t[name] = nc.dram_tensor("dbg_" + name, shp, dt_, kind="ExternalOutput")

    import contextlib
    stack = contextlib.ExitStack()
    arena_guard = stack.enter_context(nc.sbuf_tensor("arena", [128, 192 * KB],
                                                     mybir.dt.uint8))

    smalls = {}
    def small(name, shape, dtype):
        h = stack.enter_context(nc.sbuf_tensor(name, shape, dtype))
        smalls[name] = h
        return h

    rowA = small("rowA", [1, T], F32R)
    rowB = small("rowB", [1, T], F32R)
    rowC = small("rowC", [1, T], F32R)
    csq = small("csq", [128, 64], F32)
    rcol = small("rcol", [128, 8], F32)
    zcol = small("zcol", [128, 8], F32)
    logd = small("logd", [128, 4], F32)
    djc = small("djc", [128, 4], F32)
    dCc = small("dCc", [128, 4], F32)
    wcs = small("wcs", [128, 8], F32)
    osc = small("oscol", [128, 1], F32)
    identb = small("identb_s", [128, 128], BF16)
    identr = small("identr_s", [128, 128], F32R)
    onesc = small("onesc_s", [128, 2], F32R)
    cnst = small("cnst_s", [128, 2], F32)
    nc.const_aps.register(F32, EPS, cnst[:, 0:1])
    nc.const_aps.register(F32, -0.5, cnst[:, 1:2])
    onescb = small("onescb", [128, 2], BF16)
    iotac = small("iotac_s", [128, 2], F32)
    ones1 = small("ones1_s", [1, 128], F32R)
    ones1b = small("ones1b_s", [1, 128], BF16)
    pmat = small("pmat_s", [8, 128], BF16)
    dlrow = small("dlrow", [1, 16], F32)
    dlogr = small("dlogr", [1, 4], F32R)
    oscs = small("oscs", [1, 2], F32R)
    bxrow = small("bxrow", [1, 16], F32)
    wgs = small("wgs", [128, KD * 4], F32R)
    rts = small("rts", [128, KD * 8], F32R)

    base_addr = nc.lookup_mloc(arena_guard).addr
    alias_n = [0]

    def alias(shape, dtype, off_kb):
        alias_n[0] += 1
        return nc.alloc_sbuf_tensor_at(
            "al%d" % alias_n[0], shape, dtype, offset=base_addr + int(off_kb * KB))

    lA = alias([128, KD * 128], F32R, 128)
    lB = alias([128, 16 * 128], F32R, 136)
    axt = alias([128, T], F32, 144)
    axw = alias([128, T], F32R, 148)
    expnT = alias([128, 128], F32, 96)
    causT = alias([128, 128], F32, 96.5)
    gtT = alias([3, T], BF16, 97)
    maskb = alias([128, NH * 128], BF16, 99)
    selh = alias([3, NH * 128], BF16, 99.75)
    wdt = alias([8, T], BF16, 84)
    glog = alias([128, 8 * 4], F32, 86)
    gmx = alias([128, 8 * 4], F32, 86.25)
    glog2 = alias([128, 8 * 4], F32, 87)
    rlog2 = alias([128, 8 * 8], F32, 87.25)
    rlog = alias([128, 8 * 8], F32, 86.5)
    rtop = alias([128, 8 * 4], F32, 86.75)
    acc = alias([128, 96], F32, 178)
    gst = alias([128, 64], F32, 178.5)
    cbf = alias([128, 32], BF16, 178.75)
    b1c = alias([128, MI], F32, 179)
    gsc = alias([128, MI], F32, 179.25)
    usc = alias([128, MI], F32, 179.5)
    dsc = alias([128, 16], F32, 179.75)

    resid = alias([128, KD, T], F32R, 0)
    fuse = alias([128, MI, T], BF16, 0)
    scrA = alias([128, 2, T], F32R, 64)
    vFM = alias([128, 4, T], BF16, 136)
    bc = [alias([128, T], F32R, 72 + 4 * i) for i in range(3)]
    Sbf = alias([128, 4, DKH], BF16, 84)
    bstmp = alias([128, 4 * DKH], BF16, 84)
    Ebn = alias([128, 4, DKH], BF16, 88)
    sp92 = alias([128, 2 * T], BF16, 92)
    ao = alias([128, KD, T], BF16, 96)
    qT = alias([128, 4, T], BF16, 128)
    qg = alias([128, 4, T], BF16, 136)
    qgd = alias([128, 4, T], BF16, 144)
    kT = alias([128, 4, T], BF16, 152)
    kdTM = alias([128, NCHK, DKH], BF16, 160)
    vTM = alias([128, NCHK, DKH], BF16, 168)
    blendT = alias([128, 4, T], BF16, 176)
    wst = alias([128, 2, 8 * 128], F32R, 184)
    xn5 = alias([128, KD, T], BF16, 128)
    kscr = alias([128, T], BF16, 160)
    vscr = alias([128, T], BF16, 164)
    wbl1 = alias([128, 2, 8 * 128], BF16, 184)
    xn6 = alias([128, KD, T], BF16, 128)
    gwb = alias([128, 2, D], BF16, 160)
    uwb = alias([128, 2, D], BF16, 168)
    sgs = alias([128, NT], F32, 176)
    dwb = alias([128, 2, MI * 128], BF16, 128)

    pb = [stack.enter_context(nc.psum_tensor("pb%d" % i, [128, NT], F32))
          for i in range(6)]
    ptb = stack.enter_context(nc.psum_tensor("ptb", [128, 4 * 128], BF16))
    ptr = stack.enter_context(nc.psum_tensor("ptr", [128, 128], F32R))

    sems = {}
    for q in ("dg", "dg2", "dy", "dy2", "st", "sv", "ss", "sg", "cc"):
        sems[q] = stack.enter_context(nc.semaphore(q))
    p.sem = {q: sems[q] for q in sems}

    bankuser = {}

    def bank_wait(e, i):
        u = bankuser.get(i)
        if u:
            p.wait(e, u[0], u[1])

    def bank_set(i, q, v):
        bankuser[i] = (q, v)

    def vcopy(out, in_, inc="sv", e="v"):
        return p.run(e, lambda eng, o=out, x=in_: eng.tensor_copy(o, x), inc=inc)

    def vtt(out, a, b, op, inc="sv", e="v"):
        return p.run(e, lambda eng, o=out, a=a, b=b, op=op:
                     eng.tensor_tensor(o, a, b, op), inc=inc)

    def vts(out, a, s1, op, s2=None, op2=AL.bypass, inc="sv", e="v"):
        if s2 is None:
            return p.run(e, lambda eng, o=out, a=a, s1=s1, op=op:
                         eng.tensor_single_scalar(o, a, s1, op), inc=inc)
        return p.run(e, lambda eng, o=out, a=a, s1=s1, s2=s2, op=op, op2=op2:
                     eng.tensor_scalar(o, a, s1, s2, op, op2), inc=inc)

    def vstt(out, in0, sc, op0, in1, op1, inc="sv", e="v"):
        return p.run(e, lambda eng, o=out, a=in0, s=sc, b=in1, op0=op0, op1=op1:
                     eng.scalar_tensor_tensor(o, a, s, b, op0, op1), inc=inc)

    def vbounce():
        v = p.n["sv"]
        p.wait("s", "sv", v)
        s0 = p.run("s", lambda eng: eng.nop(), inc="ss")
        p.wait("v", "ss", s0)

    def sbounce():
        s = p.n["ss"]
        p.wait("v", "ss", s)
        v0 = p.run("v", lambda eng: eng.nop(), inc="sv")
        p.wait("s", "sv", v0)

    def sact(out, in_, func, bias=0.0, scale=1.0, accum=None, inc="ss"):
        if func not in (AF.Copy, AF.Reciprocal):
            if isinstance(bias, float):
                bias = np.float32(bias)
            if isinstance(scale, float):
                scale = np.float32(scale)
        return p.run("s", lambda eng, o=out, x=in_, f=func, b=bias, sc=scale, ac=accum:
                     eng.activation(o, x, f, bias=b, scale=sc, accum_out=ac), inc=inc)

    def mm(out, lhsT, rhs, start, stop, inc=None):
        return p.run("t", lambda eng, o=out, l=lhsT, r=rhs, a=start, b=stop:
                     eng.matmul(o, l, r, start=a, stop=b), inc=inc)

    def ttrans(out, in_, ident, inc=None):
        return p.run("t", lambda eng, o=out, x=in_, i=ident:
                     eng.transpose(o, x, i), inc=inc)

    def ln_stats(src_tile, src_dtype, rsq_ones, sq_in_tile=None):
        p.wait("t", "sv", p.n["sv"])
        p.wait("t", "ss", p.n["ss"])
        p.wait("s", "sv", p.n["sv"])
        bank_wait("t", 4)
        bank_wait("t", 5)
        for tb in range(NTB):
            for k in range(KD):
                mm(pb[4][0:1, :], rsq_ones, src_tile(k)[:, tb * NT:(tb + 1) * NT],
                   start=(k == 0), stop=(k == KD - 1),
                   inc=("st" if k == KD - 1 else None))
            p.wait("v", "st", p.n["st"])
            vsx = vcopy(rowA[0:1, tb * NT:(tb + 1) * NT], pb[4][0:1, :], inc="sv")
            p.wait("t", "sv", vsx)
        for tb in range(NTB):
            for k in range(KD):
                sq_src = (sq_in_tile or src_tile)(k)[:, tb * NT:(tb + 1) * NT]
                if k >= 2 or tb > 0:
                    p.wait("s", "st", sq_guard[k % 2])
                ssq = sact(scrA[:, k % 2, :NT], sq_src, AF.Square)
                p.wait("t", "ss", ssq)
                gv = mm(pb[5][0:1, :], onesc[:, 0:1], scrA[:, k % 2, :NT],
                        start=(k == 0), stop=(k == KD - 1),
                        inc="st")
                sq_guard[k % 2] = gv
            p.wait("v", "st", p.n["st"])
            vq = vcopy(rowB[0:1, tb * NT:(tb + 1) * NT], pb[5][0:1, :], inc="sv")
            p.wait("t", "sv", vq)
        vts(rowA[0:1, :], rowA[0:1, :], 1.0 / D, AL.mult)
        vts(rowB[0:1, :], rowB[0:1, :], 1.0 / D, AL.mult)
        vstt(rowC[0:1, :], rowA[0:1, :], -1.0, AL.mult, rowA[0:1, :], AL.mult)
        v1 = vtt(rowB[0:1, :], rowB[0:1, :], rowC[0:1, :], AL.add)
        p.wait("s", "sv", v1)
        s1 = sact(rowB[0:1, :], rowB[0:1, :], AF.Ln, bias=EPS)
        s2 = sact(rowB[0:1, :], rowB[0:1, :], AF.Exp, scale=-0.5)
        p.wait("v", "ss", s2)
        v2 = vstt(rowC[0:1, :], rowA[0:1, :], -1.0, AL.mult, rowB[0:1, :], AL.mult,
                  inc="sv")
        p.wait("t", "sv", v2)
        rows = [rowB, rowC]
        for j, dst in enumerate(bc[:2]):
            bank_wait("t", j)
            for tb in range(NTB):
                tm = mm(pb[j][:, :], ones1[0:1, :],
                        rows[j][0:1, tb * NT:(tb + 1) * NT],
                        start=True, stop=True, inc="st")
                p.wait("v", "st", tm)
                vv = vcopy(dst[:, tb * NT:(tb + 1) * NT], pb[j][:, :], inc="sv")
                p.wait("t", "sv", vv)
            bank_set(j, "sv", vv)
        return

    sq_guard = {0: 0, 1: 0}

    def rz_cols():
        for j, (src, dst) in enumerate(((bc[0], rcol), (bc[1], zcol))):
            for tg in range(8):
                tv = ttrans(ptr[:, :], src[:, tg * 128:(tg + 1) * 128],
                            identr[:, :], inc="st")
                p.wait("v", "st", tv)
                vv = vcopy(dst[:, tg:tg + 1], ptr[:, 0:1], inc="sv")
                p.wait("t", "sv", vv)

    def proj_f32r(wdram, nM, cs_slot, dst, rhs_tile=None, e1="v"):
        rhs_tile = rhs_tile or (lambda k, tb: resid[:, k, tb * NT:(tb + 1) * NT])
        last_load = {}
        for m in range(nM):
            for h in range(2):
                bi = (2 * m + h) % 2
                gv = wst_guard.get(bi, 0)
                p.wait("y", "st", gv)
                last_load[h] = p.dma("y", "dy", wst[:, bi, :],
                                     wdram.ap()[m, :, h * KD // 2 * 128:(h + 1) * KD // 2 * 128])
            p.wait("t", "dy", last_load[1])
            bank_wait("t", 2)
            for k in range(KD):
                mm(pb[2][:, 0:2], wst[:, 0 if k < 8 else 1,
                                      (k % 8) * 128:(k % 8 + 1) * 128],
                   onesc[:, 0:2], start=(k == 0), stop=(k == KD - 1),
                   inc=("st" if k == KD - 1 else None))
            tv = p.n["st"]
            p.wait("v", "st", tv)
            vv = vcopy(csq[:, cs_slot + m:cs_slot + m + 1], pb[2][:, 0:1], inc="sv")
            bank_set(2, "sv", vv)
            for tb in range(NTB):
                bi = m % 2
                bank_wait("t", bi)
                for k in range(KD):
                    mm(pb[bi][:, :], wst[:, 0 if k < 8 else 1, (k % 8) * 128:(k % 8 + 1) * 128],
                       rhs_tile(k, tb), start=(k == 0), stop=(k == KD - 1),
                       inc=("st" if k == KD - 1 else None))
                tvv = p.n["st"]
                if tb == NTB - 1:
                    wst_guard[0] = tvv
                    wst_guard[1] = tvv
                p.wait("v", "st", tvv)
                sl = slice(tb * NT, (tb + 1) * NT)
                vtt(scrA[:, 0, :NT], pb[bi][:, :], bc[0][:, sl], AL.mult)
                vv = vstt(dst(m)[:, sl], bc[1][:, sl],
                          csq[:, cs_slot + m:cs_slot + m + 1], AL.mult,
                          scrA[:, 0, :NT], AL.add, inc="sv")
                bank_set(bi, "sv", vv)

    wst_guard = {0: 0, 1: 0}
    dwb_guard = {0: 0, 1: 0}

    for k in range(KD):
        p.dma("y", "dy", resid[:, k, :], dx[k])
    xload = p.n["dy"]
    for dst, src in ((identb, dID), (identr, dIDr), (expnT, dEXP), (causT, dCAU),
                     (onesc, dONE), (iotac, dIOT), (ones1, dONE1), (ones1b, dONE1b),
                     (pmat, dPM), (selh, dSH), (cnst, dCN),
                     (wgs, dWg.ap()), (rts, dRt.ap()),
                     ):
        p.dma("y", "dy", dst[:, :] if not hasattr(dst, "ndim") else dst, src)
    p.dma("g", "dg", dlrow[0:1, 0:4], dDL)
    p.dma("g", "dg", bxrow[0:1, 0:8], dBEX)
    p.dma("g", "dg", bxrow[0:1, 8:16], dBMK)
    p.dma("g", "dg", oscs[0:1, 0:1], dOS)
    p.dma("g", "dg", oscs[0:1, 1:2], dOS)
    consts_g = p.n["dg"]
    consts_y = p.n["dy"]

    for (src, dst) in ((dGW, dQ8g), (dUW, dQ8u), (dDW, dQ8d)):
        n0 = src.ap().shape[0]
        step = n0 // 4
        for i in range(0, n0, step):
            p.dma("g", "dg", dst.ap()[i:i + step], src.ap()[i:i + step])
    quant_done = p.n["dg"]

    p.wait("v", "dy", consts_y)
    p.wait("v", "dg", consts_g)
    vcopy(onescb[:, :], onesc[:, :])
    p.wait("t", "dg", consts_g)
    p.wait("t", "dy", consts_y)
    tosc = mm(pb[3][:, 0:2], ones1[0:1, :], oscs[0:1, 0:2],
              start=True, stop=True, inc="st")
    p.wait("v", "st", tosc)
    vosc = vcopy(osc[:, 0:1], pb[3][:, 0:1], inc="sv")

    p.wait("s", "dg", consts_g)
    p.wait("s", "dy", consts_y)
    s0 = sact(dlrow[0:1, 4:8], dlrow[0:1, 0:4], AF.Sigmoid)
    sbounce()
    s1 = sact(dlrow[0:1, 8:12], dlrow[0:1, 4:8], AF.Ln)
    sbounce()
    s1r = sact(dlogr[0:1, 0:4], dlrow[0:1, 8:12], AF.Identity)
    p.wait("t", "ss", s1r)
    p.wait("t", "sv", vosc)
    tld = mm(pb[3][:, 2:6], ones1[0:1, :], dlogr[0:1, 0:4],
             start=True, stop=True, inc="st")
    p.wait("v", "st", tld)
    vld = vcopy(logd[:, 0:4], pb[3][:, 2:6], inc="sv")
    bank_set(3, "sv", vld)
    p.wait("t", "sv", vld)
    p.wait("s", "sv", vld)
    vm_hist = {}
    for i in range(NH):
        sact(djc[:, i:i + 1], iotac[:, 0:1], AF.Exp, scale=logd[:, i:i + 1])
        sact(dCc[:, i:i + 1], iotac[:, 1:2], AF.Exp, scale=logd[:, i:i + 1])
        if i >= 2:
            p.wait("s", "sv", vm_hist[i % 2])
        sm = sact(scrA[:, i % 2, 0:128], expnT[:, :], AF.Exp,
                  scale=logd[:, i:i + 1])
        p.wait("v", "ss", sm)
        vm_hist[i % 2] = vtt(maskb[:, i * 128:(i + 1) * 128], scrA[:, i % 2, 0:128],
                             causT[:, :], AL.mult, inc="sv")

    p.wait("t", "dy", xload)

    tap_guard = []

    def tap(name, src_ap, rows=128):
        if name not in dbg_t:
            return
        p.wait("g", "sv", p.n["sv"])
        p.wait("g", "ss", p.n["ss"])
        p.wait("g", "st", p.n["st"])
        tv = p.dma("g", "dg", dbg_t[name].ap()[0:rows], src_ap)
        tap_guard.append(tv)
        for _e in ("t", "v", "s", "y", "g"):
            p.wait(_e, "dg", tv)

    tap("logd", logd[:, 0:4])
    tap("djc", djc[:, 0:3])
    tap("dCc", dCc[:, 0:3])

    ln_stats(lambda k: resid[:, k, :], F32R, onesc[:, 0:1])
    rz_cols()
    if "ln1" in dbg_t:
        p.wait("g", "sv", p.n["sv"])
        tap_guard.append(p.dma("g", "dg", dbg_t["ln1"].ap()[0:1], rowB[0:1, :]))
        tap_guard.append(p.dma("g", "dg", dbg_t["ln1"].ap()[1:2], rowC[0:1, :]))

    GP = 4
    bank_wait("t", 3)
    for tg in range(8):
        for k in range(KD):
            mm(pb[3][:, tg * GP:(tg + 1) * GP], resid[:, k, tg * 128:(tg + 1) * 128],
               wgs[:, k * GP:(k + 1) * GP], start=(k == 0), stop=(k == KD - 1),
               inc=("st" if k == KD - 1 else None))
    tgl = p.n["st"]
    p.wait("s", "st", tgl)
    p.wait("s", "sv", p.n["sv"])
    sgl = 0
    for tg in range(8):
        sgl = sact(glog[:, tg * GP:(tg + 1) * GP], pb[3][:, tg * GP:(tg + 1) * GP],
                   AF.Copy, scale=rcol[:, tg:tg + 1])
    p.wait("v", "ss", sgl)
    bank_wait("t", 4)
    for k in range(KD):
        mm(pb[4][0:1, 0:GP], onesc[:, 0:1], wgs[:, k * GP:(k + 1) * GP],
           start=(k == 0), stop=(k == KD - 1), inc=("st" if k == KD - 1 else None))
    tcg = p.n["st"]
    p.wait("v", "st", tcg)
    vcg = vcopy(rowA[0:1, 0:GP], pb[4][0:1, 0:GP], inc="sv")
    p.wait("t", "sv", vcg)
    tcb = mm(pb[4][:, 0:GP], ones1[0:1, :], rowA[0:1, 0:GP],
             start=True, stop=True, inc="st")
    p.wait("s", "st", tcb)
    scs = sact(scrA[:, 0, 0:GP], pb[4][:, 0:GP], AF.Copy)
    p.wait("v", "ss", scs)
    for tg in range(8):
        vstt(glog2[:, tg * GP:(tg + 1) * GP], scrA[:, 0, 0:GP], zcol[:, tg:tg + 1],
             AL.mult, glog[:, tg * GP:(tg + 1) * GP], AL.add)
    vg1 = p.n["sv"]
    bank_set(3, "sv", vg1)
    bank_set(4, "ss", scs)
    p.wait("s", "sv", vg1)
    sex = 0
    for tg in range(8):
        sex = sact(glog[:, tg * GP:(tg + 1) * GP], glog2[:, tg * GP:(tg + 1) * GP],
                   AF.Exp)
    p.wait("v", "ss", sex)
    vsum = p.run("v", lambda eng: eng.reduce_sum(
        gmx[:, 0:8].rearrange("p (a b) -> p a b", b=1),
        glog[:, :].rearrange("p (a b) -> p a b", b=GP), axis=AX), inc="sv")
    p.wait("s", "sv", vsum)
    sm1 = sact(gmx[:, 8:16], gmx[:, 0:8], AF.Identity, bias=-1.0)
    p.wait("v", "ss", sm1)
    p.run("v", lambda eng: eng.reciprocal(gmx[:, 16:24], gmx[:, 8:16]), inc="sv")
    vrc = p.n["sv"]
    p.wait("s", "sv", vrc)
    spk = 0
    for tg in range(8):
        spk = sact(sp92[:, tg * 3:(tg + 1) * 3], glog[:, tg * GP:tg * GP + 3],
                   AF.Copy, scale=gmx[:, 16 + tg:17 + tg])
    p.wait("t", "ss", spk)
    for tg in range(8):
        tv = ttrans(ptb[0:3, 0:128], sp92[:, tg * 3:(tg + 1) * 3], identb[:, :],
                    inc="st")
        p.wait("v", "st", tv)
        vv = vcopy(gtT[0:3, tg * 128:(tg + 1) * 128], ptb[0:3, 0:128], inc="sv")
        p.wait("t", "sv", vv)
    tap("gate", gtT[0:3, :], rows=3)

    proj_f32r(dWq, 4, 0, lambda m: qT[:, m, :])
    tap("qT", qT[:, :, :].rearrange("p m t -> p (m t)"))

    bstore = []
    for hI in range(NH):
        proj_f32r(_sub(dWk, hI), 4, 8, lambda m: kT[:, m, :])
        proj_f32r(_sub(dWv, hI), 4, 16, lambda m: vFM[:, m, :])
        for tg in range(NCHK):
            for dk4 in range(4):
                tv = ttrans(ptb[:, dk4 * 128:(dk4 + 1) * 128],
                            kT[:, dk4, tg * 128:(tg + 1) * 128], identb[:, :],
                            inc=("st" if dk4 == 3 else None))
            p.wait("s", "st", tv)
            sv_ = sact(kdTM[:, tg, :], ptb[:, 0:512], AF.Copy, scale=djc[:, hI:hI + 1])
            p.wait("t", "ss", sv_)
            for dk4 in range(4):
                tv = ttrans(ptb[:, dk4 * 128:(dk4 + 1) * 128],
                            vFM[:, dk4, tg * 128:(tg + 1) * 128], identb[:, :],
                            inc=("st" if dk4 == 3 else None))
            p.wait("s", "st", tv)
            sv_ = sact(vTM[:, tg, :], ptb[:, 0:512], AF.Copy)
            p.wait("t", "ss", sv_)
        vfm_done = tv
        if hI == 0:
            tap("kT", kT[:, :, :].rearrange("p m t -> p (m t)"))
            tap("vTM", vTM[:, :, :].rearrange("p a b -> p (a b)"))
            tap("kdTM", kdTM[:, :, :].rearrange("p a b -> p (a b)"))
        p.wait("y", "ss", p.n["ss"])
        ld = p.dma("y", "dy", rowA[0:1, :], dE1)
        p.wait("s", "dy", ld)
        se1 = sact(rowC[0:1, :], rowA[0:1, :], AF.Exp, scale=dlrow[0:1, 8 + hI:9 + hI])
        gb = 0
        bank_wait("t", 0)
        for tb in range(NTB):
            tvb = mm(pb[0][:, :], selh[:, hI * 128:(hI + 1) * 128],
                     gtT[0:3, tb * NT:(tb + 1) * NT],
                     start=True, stop=True, inc="st")
            p.wait("v", "st", tvb)
            gb = vcopy(bc[2][:, tb * NT:(tb + 1) * NT], pb[0][:, :], inc="sv")
            p.wait("t", "sv", gb)
        p.wait("v", "st", vfm_done)
        if hI == 0 and "gateb" in dbg_t:
            tap("gateb", bc[2][:, :])
            p.wait("v", "dg", p.n["dg"])
        for k4 in range(4):
            vtt(qg[:, k4, :], qT[:, k4, :], bc[2][:, :], AL.mult)
        p.wait("t", "ss", se1)
        for tb in range(NTB):
            tvb = mm(pb[0][:, :], ones1[0:1, :], rowC[0:1, tb * NT:(tb + 1) * NT],
                     start=True, stop=True, inc="st")
            p.wait("v", "st", tvb)
            vcb = vcopy(bc[2][:, tb * NT:(tb + 1) * NT], pb[0][:, :], inc="sv")
            p.wait("t", "sv", vcb)
        bank_set(0, "sv", vcb)
        for k4 in range(4):
            vqg = vtt(qgd[:, k4, :], qg[:, k4, :], bc[2][:, :], AL.mult, inc="sv")
        p.wait("t", "sv", vqg)
        if hI == 0:
            tap("qg", qg[:, :, :].rearrange("p m t -> p (m t)"))
            tap("qgd", qgd[:, :, :].rearrange("p m t -> p (m t)"))

        for ci, c in enumerate(reversed(range(NCHK))):
            csl = slice(c * CH, (c + 1) * CH)
            bank_wait("t", 0)
            for dk4 in range(4):
                mm(pb[0][:, 0:128], kT[:, dk4, csl], qg[:, dk4, csl],
                   start=(dk4 == 0), stop=(dk4 == 3), inc=("st" if dk4 == 3 else None))
            tsc = p.n["st"]
            p.wait("v", "st", tsc)
            vsc = vtt(sp92[:, 0:128], pb[0][:, 0:128], maskb[:, hI * 128:(hI + 1) * 128],
                      AL.mult, inc="sv")
            bank_set(0, "sv", vsc)
            p.wait("t", "sv", vsc)
            bank_wait("t", 1)
            for dv4 in range(4):
                mm(pb[1][:, dv4 * 128:(dv4 + 1) * 128],
                   vTM[:, c, dv4 * 128:(dv4 + 1) * 128], sp92[:, 0:128],
                   start=True, stop=(ci == 0),
                   inc=("st" if (ci == 0 and dv4 == 3) else None))
                if ci > 0:
                    for dk4 in range(4):
                        mm(pb[1][:, dv4 * 128:(dv4 + 1) * 128],
                           Sbf[:, dk4, dv4 * 128:(dv4 + 1) * 128], qgd[:, dk4, csl],
                           start=False, stop=(dk4 == 3),
                           inc=("st" if (dk4 == 3 and dv4 == 3) else None))
            tre = p.n["st"]
            p.wait("v", "st", tre)
            if hI == 0:
                for dv4 in range(4):
                    vr = vcopy(blendT[:, dv4, csl], pb[1][:, dv4 * 128:(dv4 + 1) * 128],
                               inc=("sv" if dv4 == 3 else None))
            else:
                for dv4 in range(4):
                    vr = vtt(blendT[:, dv4, csl], blendT[:, dv4, csl],
                             pb[1][:, dv4 * 128:(dv4 + 1) * 128], AL.add,
                             inc=("sv" if dv4 == 3 else None))
            bank_set(1, "sv", vr)
            for dk4 in range(4):
                bank_wait("t", 2 + dk4)
                mm(pb[2 + dk4][:, :], kdTM[:, c, dk4 * 128:(dk4 + 1) * 128],
                   vTM[:, c, :], start=True, stop=True,
                   inc="st")
                tsu = p.n["st"]
                p.wait("v", "st", tsu)
                if ci == 0:
                    vu = vcopy(Sbf[:, dk4, :], pb[2 + dk4][:, :], inc="sv")
                else:
                    vu = vstt(Sbf[:, dk4, :], Sbf[:, dk4, :], dCc[:, hI:hI + 1],
                              AL.mult, pb[2 + dk4][:, :], AL.add, inc="sv")
                bank_set(2 + dk4, "sv", vu)
                p.wait("t", "sv", vu)
        p.wait("g", "sv", p.n["sv"])
        bstore.append(p.dma("g", "dg", dBin.ap()[hI * 128:(hI + 1) * 128, :],
                            Sbf[:, :, :].rearrange("p a b -> p (a b)")))
        if hI == 0:
            tap("Sst", Sbf[:, :, :].rearrange("p a b -> p (a b)"))
            tap("blend0", blendT[:, :, :].rearrange("p a b -> p (a b)"))

    p.wait("g", "dg", bstore[-1])
    p.run("g", lambda eng: eng.collective_compute(
        "AllGather", AL.bypass, replica_groups=[list(range(NCORES))],
        ins=[dBin.ap().opt()], outs=[dBout.ap().opt()]), inc="cc")
    p.wait("g", "cc", p.n["cc"])
    ebn_guard = 0
    for hI in range(NH):
        p.wait("s", "sv", p.n["sv"])
        swr = sact(rowA[0:1, 0:8], bxrow[0:1, 0:8], AF.Exp,
                   scale=dlrow[0:1, 8 + hI:9 + hI])
        p.wait("v", "ss", swr)
        vw = vtt(rowA[0:1, 8:16], rowA[0:1, 0:8], bxrow[0:1, 8:16], AL.mult,
                 inc="sv")
        p.wait("t", "sv", vw)
        bank_wait("t", 0)
        twb = mm(pb[0][:, 0:8], ones1[0:1, :], rowA[0:1, 8:16], start=True, stop=True,
                 inc="st")
        p.wait("v", "st", twb)
        vwc = vcopy(wcs[:, 0:8], pb[0][:, 0:8], inc="sv")
        bank_set(0, "sv", vwc)
        for r in range(NCORES):
            lb = p.dma("g", "dg", bstmp[:, :],
                       dBout.ap()[(r * NH + hI) * 128:(r * NH + hI + 1) * 128, :])
            p.wait("v", "dg", lb)
            p.wait("v", "sv", vwc)
            if r == 0:
                p.wait("v", "st", ebn_guard)
                vE = vts(Ebn[:, :, :].rearrange("p a b -> p (a b)"), bstmp[:, :],
                         wcs[:, 0:1], AL.mult, inc="sv")
            else:
                vE = vstt(Ebn[:, :, :].rearrange("p a b -> p (a b)"), bstmp[:, :],
                          wcs[:, r:r + 1], AL.mult,
                          Ebn[:, :, :].rearrange("p a b -> p (a b)"), AL.add, inc="sv")
            p.wait("g", "sv", vE)
        if hI == 0:
            tap("Ebn", Ebn[:, :, :].rearrange("p a b -> p (a b)"))
        p.wait("y", "ss", p.n["ss"])
        ld = p.dma("y", "dy", rowB[0:1, :], dE2)
        p.wait("s", "dy", ld)
        se2 = sact(rowC[0:1, :], rowB[0:1, :], AF.Exp, scale=dlrow[0:1, 8 + hI:9 + hI])
        p.wait("t", "ss", se2)
        bank_wait("t", 1)
        for tb in range(NTB):
            tvb = mm(pb[1][:, :], ones1[0:1, :], rowC[0:1, tb * NT:(tb + 1) * NT],
                     start=True, stop=True, inc="st")
            p.wait("v", "st", tvb)
            ve2 = vcopy(bc[2][:, tb * NT:(tb + 1) * NT], pb[1][:, :], inc="sv")
            p.wait("t", "sv", ve2)
        for tb in range(NTB):
            tvb = mm(pb[1][:, :], selh[:, hI * 128:(hI + 1) * 128],
                     gtT[0:3, tb * NT:(tb + 1) * NT],
                     start=True, stop=True, inc="st")
            p.wait("v", "st", tvb)
            vgb = vstt(bc[2][:, tb * NT:(tb + 1) * NT], bc[2][:, tb * NT:(tb + 1) * NT],
                       1.0, AL.mult, pb[1][:, :], AL.mult, inc="sv")
            p.wait("t", "sv", vgb)
        bank_set(1, "sv", vgb)
        p.wait("v", "st", ebn_guard)
        for k4 in range(4):
            vq = vtt(qgd[:, k4, :], qT[:, k4, :], bc[2][:, :], AL.mult, inc="sv")
        p.wait("t", "sv", vq)
        for dv4 in range(4):
            for tb in range(NTB):
                bi = (dv4 * NTB + tb) % 2
                bank_wait("t", bi)
                for dk4 in range(4):
                    mm(pb[bi][:, :], Ebn[:, dk4, dv4 * 128:(dv4 + 1) * 128],
                       qgd[:, dk4, tb * NT:(tb + 1) * NT],
                       start=(dk4 == 0), stop=(dk4 == 3),
                       inc=("st" if dk4 == 3 else None))
                tb_ = p.n["st"]
                p.wait("v", "st", tb_)
                vb = vtt(blendT[:, dv4, tb * NT:(tb + 1) * NT],
                         blendT[:, dv4, tb * NT:(tb + 1) * NT], pb[bi][:, :],
                         AL.add, inc="sv")
                bank_set(bi, "sv", vb)
        ebn_guard = p.n["st"]
    tap("blend", blendT[:, :, :].rearrange("p a b -> p (a b)"))

    p.wait("y", "sv", p.n["sv"])
    p.wait("y", "st", p.n["st"])
    p.dma("y", "dy", lA[:, :], dA.ap())
    lab_load = p.dma("y", "dy", lB[:, :], dB.ap())

    for k4 in range(4):
        vos = vts(blendT[:, k4, :], blendT[:, k4, :], osc[:, 0:1], AL.mult,
                  inc=("sv" if k4 == 3 else None))
    p.wait("t", "sv", vos)

    p.wait("g", "st", max(wst_guard[0], wst_guard[1]))
    for m in range(16):
        gv = wst_guard.get(4 + m % 2, 0)
        p.wait("g", "st", gv)
        wl = p.dma("g", "dg", wbl1[:, m % 2, 0:512], dWo.ap()[m])
        p.wait("t", "dg", wl)
        for tb in range(NTB):
            bi = m % 2
            bank_wait("t", bi)
            for k4 in range(4):
                mm(pb[bi][:, :], wbl1[:, m % 2, k4 * 128:(k4 + 1) * 128],
                   blendT[:, k4, tb * NT:(tb + 1) * NT],
                   start=(k4 == 0), stop=(k4 == 3), inc=("st" if k4 == 3 else None))
            tww = p.n["st"]
            if tb == NTB - 1:
                wst_guard[4 + m % 2] = tww
            p.wait("v", "st", tww)
            vw = vcopy(ao[:, m, tb * NT:(tb + 1) * NT], pb[bi][:, :], inc="sv")
            bank_set(bi, "sv", vw)
    wst_guard[0] = wst_guard[1] = max(wst_guard[4], wst_guard[5])
    tap("ao", ao[:, :, :].rearrange("p k t -> p (k t)"))

    def post_ln_update():
        ln_stats(lambda k: ao[:, k, :], BF16, onescb[:, 0:1],
                 sq_in_tile=lambda k: ao[:, k, :])
        for k in range(KD):
            v1 = vtt(scrA[:, k % 2, :], ao[:, k, :], bc[0][:, :], AL.mult)
            v2 = vtt(scrA[:, k % 2, :], scrA[:, k % 2, :], bc[1][:, :], AL.add)
            v3 = vtt(resid[:, k, :], resid[:, k, :], scrA[:, k % 2, :], AL.add,
                     inc="sv")
        p.wait("t", "sv", p.n["sv"])
        p.wait("s", "sv", p.n["sv"])

    post_ln_update()
    tap("x1", resid[:, :, :].rearrange("p k t -> p (k t)"))

    ln_stats(lambda k: resid[:, k, :], F32R, onesc[:, 0:1])
    rz_cols()
    vrr = p.run("v", lambda eng: eng.reciprocal(rowA[0:1, :], rowB[0:1, :]), inc="sv")
    p.wait("t", "sv", vrr)
    bank_wait("t", 0)
    for tb in range(NTB):
        tvb = mm(pb[0][:, :], ones1[0:1, :], rowA[0:1, tb * NT:(tb + 1) * NT],
                 start=True, stop=True, inc="st")
        p.wait("v", "st", tvb)
        vv = vcopy(bc[2][:, tb * NT:(tb + 1) * NT], pb[0][:, :], inc="sv")
        p.wait("t", "sv", vv)
        bank_set(0, "sv", vv)

    bank_wait("t", 3)
    for tg in range(8):
        for k in range(KD):
            mm(pb[3][:, tg * 8:(tg + 1) * 8], resid[:, k, tg * 128:(tg + 1) * 128],
               rts[:, k * 8:(k + 1) * 8], start=(k == 0), stop=(k == KD - 1),
               inc=("st" if k == KD - 1 else None))
    trl = p.n["st"]
    bank_wait("t", 4)
    for k in range(KD):
        mm(pb[4][0:1, 0:8], onesc[:, 0:1], rts[:, k * 8:(k + 1) * 8],
           start=(k == 0), stop=(k == KD - 1), inc=("st" if k == KD - 1 else None))
    tcr = p.n["st"]
    p.wait("v", "st", tcr)
    vcr = vcopy(rowA[0:1, 0:8], pb[4][0:1, 0:8], inc="sv")
    p.wait("t", "sv", vcr)
    tcb2 = mm(pb[4][:, 0:8], ones1[0:1, :], rowA[0:1, 0:8], start=True, stop=True,
              inc="st")
    p.wait("v", "st", tcb2)
    vcs = vcopy(scrA[:, 0, 0:8], pb[4][:, 0:8], inc="sv")
    bank_set(4, "sv", vcs)
    p.wait("s", "st", trl)
    p.wait("s", "sv", vcs)
    srl = 0
    for tg in range(8):
        srl = sact(rlog[:, tg * 8:(tg + 1) * 8], pb[3][:, tg * 8:(tg + 1) * 8],
                   AF.Copy, scale=rcol[:, tg:tg + 1])
    p.wait("v", "ss", srl)
    for tg in range(8):
        vstt(rlog2[:, tg * 8:(tg + 1) * 8], scrA[:, 0, 0:8], zcol[:, tg:tg + 1],
             AL.mult, rlog[:, tg * 8:(tg + 1) * 8], AL.add)
    vrl = p.n["sv"]
    bank_set(3, "sv", vrl)
    vbounce()
    p.run("v", lambda eng: eng.reduce_max(
        rtop[:, 0:8].rearrange("p (a b) -> p a b", b=1),
        rlog2[:, :].rearrange("p (a b) -> p a b", b=8), axis=AX))
    vbounce()
    for tg in range(8):
        vts(sp92[:, tg * 8:(tg + 1) * 8],
            rlog2[:, tg * 8:(tg + 1) * 8], rtop[:, tg:tg + 1], AL.is_equal)
    vbounce()
    for tg in range(8):
        vstt(scrA[:, 0, tg * 8:(tg + 1) * 8], sp92[:, tg * 8:(tg + 1) * 8], -BIG,
             AL.mult, rlog2[:, tg * 8:(tg + 1) * 8], AL.add)
    vbounce()
    p.run("v", lambda eng: eng.reduce_max(
        rtop[:, 8:16].rearrange("p (a b) -> p a b", b=1),
        scrA[:, 0, 0:64].rearrange("p (a b) -> p a b", b=8), axis=AX))
    vbounce()
    v12 = vtt(rtop[:, 16:24], rtop[:, 0:8], rtop[:, 8:16], AL.subtract, inc="sv")
    p.wait("s", "sv", v12)
    sw1 = sact(rtop[:, 24:32], rtop[:, 16:24], AF.Sigmoid)
    p.wait("v", "ss", sw1)
    vts(gmx[:, 0:8], rtop[:, 24:32], 2.0, AL.mult)
    vts(gmx[:, 8:16], rtop[:, 24:32], -2.0, AL.mult, 2.0, AL.add)
    for tg in range(8):
        vts(scrA[:, 1, tg * 8:(tg + 1) * 8], scrA[:, 0, tg * 8:(tg + 1) * 8],
            rtop[:, 8 + tg:9 + tg], AL.is_equal)
    vbounce()
    for tg in range(8):
        vts(rlog[:, tg * 8:(tg + 1) * 8], sp92[:, tg * 8:(tg + 1) * 8],
            gmx[:, tg:tg + 1], AL.mult)
    vbounce()
    for tg in range(8):
        sl8 = slice(tg * 8, (tg + 1) * 8)
        vwd = vstt(sp92[:, sl8], scrA[:, 1, sl8], gmx[:, 8 + tg:9 + tg],
                   AL.mult, rlog[:, sl8], AL.add, inc="sv")
    p.wait("t", "sv", p.n["sv"])
    for tg in range(8):
        tv = ttrans(ptb[0:8, 0:128], sp92[:, tg * 8:(tg + 1) * 8], identb[:, :],
                    inc="st")
        p.wait("v", "st", tv)
        vv = vcopy(wdt[0:8, tg * 128:(tg + 1) * 128], ptb[0:8, 0:128], inc="sv")
        p.wait("t", "sv", vv)
    tap("wd", wdt[0:8, :], rows=8)

    p.wait("t", "dy", lab_load)
    for tb in range(NTB):
        bi = tb % 2
        bank_wait("t", bi)
        for k in range(KD):
            mm(pb[bi][:, :], lA[:, k * 128:(k + 1) * 128],
               resid[:, k, tb * NT:(tb + 1) * NT],
               start=(k == 0), stop=(k == KD - 1), inc=("st" if k == KD - 1 else None))
        tax = p.n["st"]
        if tb == 0:
            bank_wait("t", 2)
            for k in range(KD):
                mm(pb[2][:, 0:2], lA[:, k * 128:(k + 1) * 128], onesc[:, 0:2],
                   start=(k == 0), stop=(k == KD - 1),
                   inc=("st" if k == KD - 1 else None))
            tca = p.n["st"]
            p.wait("v", "st", tca)
            vca = vcopy(csq[:, 32:33], pb[2][:, 0:1], inc="sv")
            bank_set(2, "sv", vca)
        p.wait("v", "st", tax)
        sl = slice(tb * NT, (tb + 1) * NT)
        vtt(scrA[:, 0, :NT], pb[bi][:, :], bc[0][:, sl], AL.mult)
        vax = vstt(axt[:, sl], bc[1][:, sl], csq[:, 32:33], AL.mult,
                   scrA[:, 0, :NT], AL.add, inc="sv")
        bank_set(bi, "sv", vax)
    tap("axt", axt[:, :])
    for tb in range(NTB):
        bank_wait("t", 0)
        twd = mm(pb[0][:, :], pmat[:, :], wdt[0:8, tb * NT:(tb + 1) * NT],
                 start=True, stop=True, inc="st")
        p.wait("v", "st", twd)
        sl = slice(tb * NT, (tb + 1) * NT)
        vtt(scrA[:, 0, :NT], pb[0][:, :], bc[2][:, sl], AL.mult)
        vwx = vtt(axw[:, sl], axt[:, sl], scrA[:, 0, :NT], AL.mult, inc="sv")
        bank_set(0, "sv", vwx)
    p.wait("t", "sv", p.n["sv"])

    for m in range(16):
        for h in range(2):
            bi = h
            gv = wst_guard.get(bi, 0)
            p.wait("y", "st", gv)
            p.dma("y", "dy", wst[:, bi, :], dWb.ap()[m, :, h * 8 * 128:(h + 1) * 8 * 128])
        wl = p.n["dy"]
        p.wait("t", "dy", wl)
        bank_wait("t", 2)
        for k in range(KD):
            mm(pb[2][:, 0:2], wst[:, 0 if k < 8 else 1, (k % 8) * 128:(k % 8 + 1) * 128],
               onesc[:, 0:2], start=(k == 0), stop=(k == KD - 1),
               inc=("st" if k == KD - 1 else None))
        tcs = p.n["st"]
        p.wait("v", "st", tcs)
        vcsb = vcopy(csq[:, 40 + (m % 8):41 + (m % 8)], pb[2][:, 0:1], inc="sv")
        bank_set(2, "sv", vcsb)
        for tb in range(NTB):
            bi = m % 2
            bank_wait("t", bi)
            for k in range(KD):
                mm(pb[bi][:, :], wst[:, 0 if k < 8 else 1, (k % 8) * 128:(k % 8 + 1) * 128],
                   resid[:, k, tb * NT:(tb + 1) * NT],
                   start=(k == 0), stop=False)
            mm(pb[bi][:, :], lB[:, m * 128:(m + 1) * 128],
               axw[:, tb * NT:(tb + 1) * NT], start=False, stop=True, inc="st")
            tbm = p.n["st"]
            if tb == NTB - 1:
                wst_guard[0] = tbm
                wst_guard[1] = tbm
            p.wait("v", "st", tbm)
            p.wait("v", "sv", vcsb)
            sl = slice(tb * NT, (tb + 1) * NT)
            vtt(scrA[:, 0, :NT], pb[bi][:, :], bc[0][:, sl], AL.mult)
            vbs = vstt(ao[:, m, sl], bc[1][:, sl], csq[:, 40 + (m % 8):41 + (m % 8)],
                       AL.mult, scrA[:, 0, :NT], AL.add, inc="sv")
            bank_set(bi, "sv", vbs)

    post_ln_update()
    tap("x2", resid[:, :, :].rearrange("p k t -> p (k t)"))

    ln_stats(lambda k: resid[:, k, :], F32R, onesc[:, 0:1])
    for k in range(KD):
        vtt(scrA[:, k % 2, :], resid[:, k, :], bc[0][:, :], AL.mult)
        vx5 = vtt(xn5[:, k, :], scrA[:, k % 2, :], bc[1][:, :], AL.add, inc="sv")
    p.wait("t", "sv", vx5)
    tap("xn5", xn5[:, :, :].rearrange("p k t -> p (k t)"))

    q_l1 = ao
    wst_guard[4] = wst_guard[5] = max(wst_guard[0], wst_guard[1])
    p.wait("y", "st", wst_guard[0])
    for dst, src in ((b1c, dB1), (gsc, dGS), (usc, dUS), (dsc, dDS)):
        scl_load = p.dma("y", "dy", dst[:, :], src)
    p.wait("v", "dy", scl_load)
    p.wait("s", "dy", scl_load)
    for j in range(16):
        for part, mofs in (("q", 0), ("k", 16), ("v", 32)):
            m = mofs + j
            for h in range(2):
                bi = h
                gv = wst_guard.get(4 + bi, 0)
                p.wait("g", "st", gv)
                p.dma("g", "dg", wbl1[:, bi, :],
                      dW1.ap()[m, :, h * 8 * 128:(h + 1) * 8 * 128])
            wl = p.n["dg"]
            p.wait("t", "dg", wl)
            for tb in range(NTB):
                bi = (m * NTB + tb) % 2
                bank_wait("t", bi)
                for k in range(KD):
                    mm(pb[bi][:, :], wbl1[:, 0 if k < 8 else 1, (k % 8) * 128:(k % 8 + 1) * 128],
                       xn5[:, k, tb * NT:(tb + 1) * NT],
                       start=(k == 0), stop=(k == KD - 1),
                       inc=("st" if k == KD - 1 else None))
                tqk = p.n["st"]
                if tb == NTB - 1:
                    wst_guard[4] = tqk
                    wst_guard[5] = tqk
                sl = slice(tb * NT, (tb + 1) * NT)
                if part == "q":
                    p.wait("v", "st", tqk)
                    vq1 = vts(q_l1[:, j, sl], pb[bi][:, :], b1c[:, m:m + 1], AL.add,
                              inc="sv")
                    bank_set(bi, "sv", vq1)
                elif part == "k":
                    p.wait("s", "st", tqk)
                    sk = sact(kscr[:, sl], pb[bi][:, :], AF.Identity,
                              bias=b1c[:, m:m + 1])
                    bank_set(bi, "ss", sk)
                else:
                    p.wait("s", "st", tqk)
                    svv = sact(vscr[:, sl], pb[bi][:, :], AF.Identity,
                               bias=b1c[:, m:m + 1])
                    bank_set(bi, "ss", svv)
        p.wait("s", "sv", p.n["sv"])
        sq2 = sact(sp92[:, 0:T], q_l1[:, j, :], AF.Square, accum=acc[:, j:j + 1])
        sk2 = sact(sp92[:, 0:T], kscr[:, :], AF.Square, accum=acc[:, 16 + j:17 + j])
        p.wait("v", "ss", sk2)
        vkv = vtt(sp92[:, T:2 * T], kscr[:, :], vscr[:, :], AL.mult)
        vkr = p.run("v", lambda eng, j=j: eng.reduce_sum(
            acc[:, 32 + j:33 + j].rearrange("p (a b) -> p a b", b=1),
            sp92[:, T:2 * T].rearrange("p (a b) -> p a b", b=T), axis=AX), inc="sv")
        p.wait("s", "sv", vkr)

    p.wait("g", "sv", p.n["sv"])
    p.wait("g", "ss", p.n["ss"])
    arin = p.dma("g", "dg", dRin.ap(), acc[:, 0:48])
    p.wait("g", "dg", arin)
    p.run("g", lambda eng: eng.collective_compute(
        "AllReduce", AL.add, replica_groups=[list(range(NCORES))],
        ins=[dRin.ap().opt()], outs=[dRout.ap().opt()]), inc="cc")
    p.wait("g", "cc", p.n["cc"])
    grl = p.dma("g", "dg", acc[:, 48:96], dRout.ap())
    p.wait("v", "dg", grl)
    vnn = vtt(gst[:, 0:16], acc[:, 48:64], acc[:, 64:80], AL.mult, inc="sv")
    p.wait("s", "sv", vnn)
    snn = sact(gst[:, 16:32], gst[:, 0:16], AF.Sqrt)
    p.wait("v", "ss", snn)
    vrc2 = p.run("v", lambda eng: eng.reciprocal(gst[:, 32:48], gst[:, 16:32]),
                 inc="sv")
    p.wait("s", "sv", vrc2)
    scp = sact(gst[:, 48:64], gst[:, 32:48], AF.Copy)
    p.wait("v", "ss", scp)
    vcf = vtt(gst[:, 0:16], acc[:, 80:96], gst[:, 48:64], AL.mult, inc="sv")
    p.wait("s", "sv", vcf)
    scb = sact(cbf[:, 0:16], gst[:, 0:16], AF.Copy)
    sc2 = sact(cbf[:, 16:32], gst[:, 0:16], AF.Square)
    p.wait("t", "ss", sc2)
    p.wait("v", "ss", sc2)
    tap("qL1", q_l1[:, :, :].rearrange("p k t -> p (k t)"))
    tap("cfac", gst[:, 0:16])

    bank_wait("t", 4)
    bank_wait("t", 5)
    for tb in range(NTB):
        for k in range(KD):
            mm(pb[4][0:1, :], cbf[:, k:k + 1], q_l1[:, k, tb * NT:(tb + 1) * NT],
               start=(k == 0), stop=(k == KD - 1), inc=("st" if k == KD - 1 else None))
        p.wait("v", "st", p.n["st"])
        vsx = vcopy(rowA[0:1, tb * NT:(tb + 1) * NT], pb[4][0:1, :], inc="sv")
        p.wait("t", "sv", vsx)
    for tb in range(NTB):
        for k in range(KD):
            gvq = sq_guard[k % 2]
            p.wait("s", "st", gvq)
            sqq = sact(sp92[:, (k % 2) * NT:(k % 2 + 1) * NT],
                       q_l1[:, k, tb * NT:(tb + 1) * NT], AF.Square)
            p.wait("t", "ss", sqq)
            gv2 = mm(pb[5][0:1, :], cbf[:, 16 + k:17 + k],
                     sp92[:, (k % 2) * NT:(k % 2 + 1) * NT],
                     start=(k == 0), stop=(k == KD - 1), inc="st")
            sq_guard[k % 2] = gv2
        p.wait("v", "st", p.n["st"])
        vsq = vcopy(rowB[0:1, tb * NT:(tb + 1) * NT], pb[5][0:1, :], inc="sv")
        p.wait("t", "sv", vsq)
    vts(rowA[0:1, :], rowA[0:1, :], 1.0 / D, AL.mult)
    vts(rowB[0:1, :], rowB[0:1, :], 1.0 / D, AL.mult)
    vstt(rowC[0:1, :], rowA[0:1, :], -1.0, AL.mult, rowA[0:1, :], AL.mult)
    v1 = vtt(rowB[0:1, :], rowB[0:1, :], rowC[0:1, :], AL.add)
    p.wait("s", "sv", v1)
    s1 = sact(rowB[0:1, :], rowB[0:1, :], AF.Ln, bias=EPS)
    s2 = sact(rowB[0:1, :], rowB[0:1, :], AF.Exp, scale=-0.5)
    p.wait("v", "ss", s2)
    vz7 = vstt(rowC[0:1, :], rowA[0:1, :], -1.0, AL.mult, rowB[0:1, :], AL.mult,
               inc="sv")
    p.wait("t", "sv", vz7)
    rows = [rowB, rowC]
    for jj in range(2):
        bank_wait("t", jj)
        for tb in range(NTB):
            tm = mm(pb[jj][:, :], ones1[0:1, :], rows[jj][0:1, tb * NT:(tb + 1) * NT],
                    start=True, stop=True, inc="st")
            p.wait("v", "st", tm)
            vv = vcopy(bc[jj][:, tb * NT:(tb + 1) * NT], pb[jj][:, :], inc="sv")
            p.wait("t", "sv", vv)
        bank_set(jj, "sv", vv)
    for k in range(KD):
        vstt(scrA[:, k % 2, :], q_l1[:, k, :], gst[:, k:k + 1], AL.mult,
             bc[0][:, :], AL.mult)
        vtt(scrA[:, k % 2, :], scrA[:, k % 2, :], bc[1][:, :], AL.add)
        vup = vtt(resid[:, k, :], resid[:, k, :], scrA[:, k % 2, :], AL.add, inc="sv")
    p.wait("t", "sv", vup)
    p.wait("s", "sv", vup)
    tap("x3", resid[:, :, :].rearrange("p k t -> p (k t)"))

    ln_stats(lambda k: resid[:, k, :], F32R, onesc[:, 0:1])
    for k in range(KD):
        vtt(scrA[:, k % 2, :], resid[:, k, :], bc[0][:, :], AL.mult)
        vx6 = vtt(xn6[:, k, :], scrA[:, k % 2, :], bc[1][:, :], AL.add, inc="sv")
    p.wait("y", "sv", vx6)
    p.wait("t", "sv", vx6)
    tap("xn6", xn6[:, :, :].rearrange("p k t -> p (k t)"))
    spv = p.dma("y", "dy", dSpill.ap(), resid[:, :, :].rearrange("p k t -> p (k t)"))
    p.wait("v", "dy", spv)
    p.wait("g", "dg", quant_done)
    p.wait("g", "sv", p.n["sv"])
    p.wait("g", "ss", p.n["ss"])
    p.wait("g", "dy", spv)

    for m in range(MI):
        bi = m % 2
        gv = wst_guard.get(6 + bi, 0)
        p.wait("g", "st", gv)
        lg = p.dma("g", "dg", gwb[:, bi, :], dQ8g.ap()[m])
        lu = p.dma("g", "dg", uwb[:, bi, :], dQ8u.ap()[m])
        p.wait("t", "dg", lu)
        for tb in range(NTB):
            b0 = (m * NTB + tb) % 2
            b1_ = 2 + (m * NTB + tb) % 2
            bank_wait("t", b0)
            for k in range(KD):
                mm(pb[b0][:, :], gwb[:, bi, k * 128:(k + 1) * 128],
                   xn6[:, k, tb * NT:(tb + 1) * NT],
                   start=(k == 0), stop=(k == KD - 1), inc=("st" if k == KD - 1 else None))
            tgate = p.n["st"]
            bank_wait("t", b1_)
            for k in range(KD):
                mm(pb[b1_][:, :], uwb[:, bi, k * 128:(k + 1) * 128],
                   xn6[:, k, tb * NT:(tb + 1) * NT],
                   start=(k == 0), stop=(k == KD - 1), inc=("st" if k == KD - 1 else None))
            tup = p.n["st"]
            if tb == NTB - 1:
                wst_guard[6 + bi] = tup
            sl = slice(tb * NT, (tb + 1) * NT)
            p.wait("s", "st", tgate)
            sg_ = sact(sgs[:, :], pb[b0][:, :], AF.Silu, scale=gsc[:, m:m + 1])
            bank_set(b0, "ss", sg_)
            p.wait("v", "st", tup)
            p.wait("v", "ss", sg_)
            vfu = vstt(fuse[:, m, sl], pb[b1_][:, :], usc[:, m:m + 1], AL.mult,
                       sgs[:, :], AL.mult, inc="sv")
            bank_set(b1_, "sv", vfu)
            p.wait("s", "sv", vfu)
    tap("fuse", fuse[:, 0:4, :].rearrange("p a b -> p (a b)"))

    p.wait("g", "sv", p.n["sv"])
    p.wait("g", "ss", p.n["ss"])
    wst_guard[6] = wst_guard[7] = max(wst_guard[6], wst_guard[7])
    for mo in range(16):
        bi = mo % 2
        gv = wst_guard.get(6 + bi, 0) if mo < 2 else dwb_guard.get(bi, 0)
        p.wait("g", "st", gv)
        ld = p.dma("g", "dg", dwb[:, bi, :], dQ8d.ap()[mo])
        p.wait("t", "dg", ld)
        for tb in range(NTB):
            b0 = (mo * NTB + tb) % 4
            bank_wait("t", b0)
            for ki in range(MI):
                mm(pb[b0][:, :], dwb[:, bi, ki * 128:(ki + 1) * 128],
                   fuse[:, ki, tb * NT:(tb + 1) * NT],
                   start=(ki == 0), stop=(ki == MI - 1),
                   inc=("st" if ki == MI - 1 else None))
            tdn = p.n["st"]
            if tb == NTB - 1:
                dwb_guard[bi] = tdn
            p.wait("s", "st", tdn)
            sdn = sact(ao[:, mo, tb * NT:(tb + 1) * NT], pb[b0][:, :], AF.Copy,
                       scale=dsc[:, mo:mo + 1])
            bank_set(b0, "ss", sdn)
    p.wait("v", "ss", p.n["ss"])
    tap("ffo", ao[:, :, :].rearrange("p k t -> p (k t)"))
    last_dn = p.n["st"]
    p.wait("y", "st", last_dn)
    p.wait("s", "st", last_dn)
    rlv = p.dma("y", "dy", resid[:, :, :].rearrange("p k t -> p (k t)"), dSpill.ap())
    p.wait("v", "dy", rlv)
    p.wait("t", "dy", rlv)

    post_ln_update()

    p.wait("y", "sv", p.n["sv"])
    for k in range(KD):
        ov = p.dma("y", "dy", dout.ap()[k], resid[:, k, :])
    p.wait("y", "dy", ov)
    p.wait("g", "dg", p.n["dg"])

    lp = nc.allow_low_precision(
        reason="f32r intermediates (fp22 reads) are well inside the "
               "validated tolerance; accumulations stay fp32 in PSUM")
    lp.__enter__()
    with nc.Block() as block:
        @block.tensor
        def _(eng):
            for f in p.ops["t"]:
                f(eng)

        @block.vector
        def _(eng):
            for f in p.ops["v"]:
                f(eng)

        @block.scalar
        def _(eng):
            for f in p.ops["s"]:
                f(eng)

        @block.gpsimd
        def _(eng):
            for f in p.ops["g"]:
                f(eng)

        @block.sync
        def _(eng):
            for f in p.ops["y"]:
                f(eng)

    lp.__exit__(None, None, None)
    stack.close()
    return nc


def _sub(dram, i):
    class W:
        def ap(self):
            return dram.ap()[i]
    return W()


def host_inputs(inputs):
    x = np.asarray(inputs["L0_attn_x"])[0]
    Wq = _retile(np.asarray(inputs["L0_attn_Wq"]))
    Wk = np.stack([_retile(np.asarray(inputs["L0_attn_Wk"])[i]) for i in range(NH)])
    Wv = np.stack([_retile(np.asarray(inputs["L0_attn_Wv"])[i]) for i in range(NH)])
    Wo = _retile(np.asarray(inputs["L0_attn_Wo"]))
    Wg0 = np.asarray(inputs["L0_attn_Wg"]).reshape(KD, 128, 3)
    Wg = np.zeros((128, KD * 4), dtype=np.float32)
    Wg.reshape(128, KD, 4)[:, :, 0:3] = Wg0.transpose(1, 0, 2)
    dl = np.zeros((1, 4), dtype=np.float32)
    dl[0, 0:3] = np.asarray(inputs["L0_attn_decay_logits"])
    osc_ = np.asarray(inputs["L0_attn_out_scale"]).reshape(1, 1)
    Wb = _retile(np.ascontiguousarray(np.asarray(inputs["L0_ffn_weight"]).T))
    A = np.asarray(inputs["L0_ffn_lora_A"])
    Ap = np.ascontiguousarray(_retile(np.ascontiguousarray(A.reshape(128, D).T))
                              .reshape(128, D))
    B = np.asarray(inputs["L0_ffn_lora_B"])
    Bp = np.ascontiguousarray(
        _retile(np.ascontiguousarray(B.transpose(0, 2, 1).reshape(128, D)))
        .transpose(1, 0, 2).reshape(128, 16 * 128))
    Rt = np.ascontiguousarray(np.asarray(inputs["L0_ffn_router_w"]).T
                              .reshape(KD, 128, 8).transpose(1, 0, 2)
                              .reshape(128, KD * 8))
    W1 = _retile(np.asarray(inputs["L1_attn_W"]))
    b1 = _cols(np.asarray(inputs["L1_attn_b"]))
    GW = _retile(np.asarray(inputs["L1_ffn_gate_w"]))
    UW = _retile(np.asarray(inputs["L1_ffn_up_w"]))
    DW = _retile(np.asarray(inputs["L1_ffn_down_w"]))
    GS = _cols(np.asarray(inputs["L1_ffn_gate_scale"]))
    US = _cols(np.asarray(inputs["L1_ffn_up_scale"]))
    DS = _cols(np.asarray(inputs["L1_ffn_down_scale"]))

    import ml_dtypes
    identb = np.eye(128, dtype=ml_dtypes.bfloat16)
    identr = np.eye(128, dtype=np.float32)
    s_i = np.arange(128)
    expnT = np.maximum(s_i[:, None] - s_i[None, :] - 1, 0).astype(np.float32)
    causT = (s_i[:, None] > s_i[None, :]).astype(np.float32)
    tpos = np.arange(T)
    e1 = (127 - (tpos % 128)).astype(np.float32).reshape(1, T)
    e2 = (1023 - tpos).astype(np.float32).reshape(1, T)
    iotac = np.stack([np.arange(128, dtype=np.float32),
                      np.full(128, 128.0, dtype=np.float32)], axis=1)
    onesc = np.ones((128, 2), dtype=np.float32)
    ones1 = np.ones((1, 128), dtype=np.float32)
    ones1b = np.ones((1, 128), dtype=ml_dtypes.bfloat16)
    pm = (np.arange(128)[None, :] // 16 == np.arange(8)[:, None]).astype(ml_dtypes.bfloat16)
    sh = np.zeros((3, NH * 128), dtype=ml_dtypes.bfloat16)
    for hh in range(NH):
        sh[hh, hh * 128:(hh + 1) * 128] = 1.0
    cn = np.tile(np.array([[EPS, -0.5]], dtype=np.float32), (128, 1))

    common = dict(Wq=Wq, Wk=Wk, Wv=Wv, Wo=Wo, Wg=Wg, decay_logits=dl, out_scale=osc_,
                  Wbase=Wb, loraA=Ap, loraB=Bp, router=Rt, W1=W1, b1=b1,
                  gatew=GW, upw=UW, downw=DW, gates=GS, ups=US, downs=DS,
                  identb=identb, identr=identr, expnT=expnT, causT=causT,
                  e1row=e1, e2row=e2, iotac=iotac, onesc=onesc, ones1=ones1,
                  ones1b=ones1b, pmat=pm, selh=sh, cnsts=cn)
    in_maps = []
    for r in range(NCORES):
        xs = np.ascontiguousarray(x[r * T:(r + 1) * T, :].T).reshape(KD, 128, T)
        bexpo = np.array([[max(1024.0 * (rp - r - 1), 0.0) for rp in range(8)]],
                         dtype=np.float32)
        bmask = np.array([[1.0 if rp > r else 0.0 for rp in range(8)]],
                         dtype=np.float32)
        m = dict(common)
        m.update(x=xs, bexpo=bexpo, bmask=bmask)
        in_maps.append(m)
    return in_maps


def kernel(**inputs):
    in_maps = host_inputs(inputs)
    nc = build_program()
    res = run_bass_kernel_spmd(nc, in_maps, core_ids=list(range(NCORES)))
    outs = []
    for r in range(NCORES):
        o = np.asarray(res.results[r]["out"])
        outs.append(o.transpose(2, 0, 1).reshape(T, D))
    return np.concatenate(outs, axis=0)[None, :, :].astype(np.float32)


# version 10
# speedup vs baseline: 1.0481x; 1.0481x over previous
import numpy as np
import concourse.bass as bass
import concourse.mybir as mybir
from concourse.bass_utils import run_bass_kernel_spmd

F32 = mybir.dt.float32
F32R = mybir.dt.float32r
BF16 = mybir.dt.bfloat16
FP8 = mybir.dt.float8e4
AF = mybir.ActivationFunctionType
AL = mybir.AluOpType
AX = mybir.AxisListType.X

NCORES = 8
T = 1024
D = 2048
KD = 16
NT = 512
NTB = T // NT
DKH = 512
NH = 3
CH = 128
NCHK = T // CH
DI = 6144
MI = DI // 128
EPS = 1e-5
BIG = 1.0e30

KB = 1024


def _retile(w):
    din, dout = w.shape
    K, M = din // 128, dout // 128
    return np.ascontiguousarray(
        w.reshape(K, 128, M, 128).transpose(2, 1, 0, 3)).reshape(M, 128, K * 128)


def _cols(v):
    return np.ascontiguousarray(v.reshape(-1, 128).T)



class PB:
    ENGS = ("t", "v", "s", "g", "y")
    DQ = {"dg": ("dg", "dg2"), "dy": ("dy", "dy2")}

    def __init__(self):
        self.ops = {e: [] for e in self.ENGS}
        self.sem = {}
        self.n = {"dg": 0, "dy": 0, "st": 0, "sv": 0, "ss": 0, "sg": 0, "cc": 0}
        self.di = {"dg": 0, "dy": 0}

    def raw(self, e, fn):
        self.ops[e].append(fn)

    def _wait1(self, e, q, v):
        if v <= 0:
            return
        self.raw(e, lambda eng, q=q, v=v: eng.wait_ge(self.sem[q], v))

    def wait(self, e, q, v):
        if v <= 0:
            return
        if q in self.DQ:
            i = v // 16
            a, b = self.DQ[q]
            self._wait1(e, a, 16 * ((i + 1) // 2))
            self._wait1(e, b, 16 * (i // 2))
        else:
            self._wait1(e, q, v)

    def run(self, e, fn, inc=None):
        if inc is None:
            self.raw(e, fn)
            return self.n.get(None, 0)
        self.n[inc] += 1
        v = self.n[inc]
        self.raw(e, lambda eng, fn=fn, q=inc: fn(eng).then_inc(self.sem[q], 1))
        return v

    def dma(self, e, q, out, in_):
        self.di[q] += 1
        i = self.di[q]
        a, b = self.DQ[q]
        sem = a if i % 2 == 1 else b
        self._wait1(e, sem, 16 * ((i + 1) // 2 - 1))
        self.raw(e, lambda eng, out=out, in_=in_, s=sem:
                 eng.dma_start(out=out, in_=in_).then_inc(self.sem[s], 16))
        self.n[q] = 16 * i
        return 16 * i


def build_program(dbg=()):
    nc = bass.Bass("TRN2", target_bir_lowering=False)
    p = PB()

    dx = nc.dram_tensor("x", [128, KD * T], F32R, kind="ExternalInput").ap()
    dout = nc.dram_tensor("out", [128, KD * T], F32R, kind="ExternalOutput")

    dWq = nc.dram_tensor("Wq", [4, 128, D], F32R, kind="ExternalInput")
    dWk = nc.dram_tensor("Wk", [NH, 4, 128, D], F32R, kind="ExternalInput")
    dWv = nc.dram_tensor("Wv", [NH, 4, 128, D], F32R, kind="ExternalInput")
    dWo = nc.dram_tensor("Wo", [16, 128, DKH], F32, kind="ExternalInput")
    dWg = nc.dram_tensor("Wg", [128, KD * 4], F32R, kind="ExternalInput")
    dDL = nc.dram_tensor("decay_logits", [1, 4], F32, kind="ExternalInput").ap()
    dOS = nc.dram_tensor("out_scale", [1, 1], F32R, kind="ExternalInput").ap()
    dWb = nc.dram_tensor("Wbase", [16, 128, D], F32R, kind="ExternalInput")
    dA = nc.dram_tensor("loraA", [128, D], F32R, kind="ExternalInput")
    dB = nc.dram_tensor("loraB", [128, 16 * 128], F32R, kind="ExternalInput")
    dRt = nc.dram_tensor("router", [128, KD * 8], F32R, kind="ExternalInput")
    dW1 = nc.dram_tensor("W1", [MI, 128, D], F32, kind="ExternalInput")
    dB1 = nc.dram_tensor("b1", [128, MI], F32, kind="ExternalInput").ap()
    dGW = nc.dram_tensor("gatew", [MI, 128, D], F32, kind="ExternalInput")
    dUW = nc.dram_tensor("upw", [MI, 128, D], F32, kind="ExternalInput")
    dDW = nc.dram_tensor("downw", [16, 128, DI], F32, kind="ExternalInput")
    dGS = nc.dram_tensor("gates", [128, MI], F32, kind="ExternalInput").ap()
    dUS = nc.dram_tensor("ups", [128, MI], F32, kind="ExternalInput").ap()
    dDS = nc.dram_tensor("downs", [128, 16], F32, kind="ExternalInput").ap()

    dID = nc.dram_tensor("identb", [128, 128], BF16, kind="ExternalInput").ap()
    dIDr = nc.dram_tensor("identr", [128, 128], F32R, kind="ExternalInput").ap()
    dEXP = nc.dram_tensor("expnT", [128, 128], F32, kind="ExternalInput").ap()
    dCAU = nc.dram_tensor("causT", [128, 128], F32, kind="ExternalInput").ap()
    dE1 = nc.dram_tensor("e1row", [1, T], F32R, kind="ExternalInput").ap()
    dE2 = nc.dram_tensor("e2row", [1, T], F32R, kind="ExternalInput").ap()
    dIOT = nc.dram_tensor("iotac", [128, 2], F32, kind="ExternalInput").ap()
    dONE = nc.dram_tensor("onesc", [128, 2], F32R, kind="ExternalInput").ap()
    dONE1 = nc.dram_tensor("ones1", [1, 128], F32R, kind="ExternalInput").ap()
    dONE1b = nc.dram_tensor("ones1b", [1, 128], BF16, kind="ExternalInput").ap()
    dPM = nc.dram_tensor("pmat", [8, 128], BF16, kind="ExternalInput").ap()
    dSH = nc.dram_tensor("selh", [3, NH * 128], BF16, kind="ExternalInput").ap()
    dCN = nc.dram_tensor("cnsts", [128, 2], F32, kind="ExternalInput").ap()
    dBEX = nc.dram_tensor("bexpo", [1, 8], F32, kind="ExternalInput").ap()
    dBMK = nc.dram_tensor("bmask", [1, 8], F32, kind="ExternalInput").ap()

    dQ8g = nc.dram_tensor("q8g", [MI, 128, D], FP8)
    dQ8u = nc.dram_tensor("q8u", [MI, 128, D], FP8)
    dQ8d = nc.dram_tensor("q8d", [16, 128, DI], FP8)
    dSpill = nc.dram_tensor("spill", [128, KD * T], F32R)
    dBin = nc.dram_tensor("bstates_in", [NH * 128, 4 * DKH], BF16)
    dBout = nc.dram_tensor("bstates_out", [NCORES * NH * 128, 4 * DKH], BF16)
    dRin = nc.dram_tensor("rstats_in", [128, 48], F32)
    dRout = nc.dram_tensor("rstats_out", [128, 48], F32)

    dbg_t = {}
    _f32r_taps = {"ln1", "x1", "x2", "x3", "axt"}
    for name in dbg:
        shp = dict(ln1=[2, T], qT=[128, 4 * T], kT=[128, 4 * T], vTM=[128, 8 * DKH],
                   gate=[3, T], blend=[128, 4 * T], ao=[128, KD * T], x1=[128, KD * T],
                   x2=[128, KD * T], x3=[128, KD * T], wd=[8, T], axt=[128, T],
                   xn5=[128, KD * T], qL1=[128, KD * T], cfac=[128, 16],
                   xn6=[128, KD * T], fuse=[128, 4 * T], ffo=[128, KD * T],
                   Sst=[128, 4 * DKH], Ebn=[128, 4 * DKH],
                   qg=[128, 4 * T], qgd=[128, 4 * T], kdTM=[128, 8 * DKH],
                   blend0=[128, 4 * T], blendi=[128, 4 * T],
                   logd=[128, 4], djc=[128, 3], dCc=[128, 3],
                   gateb=[128, T])[name]
        dt_ = F32R if name in _f32r_taps else F32
        dbg_t[name] = nc.dram_tensor("dbg_" + name, shp, dt_, kind="ExternalOutput")

    import contextlib
    stack = contextlib.ExitStack()
    arena_guard = stack.enter_context(nc.sbuf_tensor("arena", [128, 192 * KB],
                                                     mybir.dt.uint8))

    smalls = {}
    def small(name, shape, dtype):
        h = stack.enter_context(nc.sbuf_tensor(name, shape, dtype))
        smalls[name] = h
        return h

    rowA = small("rowA", [1, T], F32R)
    rowB = small("rowB", [1, T], F32R)
    rowC = small("rowC", [1, T], F32R)
    csq = small("csq", [128, 64], F32)
    rcol = small("rcol", [128, 8], F32)
    zcol = small("zcol", [128, 8], F32)
    logd = small("logd", [128, 4], F32)
    djc = small("djc", [128, 4], F32)
    dCc = small("dCc", [128, 4], F32)
    wcs = small("wcs", [128, 8], F32)
    osc = small("oscol", [128, 1], F32)
    identb = small("identb_s", [128, 128], BF16)
    identr = small("identr_s", [128, 128], F32R)
    onesc = small("onesc_s", [128, 2], F32R)
    cnst = small("cnst_s", [128, 2], F32)
    nc.const_aps.register(F32, EPS, cnst[:, 0:1])
    nc.const_aps.register(F32, -0.5, cnst[:, 1:2])
    onescb = small("onescb", [128, 2], BF16)
    iotac = small("iotac_s", [128, 2], F32)
    ones1 = small("ones1_s", [1, 128], F32R)
    ones1b = small("ones1b_s", [1, 128], BF16)
    pmat = small("pmat_s", [8, 128], BF16)
    dlrow = small("dlrow", [1, 16], F32)
    dlogr = small("dlogr", [1, 4], F32R)
    oscs = small("oscs", [1, 2], F32R)
    bxrow = small("bxrow", [1, 16], F32)
    wgs = small("wgs", [128, KD * 4], F32R)
    rts = small("rts", [128, KD * 8], F32R)

    base_addr = nc.lookup_mloc(arena_guard).addr
    alias_n = [0]

    def alias(shape, dtype, off_kb):
        alias_n[0] += 1
        return nc.alloc_sbuf_tensor_at(
            "al%d" % alias_n[0], shape, dtype, offset=base_addr + int(off_kb * KB))

    lA = alias([128, KD * 128], F32R, 128)
    lB = alias([128, 16 * 128], F32R, 136)
    axt = alias([128, T], F32, 144)
    axw = alias([128, T], F32R, 148)
    expnT = alias([128, 128], F32, 96)
    causT = alias([128, 128], F32, 96.5)
    gtT = alias([3, T], BF16, 97)
    maskb = alias([128, NH * 128], BF16, 99)
    selh = alias([3, NH * 128], BF16, 99.75)
    wdt = alias([8, T], BF16, 84)
    glog = alias([128, 8 * 4], F32, 86)
    gmx = alias([128, 8 * 4], F32, 86.25)
    glog2 = alias([128, 8 * 4], F32, 87)
    rlog2 = alias([128, 8 * 8], F32, 87.25)
    rlog = alias([128, 8 * 8], F32, 86.5)
    rtop = alias([128, 8 * 4], F32, 86.75)
    acc = alias([128, 96], F32, 178)
    gst = alias([128, 64], F32, 178.5)
    cbf = alias([128, 32], BF16, 178.75)
    b1c = alias([128, MI], F32, 179)
    gsc = alias([128, MI], F32, 179.25)
    usc = alias([128, MI], F32, 179.5)
    dsc = alias([128, 16], F32, 179.75)

    resid = alias([128, KD, T], F32R, 0)
    fuse = alias([128, MI, T], BF16, 0)
    scrA = alias([128, 2, T], F32R, 64)
    vFM = alias([128, 4, T], BF16, 136)
    bc = [alias([128, T], F32R, 72 + 4 * i) for i in range(3)]
    Sbf = alias([128, 4, DKH], BF16, 84)
    bstmp = alias([128, 4 * DKH], BF16, 84)
    Ebn = alias([128, 4, DKH], BF16, 88)
    sp92 = alias([128, 2 * T], BF16, 92)
    ao = alias([128, KD, T], BF16, 96)
    qT = alias([128, 4, T], BF16, 128)
    qg = alias([128, 4, T], BF16, 136)
    qgd = alias([128, 4, T], BF16, 144)
    kT = alias([128, 4, T], BF16, 152)
    kdTM = alias([128, NCHK, DKH], BF16, 160)
    vTM = alias([128, NCHK, DKH], BF16, 168)
    blendT = alias([128, 4, T], BF16, 176)
    wst = alias([128, 2, 8 * 128], F32R, 184)
    xn5 = alias([128, KD, T], BF16, 128)
    kscr = alias([128, T], BF16, 160)
    vscr = alias([128, T], BF16, 164)
    wbl1 = alias([128, 2, 8 * 128], BF16, 184)
    xn6 = alias([128, KD, T], BF16, 128)
    gwb = alias([128, 2, D], BF16, 160)
    uwb = alias([128, 2, D], BF16, 168)
    sgs = alias([128, NT], F32, 176)
    dwb = alias([128, 2, MI * 128], BF16, 128)

    pb = [stack.enter_context(nc.psum_tensor("pb%d" % i, [128, NT], F32))
          for i in range(6)]
    ptb = stack.enter_context(nc.psum_tensor("ptb", [128, 4 * 128], BF16))
    ptr = stack.enter_context(nc.psum_tensor("ptr", [128, 128], F32R))

    sems = {}
    for q in ("dg", "dg2", "dy", "dy2", "st", "sv", "ss", "sg", "cc"):
        sems[q] = stack.enter_context(nc.semaphore(q))
    p.sem = {q: sems[q] for q in sems}

    bankuser = {}

    def bank_wait(e, i):
        u = bankuser.get(i)
        if u:
            p.wait(e, u[0], u[1])

    def bank_set(i, q, v):
        bankuser[i] = (q, v)

    def vcopy(out, in_, inc="sv", e="v"):
        return p.run(e, lambda eng, o=out, x=in_: eng.tensor_copy(o, x), inc=inc)

    def vtt(out, a, b, op, inc="sv", e="v"):
        return p.run(e, lambda eng, o=out, a=a, b=b, op=op:
                     eng.tensor_tensor(o, a, b, op), inc=inc)

    def vts(out, a, s1, op, s2=None, op2=AL.bypass, inc="sv", e="v"):
        if s2 is None:
            return p.run(e, lambda eng, o=out, a=a, s1=s1, op=op:
                         eng.tensor_single_scalar(o, a, s1, op), inc=inc)
        return p.run(e, lambda eng, o=out, a=a, s1=s1, s2=s2, op=op, op2=op2:
                     eng.tensor_scalar(o, a, s1, s2, op, op2), inc=inc)

    def vstt(out, in0, sc, op0, in1, op1, inc="sv", e="v"):
        return p.run(e, lambda eng, o=out, a=in0, s=sc, b=in1, op0=op0, op1=op1:
                     eng.scalar_tensor_tensor(o, a, s, b, op0, op1), inc=inc)

    def vbounce():
        v = p.n["sv"]
        p.wait("s", "sv", v)
        s0 = p.run("s", lambda eng: eng.nop(), inc="ss")
        p.wait("v", "ss", s0)

    def sbounce():
        s = p.n["ss"]
        p.wait("v", "ss", s)
        v0 = p.run("v", lambda eng: eng.nop(), inc="sv")
        p.wait("s", "sv", v0)

    def sact(out, in_, func, bias=0.0, scale=1.0, accum=None, inc="ss"):
        if func not in (AF.Copy, AF.Reciprocal):
            if isinstance(bias, float):
                bias = np.float32(bias)
            if isinstance(scale, float):
                scale = np.float32(scale)
        return p.run("s", lambda eng, o=out, x=in_, f=func, b=bias, sc=scale, ac=accum:
                     eng.activation(o, x, f, bias=b, scale=sc, accum_out=ac), inc=inc)

    def mm(out, lhsT, rhs, start, stop, inc=None):
        return p.run("t", lambda eng, o=out, l=lhsT, r=rhs, a=start, b=stop:
                     eng.matmul(o, l, r, start=a, stop=b), inc=inc)

    def ttrans(out, in_, ident, inc=None):
        return p.run("t", lambda eng, o=out, x=in_, i=ident:
                     eng.transpose(o, x, i), inc=inc)

    def ln_stats(src_tile, src_dtype, rsq_ones, sq_in_tile=None):
        p.wait("t", "sv", p.n["sv"])
        p.wait("t", "ss", p.n["ss"])
        p.wait("s", "sv", p.n["sv"])
        bank_wait("t", 4)
        bank_wait("t", 5)
        for tb in range(NTB):
            for k in range(KD):
                mm(pb[4][0:1, :], rsq_ones, src_tile(k)[:, tb * NT:(tb + 1) * NT],
                   start=(k == 0), stop=(k == KD - 1),
                   inc=("st" if k == KD - 1 else None))
            p.wait("v", "st", p.n["st"])
            vsx = vcopy(rowA[0:1, tb * NT:(tb + 1) * NT], pb[4][0:1, :], inc="sv")
            p.wait("t", "sv", vsx)
        for tb in range(NTB):
            for k in range(KD):
                sq_src = (sq_in_tile or src_tile)(k)[:, tb * NT:(tb + 1) * NT]
                if k >= 2 or tb > 0:
                    p.wait("s", "st", sq_guard[k % 2])
                ssq = sact(scrA[:, k % 2, :NT], sq_src, AF.Square)
                p.wait("t", "ss", ssq)
                gv = mm(pb[5][0:1, :], onesc[:, 0:1], scrA[:, k % 2, :NT],
                        start=(k == 0), stop=(k == KD - 1),
                        inc="st")
                sq_guard[k % 2] = gv
            p.wait("v", "st", p.n["st"])
            vq = vcopy(rowB[0:1, tb * NT:(tb + 1) * NT], pb[5][0:1, :], inc="sv")
            p.wait("t", "sv", vq)
        vts(rowA[0:1, :], rowA[0:1, :], 1.0 / D, AL.mult)
        vts(rowB[0:1, :], rowB[0:1, :], 1.0 / D, AL.mult)
        vstt(rowC[0:1, :], rowA[0:1, :], -1.0, AL.mult, rowA[0:1, :], AL.mult)
        v1 = vtt(rowB[0:1, :], rowB[0:1, :], rowC[0:1, :], AL.add)
        p.wait("s", "sv", v1)
        s1 = sact(rowB[0:1, :], rowB[0:1, :], AF.Ln, bias=EPS)
        s2 = sact(rowB[0:1, :], rowB[0:1, :], AF.Exp, scale=-0.5)
        p.wait("v", "ss", s2)
        v2 = vstt(rowC[0:1, :], rowA[0:1, :], -1.0, AL.mult, rowB[0:1, :], AL.mult,
                  inc="sv")
        p.wait("t", "sv", v2)
        rows = [rowB, rowC]
        for j, dst in enumerate(bc[:2]):
            bank_wait("t", j)
            for tb in range(NTB):
                tm = mm(pb[j][:, :], ones1[0:1, :],
                        rows[j][0:1, tb * NT:(tb + 1) * NT],
                        start=True, stop=True, inc="st")
                p.wait("v", "st", tm)
                vv = vcopy(dst[:, tb * NT:(tb + 1) * NT], pb[j][:, :], inc="sv")
                p.wait("t", "sv", vv)
            bank_set(j, "sv", vv)
        return

    sq_guard = {0: 0, 1: 0}

    def rz_cols():
        for j, (src, dst) in enumerate(((bc[0], rcol), (bc[1], zcol))):
            for tg in range(8):
                tv = ttrans(ptr[:, :], src[:, tg * 128:(tg + 1) * 128],
                            identr[:, :], inc="st")
                p.wait("v", "st", tv)
                vv = vcopy(dst[:, tg:tg + 1], ptr[:, 0:1], inc="sv")
                p.wait("t", "sv", vv)

    def proj_f32r(wdram, nM, cs_slot, dst, rhs_tile=None, e1="v"):
        rhs_tile = rhs_tile or (lambda k, tb: resid[:, k, tb * NT:(tb + 1) * NT])
        last_load = {}
        for m in range(nM):
            for h in range(2):
                bi = (2 * m + h) % 2
                gv = wst_guard.get(bi, 0)
                p.wait("y", "st", gv)
                last_load[h] = p.dma("y", "dy", wst[:, bi, :],
                                     wdram.ap()[m, :, h * KD // 2 * 128:(h + 1) * KD // 2 * 128])
            p.wait("t", "dy", last_load[1])
            bank_wait("t", 2)
            for k in range(KD):
                mm(pb[2][:, 0:2], wst[:, 0 if k < 8 else 1,
                                      (k % 8) * 128:(k % 8 + 1) * 128],
                   onesc[:, 0:2], start=(k == 0), stop=(k == KD - 1),
                   inc=("st" if k == KD - 1 else None))
            tv = p.n["st"]
            p.wait("v", "st", tv)
            vv = vcopy(csq[:, cs_slot + m:cs_slot + m + 1], pb[2][:, 0:1], inc="sv")
            bank_set(2, "sv", vv)
            for tb in range(NTB):
                bi = m % 2
                bank_wait("t", bi)
                for k in range(KD):
                    mm(pb[bi][:, :], wst[:, 0 if k < 8 else 1, (k % 8) * 128:(k % 8 + 1) * 128],
                       rhs_tile(k, tb), start=(k == 0), stop=(k == KD - 1),
                       inc=("st" if k == KD - 1 else None))
                tvv = p.n["st"]
                if tb == NTB - 1:
                    wst_guard[0] = tvv
                    wst_guard[1] = tvv
                p.wait("v", "st", tvv)
                sl = slice(tb * NT, (tb + 1) * NT)
                vtt(scrA[:, 0, :NT], pb[bi][:, :], bc[0][:, sl], AL.mult)
                vv = vstt(dst(m)[:, sl], bc[1][:, sl],
                          csq[:, cs_slot + m:cs_slot + m + 1], AL.mult,
                          scrA[:, 0, :NT], AL.add, inc="sv")
                bank_set(bi, "sv", vv)

    wst_guard = {0: 0, 1: 0}
    dwb_guard = {0: 0, 1: 0}

    p.dma("y", "dy", resid[:, :, :].rearrange("p k t -> p (k t)"), dx)
    xload = p.n["dy"]
    for dst, src in ((identb, dID), (identr, dIDr), (expnT, dEXP), (causT, dCAU),
                     (onesc, dONE), (iotac, dIOT), (ones1, dONE1), (ones1b, dONE1b),
                     (pmat, dPM), (selh, dSH), (cnst, dCN),
                     (wgs, dWg.ap()), (rts, dRt.ap()),
                     ):
        p.dma("y", "dy", dst[:, :] if not hasattr(dst, "ndim") else dst, src)
    p.dma("g", "dg", dlrow[0:1, 0:4], dDL)
    p.dma("g", "dg", bxrow[0:1, 0:8], dBEX)
    p.dma("g", "dg", bxrow[0:1, 8:16], dBMK)
    p.dma("g", "dg", oscs[0:1, 0:1], dOS)
    p.dma("g", "dg", oscs[0:1, 1:2], dOS)
    consts_g = p.n["dg"]
    consts_y = p.n["dy"]

    p.wait("g", "dy", consts_y)
    for (src, dst) in ((dGW, dQ8g), (dUW, dQ8u), (dDW, dQ8d)):
        n0 = src.ap().shape[0]
        step = n0 // 4
        for i in range(0, n0, step):
            p.dma("g", "dg", dst.ap()[i:i + step], src.ap()[i:i + step])
    quant_done = p.n["dg"]

    p.wait("v", "dy", consts_y)
    p.wait("v", "dg", consts_g)
    vcopy(onescb[:, :], onesc[:, :])
    p.wait("t", "dg", consts_g)
    p.wait("t", "dy", consts_y)
    tosc = mm(pb[3][:, 0:2], ones1[0:1, :], oscs[0:1, 0:2],
              start=True, stop=True, inc="st")
    p.wait("v", "st", tosc)
    vosc = vcopy(osc[:, 0:1], pb[3][:, 0:1], inc="sv")

    p.wait("s", "dg", consts_g)
    p.wait("s", "dy", consts_y)
    s0 = sact(dlrow[0:1, 4:8], dlrow[0:1, 0:4], AF.Sigmoid)
    sbounce()
    s1 = sact(dlrow[0:1, 8:12], dlrow[0:1, 4:8], AF.Ln)
    sbounce()
    s1r = sact(dlogr[0:1, 0:4], dlrow[0:1, 8:12], AF.Identity)
    p.wait("t", "ss", s1r)
    p.wait("t", "sv", vosc)
    tld = mm(pb[3][:, 2:6], ones1[0:1, :], dlogr[0:1, 0:4],
             start=True, stop=True, inc="st")
    p.wait("v", "st", tld)
    vld = vcopy(logd[:, 0:4], pb[3][:, 2:6], inc="sv")
    bank_set(3, "sv", vld)
    p.wait("t", "sv", vld)
    p.wait("s", "sv", vld)
    vm_hist = {}
    for i in range(NH):
        sact(djc[:, i:i + 1], iotac[:, 0:1], AF.Exp, scale=logd[:, i:i + 1])
        sact(dCc[:, i:i + 1], iotac[:, 1:2], AF.Exp, scale=logd[:, i:i + 1])
        if i >= 2:
            p.wait("s", "sv", vm_hist[i % 2])
        sm = sact(scrA[:, i % 2, 0:128], expnT[:, :], AF.Exp,
                  scale=logd[:, i:i + 1])
        p.wait("v", "ss", sm)
        vm_hist[i % 2] = vtt(maskb[:, i * 128:(i + 1) * 128], scrA[:, i % 2, 0:128],
                             causT[:, :], AL.mult, inc="sv")

    p.wait("t", "dy", xload)

    tap_guard = []

    def tap(name, src_ap, rows=128):
        if name not in dbg_t:
            return
        p.wait("g", "sv", p.n["sv"])
        p.wait("g", "ss", p.n["ss"])
        p.wait("g", "st", p.n["st"])
        tv = p.dma("g", "dg", dbg_t[name].ap()[0:rows], src_ap)
        tap_guard.append(tv)
        for _e in ("t", "v", "s", "y", "g"):
            p.wait(_e, "dg", tv)

    tap("logd", logd[:, 0:4])
    tap("djc", djc[:, 0:3])
    tap("dCc", dCc[:, 0:3])

    ln_stats(lambda k: resid[:, k, :], F32R, onesc[:, 0:1])
    rz_cols()
    if "ln1" in dbg_t:
        p.wait("g", "sv", p.n["sv"])
        tap_guard.append(p.dma("g", "dg", dbg_t["ln1"].ap()[0:1], rowB[0:1, :]))
        tap_guard.append(p.dma("g", "dg", dbg_t["ln1"].ap()[1:2], rowC[0:1, :]))

    GP = 4
    bank_wait("t", 3)
    for tg in range(8):
        for k in range(KD):
            mm(pb[3][:, tg * GP:(tg + 1) * GP], resid[:, k, tg * 128:(tg + 1) * 128],
               wgs[:, k * GP:(k + 1) * GP], start=(k == 0), stop=(k == KD - 1),
               inc=("st" if k == KD - 1 else None))
    tgl = p.n["st"]
    p.wait("s", "st", tgl)
    p.wait("s", "sv", p.n["sv"])
    sgl = 0
    for tg in range(8):
        sgl = sact(glog[:, tg * GP:(tg + 1) * GP], pb[3][:, tg * GP:(tg + 1) * GP],
                   AF.Copy, scale=rcol[:, tg:tg + 1])
    p.wait("v", "ss", sgl)
    bank_wait("t", 4)
    for k in range(KD):
        mm(pb[4][0:1, 0:GP], onesc[:, 0:1], wgs[:, k * GP:(k + 1) * GP],
           start=(k == 0), stop=(k == KD - 1), inc=("st" if k == KD - 1 else None))
    tcg = p.n["st"]
    p.wait("v", "st", tcg)
    vcg = vcopy(rowA[0:1, 0:GP], pb[4][0:1, 0:GP], inc="sv")
    p.wait("t", "sv", vcg)
    tcb = mm(pb[4][:, 0:GP], ones1[0:1, :], rowA[0:1, 0:GP],
             start=True, stop=True, inc="st")
    p.wait("s", "st", tcb)
    scs = sact(scrA[:, 0, 0:GP], pb[4][:, 0:GP], AF.Copy)
    p.wait("v", "ss", scs)
    for tg in range(8):
        vstt(glog2[:, tg * GP:(tg + 1) * GP], scrA[:, 0, 0:GP], zcol[:, tg:tg + 1],
             AL.mult, glog[:, tg * GP:(tg + 1) * GP], AL.add)
    vg1 = p.n["sv"]
    bank_set(3, "sv", vg1)
    bank_set(4, "ss", scs)
    p.wait("s", "sv", vg1)
    sex = 0
    for tg in range(8):
        sex = sact(glog[:, tg * GP:(tg + 1) * GP], glog2[:, tg * GP:(tg + 1) * GP],
                   AF.Exp)
    p.wait("v", "ss", sex)
    vsum = p.run("v", lambda eng: eng.reduce_sum(
        gmx[:, 0:8].rearrange("p (a b) -> p a b", b=1),
        glog[:, :].rearrange("p (a b) -> p a b", b=GP), axis=AX), inc="sv")
    p.wait("s", "sv", vsum)
    sm1 = sact(gmx[:, 8:16], gmx[:, 0:8], AF.Identity, bias=-1.0)
    p.wait("v", "ss", sm1)
    p.run("v", lambda eng: eng.reciprocal(gmx[:, 16:24], gmx[:, 8:16]), inc="sv")
    vrc = p.n["sv"]
    p.wait("s", "sv", vrc)
    spk = 0
    for tg in range(8):
        spk = sact(sp92[:, tg * 3:(tg + 1) * 3], glog[:, tg * GP:tg * GP + 3],
                   AF.Copy, scale=gmx[:, 16 + tg:17 + tg])
    p.wait("t", "ss", spk)
    for tg in range(8):
        tv = ttrans(ptb[0:3, 0:128], sp92[:, tg * 3:(tg + 1) * 3], identb[:, :],
                    inc="st")
        p.wait("v", "st", tv)
        vv = vcopy(gtT[0:3, tg * 128:(tg + 1) * 128], ptb[0:3, 0:128], inc="sv")
        p.wait("t", "sv", vv)
    tap("gate", gtT[0:3, :], rows=3)

    proj_f32r(dWq, 4, 0, lambda m: qT[:, m, :])
    tap("qT", qT[:, :, :].rearrange("p m t -> p (m t)"))

    bstore = []
    for hI in range(NH):
        proj_f32r(_sub(dWk, hI), 4, 8, lambda m: kT[:, m, :])
        proj_f32r(_sub(dWv, hI), 4, 16, lambda m: vFM[:, m, :])
        for tg in range(NCHK):
            for dk4 in range(4):
                tv = ttrans(ptb[:, dk4 * 128:(dk4 + 1) * 128],
                            kT[:, dk4, tg * 128:(tg + 1) * 128], identb[:, :],
                            inc=("st" if dk4 == 3 else None))
            p.wait("s", "st", tv)
            sv_ = sact(kdTM[:, tg, :], ptb[:, 0:512], AF.Copy, scale=djc[:, hI:hI + 1])
            p.wait("t", "ss", sv_)
            for dk4 in range(4):
                tv = ttrans(ptb[:, dk4 * 128:(dk4 + 1) * 128],
                            vFM[:, dk4, tg * 128:(tg + 1) * 128], identb[:, :],
                            inc=("st" if dk4 == 3 else None))
            p.wait("s", "st", tv)
            sv_ = sact(vTM[:, tg, :], ptb[:, 0:512], AF.Copy)
            p.wait("t", "ss", sv_)
        vfm_done = tv
        if hI == 0:
            tap("kT", kT[:, :, :].rearrange("p m t -> p (m t)"))
            tap("vTM", vTM[:, :, :].rearrange("p a b -> p (a b)"))
            tap("kdTM", kdTM[:, :, :].rearrange("p a b -> p (a b)"))
        p.wait("y", "ss", p.n["ss"])
        ld = p.dma("y", "dy", rowA[0:1, :], dE1)
        p.wait("s", "dy", ld)
        se1 = sact(rowC[0:1, :], rowA[0:1, :], AF.Exp, scale=dlrow[0:1, 8 + hI:9 + hI])
        gb = 0
        bank_wait("t", 0)
        for tb in range(NTB):
            tvb = mm(pb[0][:, :], selh[:, hI * 128:(hI + 1) * 128],
                     gtT[0:3, tb * NT:(tb + 1) * NT],
                     start=True, stop=True, inc="st")
            p.wait("v", "st", tvb)
            gb = vcopy(bc[2][:, tb * NT:(tb + 1) * NT], pb[0][:, :], inc="sv")
            p.wait("t", "sv", gb)
        p.wait("v", "st", vfm_done)
        if hI == 0 and "gateb" in dbg_t:
            tap("gateb", bc[2][:, :])
            p.wait("v", "dg", p.n["dg"])
        for k4 in range(4):
            vtt(qg[:, k4, :], qT[:, k4, :], bc[2][:, :], AL.mult)
        p.wait("t", "ss", se1)
        for tb in range(NTB):
            tvb = mm(pb[0][:, :], ones1[0:1, :], rowC[0:1, tb * NT:(tb + 1) * NT],
                     start=True, stop=True, inc="st")
            p.wait("v", "st", tvb)
            vcb = vcopy(bc[2][:, tb * NT:(tb + 1) * NT], pb[0][:, :], inc="sv")
            p.wait("t", "sv", vcb)
        bank_set(0, "sv", vcb)
        for k4 in range(4):
            vqg = vtt(qgd[:, k4, :], qg[:, k4, :], bc[2][:, :], AL.mult, inc="sv")
        p.wait("t", "sv", vqg)
        if hI == 0:
            tap("qg", qg[:, :, :].rearrange("p m t -> p (m t)"))
            tap("qgd", qgd[:, :, :].rearrange("p m t -> p (m t)"))

        for ci, c in enumerate(reversed(range(NCHK))):
            csl = slice(c * CH, (c + 1) * CH)
            bank_wait("t", 0)
            for dk4 in range(4):
                mm(pb[0][:, 0:128], kT[:, dk4, csl], qg[:, dk4, csl],
                   start=(dk4 == 0), stop=(dk4 == 3), inc=("st" if dk4 == 3 else None))
            tsc = p.n["st"]
            p.wait("v", "st", tsc)
            vsc = vtt(sp92[:, 0:128], pb[0][:, 0:128], maskb[:, hI * 128:(hI + 1) * 128],
                      AL.mult, inc="sv")
            bank_set(0, "sv", vsc)
            p.wait("t", "sv", vsc)
            bank_wait("t", 1)
            for dv4 in range(4):
                mm(pb[1][:, dv4 * 128:(dv4 + 1) * 128],
                   vTM[:, c, dv4 * 128:(dv4 + 1) * 128], sp92[:, 0:128],
                   start=True, stop=(ci == 0),
                   inc=("st" if (ci == 0 and dv4 == 3) else None))
                if ci > 0:
                    for dk4 in range(4):
                        mm(pb[1][:, dv4 * 128:(dv4 + 1) * 128],
                           Sbf[:, dk4, dv4 * 128:(dv4 + 1) * 128], qgd[:, dk4, csl],
                           start=False, stop=(dk4 == 3),
                           inc=("st" if (dk4 == 3 and dv4 == 3) else None))
            tre = p.n["st"]
            p.wait("v", "st", tre)
            if hI == 0:
                for dv4 in range(4):
                    vr = vcopy(blendT[:, dv4, csl], pb[1][:, dv4 * 128:(dv4 + 1) * 128],
                               inc=("sv" if dv4 == 3 else None))
            else:
                for dv4 in range(4):
                    vr = vtt(blendT[:, dv4, csl], blendT[:, dv4, csl],
                             pb[1][:, dv4 * 128:(dv4 + 1) * 128], AL.add,
                             inc=("sv" if dv4 == 3 else None))
            bank_set(1, "sv", vr)
            for dk4 in range(4):
                bank_wait("t", 2 + dk4)
                mm(pb[2 + dk4][:, :], kdTM[:, c, dk4 * 128:(dk4 + 1) * 128],
                   vTM[:, c, :], start=True, stop=True,
                   inc="st")
                tsu = p.n["st"]
                p.wait("v", "st", tsu)
                if ci == 0:
                    vu = vcopy(Sbf[:, dk4, :], pb[2 + dk4][:, :], inc="sv")
                else:
                    vu = vstt(Sbf[:, dk4, :], Sbf[:, dk4, :], dCc[:, hI:hI + 1],
                              AL.mult, pb[2 + dk4][:, :], AL.add, inc="sv")
                bank_set(2 + dk4, "sv", vu)
                p.wait("t", "sv", vu)
        p.wait("g", "sv", p.n["sv"])
        bstore.append(p.dma("g", "dg", dBin.ap()[hI * 128:(hI + 1) * 128, :],
                            Sbf[:, :, :].rearrange("p a b -> p (a b)")))
        if hI == 0:
            tap("Sst", Sbf[:, :, :].rearrange("p a b -> p (a b)"))
            tap("blend0", blendT[:, :, :].rearrange("p a b -> p (a b)"))

    p.wait("g", "dg", bstore[-1])
    p.run("g", lambda eng: eng.collective_compute(
        "AllGather", AL.bypass, replica_groups=[list(range(NCORES))],
        ins=[dBin.ap().opt()], outs=[dBout.ap().opt()]), inc="cc")
    p.wait("g", "cc", p.n["cc"])
    ebn_guard = 0
    for hI in range(NH):
        p.wait("s", "sv", p.n["sv"])
        swr = sact(rowA[0:1, 0:8], bxrow[0:1, 0:8], AF.Exp,
                   scale=dlrow[0:1, 8 + hI:9 + hI])
        p.wait("v", "ss", swr)
        vw = vtt(rowA[0:1, 8:16], rowA[0:1, 0:8], bxrow[0:1, 8:16], AL.mult,
                 inc="sv")
        p.wait("t", "sv", vw)
        bank_wait("t", 0)
        twb = mm(pb[0][:, 0:8], ones1[0:1, :], rowA[0:1, 8:16], start=True, stop=True,
                 inc="st")
        p.wait("v", "st", twb)
        vwc = vcopy(wcs[:, 0:8], pb[0][:, 0:8], inc="sv")
        bank_set(0, "sv", vwc)
        for r in range(NCORES):
            lb = p.dma("g", "dg", bstmp[:, :],
                       dBout.ap()[(r * NH + hI) * 128:(r * NH + hI + 1) * 128, :])
            p.wait("v", "dg", lb)
            p.wait("v", "sv", vwc)
            if r == 0:
                p.wait("v", "st", ebn_guard)
                vE = vts(Ebn[:, :, :].rearrange("p a b -> p (a b)"), bstmp[:, :],
                         wcs[:, 0:1], AL.mult, inc="sv")
            else:
                vE = vstt(Ebn[:, :, :].rearrange("p a b -> p (a b)"), bstmp[:, :],
                          wcs[:, r:r + 1], AL.mult,
                          Ebn[:, :, :].rearrange("p a b -> p (a b)"), AL.add, inc="sv")
            p.wait("g", "sv", vE)
        if hI == 0:
            tap("Ebn", Ebn[:, :, :].rearrange("p a b -> p (a b)"))
        p.wait("y", "ss", p.n["ss"])
        ld = p.dma("y", "dy", rowB[0:1, :], dE2)
        p.wait("s", "dy", ld)
        se2 = sact(rowC[0:1, :], rowB[0:1, :], AF.Exp, scale=dlrow[0:1, 8 + hI:9 + hI])
        p.wait("t", "ss", se2)
        bank_wait("t", 1)
        for tb in range(NTB):
            tvb = mm(pb[1][:, :], ones1[0:1, :], rowC[0:1, tb * NT:(tb + 1) * NT],
                     start=True, stop=True, inc="st")
            p.wait("v", "st", tvb)
            ve2 = vcopy(bc[2][:, tb * NT:(tb + 1) * NT], pb[1][:, :], inc="sv")
            p.wait("t", "sv", ve2)
        for tb in range(NTB):
            tvb = mm(pb[1][:, :], selh[:, hI * 128:(hI + 1) * 128],
                     gtT[0:3, tb * NT:(tb + 1) * NT],
                     start=True, stop=True, inc="st")
            p.wait("v", "st", tvb)
            vgb = vstt(bc[2][:, tb * NT:(tb + 1) * NT], bc[2][:, tb * NT:(tb + 1) * NT],
                       1.0, AL.mult, pb[1][:, :], AL.mult, inc="sv")
            p.wait("t", "sv", vgb)
        bank_set(1, "sv", vgb)
        p.wait("v", "st", ebn_guard)
        for k4 in range(4):
            vq = vtt(qgd[:, k4, :], qT[:, k4, :], bc[2][:, :], AL.mult, inc="sv")
        p.wait("t", "sv", vq)
        for dv4 in range(4):
            for tb in range(NTB):
                bi = (dv4 * NTB + tb) % 2
                bank_wait("t", bi)
                for dk4 in range(4):
                    mm(pb[bi][:, :], Ebn[:, dk4, dv4 * 128:(dv4 + 1) * 128],
                       qgd[:, dk4, tb * NT:(tb + 1) * NT],
                       start=(dk4 == 0), stop=(dk4 == 3),
                       inc=("st" if dk4 == 3 else None))
                tb_ = p.n["st"]
                p.wait("v", "st", tb_)
                vb = vtt(blendT[:, dv4, tb * NT:(tb + 1) * NT],
                         blendT[:, dv4, tb * NT:(tb + 1) * NT], pb[bi][:, :],
                         AL.add, inc="sv")
                bank_set(bi, "sv", vb)
        ebn_guard = p.n["st"]
    tap("blend", blendT[:, :, :].rearrange("p a b -> p (a b)"))

    p.wait("y", "sv", p.n["sv"])
    p.wait("y", "st", p.n["st"])
    p.dma("y", "dy", lA[:, :], dA.ap())
    lab_load = p.dma("y", "dy", lB[:, :], dB.ap())

    for k4 in range(4):
        vos = vts(blendT[:, k4, :], blendT[:, k4, :], osc[:, 0:1], AL.mult,
                  inc=("sv" if k4 == 3 else None))
    p.wait("t", "sv", vos)

    p.wait("g", "st", max(wst_guard[0], wst_guard[1]))
    for m in range(16):
        gv = wst_guard.get(4 + m % 2, 0)
        p.wait("g", "st", gv)
        wl = p.dma("g", "dg", wbl1[:, m % 2, 0:512], dWo.ap()[m])
        p.wait("t", "dg", wl)
        for tb in range(NTB):
            bi = m % 2
            bank_wait("t", bi)
            for k4 in range(4):
                mm(pb[bi][:, :], wbl1[:, m % 2, k4 * 128:(k4 + 1) * 128],
                   blendT[:, k4, tb * NT:(tb + 1) * NT],
                   start=(k4 == 0), stop=(k4 == 3), inc=("st" if k4 == 3 else None))
            tww = p.n["st"]
            if tb == NTB - 1:
                wst_guard[4 + m % 2] = tww
            p.wait("v", "st", tww)
            vw = vcopy(ao[:, m, tb * NT:(tb + 1) * NT], pb[bi][:, :], inc="sv")
            bank_set(bi, "sv", vw)
    wst_guard[0] = wst_guard[1] = max(wst_guard[4], wst_guard[5])
    tap("ao", ao[:, :, :].rearrange("p k t -> p (k t)"))

    def post_ln_update():
        ln_stats(lambda k: ao[:, k, :], BF16, onescb[:, 0:1],
                 sq_in_tile=lambda k: ao[:, k, :])
        for k in range(KD):
            v1 = vtt(scrA[:, k % 2, :], ao[:, k, :], bc[0][:, :], AL.mult)
            v2 = vtt(scrA[:, k % 2, :], scrA[:, k % 2, :], bc[1][:, :], AL.add)
            v3 = vtt(resid[:, k, :], resid[:, k, :], scrA[:, k % 2, :], AL.add,
                     inc="sv")
        p.wait("t", "sv", p.n["sv"])
        p.wait("s", "sv", p.n["sv"])

    post_ln_update()
    tap("x1", resid[:, :, :].rearrange("p k t -> p (k t)"))

    ln_stats(lambda k: resid[:, k, :], F32R, onesc[:, 0:1])
    rz_cols()
    vrr = p.run("v", lambda eng: eng.reciprocal(rowA[0:1, :], rowB[0:1, :]), inc="sv")
    p.wait("t", "sv", vrr)
    bank_wait("t", 0)
    for tb in range(NTB):
        tvb = mm(pb[0][:, :], ones1[0:1, :], rowA[0:1, tb * NT:(tb + 1) * NT],
                 start=True, stop=True, inc="st")
        p.wait("v", "st", tvb)
        vv = vcopy(bc[2][:, tb * NT:(tb + 1) * NT], pb[0][:, :], inc="sv")
        p.wait("t", "sv", vv)
        bank_set(0, "sv", vv)

    bank_wait("t", 3)
    for tg in range(8):
        for k in range(KD):
            mm(pb[3][:, tg * 8:(tg + 1) * 8], resid[:, k, tg * 128:(tg + 1) * 128],
               rts[:, k * 8:(k + 1) * 8], start=(k == 0), stop=(k == KD - 1),
               inc=("st" if k == KD - 1 else None))
    trl = p.n["st"]
    bank_wait("t", 4)
    for k in range(KD):
        mm(pb[4][0:1, 0:8], onesc[:, 0:1], rts[:, k * 8:(k + 1) * 8],
           start=(k == 0), stop=(k == KD - 1), inc=("st" if k == KD - 1 else None))
    tcr = p.n["st"]
    p.wait("v", "st", tcr)
    vcr = vcopy(rowA[0:1, 0:8], pb[4][0:1, 0:8], inc="sv")
    p.wait("t", "sv", vcr)
    tcb2 = mm(pb[4][:, 0:8], ones1[0:1, :], rowA[0:1, 0:8], start=True, stop=True,
              inc="st")
    p.wait("v", "st", tcb2)
    vcs = vcopy(scrA[:, 0, 0:8], pb[4][:, 0:8], inc="sv")
    bank_set(4, "sv", vcs)
    p.wait("s", "st", trl)
    p.wait("s", "sv", vcs)
    srl = 0
    for tg in range(8):
        srl = sact(rlog[:, tg * 8:(tg + 1) * 8], pb[3][:, tg * 8:(tg + 1) * 8],
                   AF.Copy, scale=rcol[:, tg:tg + 1])
    p.wait("v", "ss", srl)
    for tg in range(8):
        vstt(rlog2[:, tg * 8:(tg + 1) * 8], scrA[:, 0, 0:8], zcol[:, tg:tg + 1],
             AL.mult, rlog[:, tg * 8:(tg + 1) * 8], AL.add)
    vrl = p.n["sv"]
    bank_set(3, "sv", vrl)
    vbounce()
    p.run("v", lambda eng: eng.reduce_max(
        rtop[:, 0:8].rearrange("p (a b) -> p a b", b=1),
        rlog2[:, :].rearrange("p (a b) -> p a b", b=8), axis=AX))
    vbounce()
    for tg in range(8):
        vts(sp92[:, tg * 8:(tg + 1) * 8],
            rlog2[:, tg * 8:(tg + 1) * 8], rtop[:, tg:tg + 1], AL.is_equal)
    vbounce()
    for tg in range(8):
        vstt(scrA[:, 0, tg * 8:(tg + 1) * 8], sp92[:, tg * 8:(tg + 1) * 8], -BIG,
             AL.mult, rlog2[:, tg * 8:(tg + 1) * 8], AL.add)
    vbounce()
    p.run("v", lambda eng: eng.reduce_max(
        rtop[:, 8:16].rearrange("p (a b) -> p a b", b=1),
        scrA[:, 0, 0:64].rearrange("p (a b) -> p a b", b=8), axis=AX))
    vbounce()
    v12 = vtt(rtop[:, 16:24], rtop[:, 0:8], rtop[:, 8:16], AL.subtract, inc="sv")
    p.wait("s", "sv", v12)
    sw1 = sact(rtop[:, 24:32], rtop[:, 16:24], AF.Sigmoid)
    p.wait("v", "ss", sw1)
    vts(gmx[:, 0:8], rtop[:, 24:32], 2.0, AL.mult)
    vts(gmx[:, 8:16], rtop[:, 24:32], -2.0, AL.mult, 2.0, AL.add)
    for tg in range(8):
        vts(scrA[:, 1, tg * 8:(tg + 1) * 8], scrA[:, 0, tg * 8:(tg + 1) * 8],
            rtop[:, 8 + tg:9 + tg], AL.is_equal)
    vbounce()
    for tg in range(8):
        vts(rlog[:, tg * 8:(tg + 1) * 8], sp92[:, tg * 8:(tg + 1) * 8],
            gmx[:, tg:tg + 1], AL.mult)
    vbounce()
    for tg in range(8):
        sl8 = slice(tg * 8, (tg + 1) * 8)
        vwd = vstt(sp92[:, sl8], scrA[:, 1, sl8], gmx[:, 8 + tg:9 + tg],
                   AL.mult, rlog[:, sl8], AL.add, inc="sv")
    p.wait("t", "sv", p.n["sv"])
    for tg in range(8):
        tv = ttrans(ptb[0:8, 0:128], sp92[:, tg * 8:(tg + 1) * 8], identb[:, :],
                    inc="st")
        p.wait("v", "st", tv)
        vv = vcopy(wdt[0:8, tg * 128:(tg + 1) * 128], ptb[0:8, 0:128], inc="sv")
        p.wait("t", "sv", vv)
    tap("wd", wdt[0:8, :], rows=8)

    p.wait("t", "dy", lab_load)
    for tb in range(NTB):
        bi = tb % 2
        bank_wait("t", bi)
        for k in range(KD):
            mm(pb[bi][:, :], lA[:, k * 128:(k + 1) * 128],
               resid[:, k, tb * NT:(tb + 1) * NT],
               start=(k == 0), stop=(k == KD - 1), inc=("st" if k == KD - 1 else None))
        tax = p.n["st"]
        if tb == 0:
            bank_wait("t", 2)
            for k in range(KD):
                mm(pb[2][:, 0:2], lA[:, k * 128:(k + 1) * 128], onesc[:, 0:2],
                   start=(k == 0), stop=(k == KD - 1),
                   inc=("st" if k == KD - 1 else None))
            tca = p.n["st"]
            p.wait("v", "st", tca)
            vca = vcopy(csq[:, 32:33], pb[2][:, 0:1], inc="sv")
            bank_set(2, "sv", vca)
        p.wait("v", "st", tax)
        sl = slice(tb * NT, (tb + 1) * NT)
        vtt(scrA[:, 0, :NT], pb[bi][:, :], bc[0][:, sl], AL.mult)
        vax = vstt(axt[:, sl], bc[1][:, sl], csq[:, 32:33], AL.mult,
                   scrA[:, 0, :NT], AL.add, inc="sv")
        bank_set(bi, "sv", vax)
    tap("axt", axt[:, :])
    for tb in range(NTB):
        bank_wait("t", 0)
        twd = mm(pb[0][:, :], pmat[:, :], wdt[0:8, tb * NT:(tb + 1) * NT],
                 start=True, stop=True, inc="st")
        p.wait("v", "st", twd)
        sl = slice(tb * NT, (tb + 1) * NT)
        vtt(scrA[:, 0, :NT], pb[0][:, :], bc[2][:, sl], AL.mult)
        vwx = vtt(axw[:, sl], axt[:, sl], scrA[:, 0, :NT], AL.mult, inc="sv")
        bank_set(0, "sv", vwx)
    p.wait("t", "sv", p.n["sv"])

    for m in range(16):
        for h in range(2):
            bi = h
            gv = wst_guard.get(bi, 0)
            p.wait("y", "st", gv)
            p.dma("y", "dy", wst[:, bi, :], dWb.ap()[m, :, h * 8 * 128:(h + 1) * 8 * 128])
        wl = p.n["dy"]
        p.wait("t", "dy", wl)
        bank_wait("t", 2)
        for k in range(KD):
            mm(pb[2][:, 0:2], wst[:, 0 if k < 8 else 1, (k % 8) * 128:(k % 8 + 1) * 128],
               onesc[:, 0:2], start=(k == 0), stop=(k == KD - 1),
               inc=("st" if k == KD - 1 else None))
        tcs = p.n["st"]
        p.wait("v", "st", tcs)
        vcsb = vcopy(csq[:, 40 + (m % 8):41 + (m % 8)], pb[2][:, 0:1], inc="sv")
        bank_set(2, "sv", vcsb)
        for tb in range(NTB):
            bi = m % 2
            bank_wait("t", bi)
            for k in range(KD):
                mm(pb[bi][:, :], wst[:, 0 if k < 8 else 1, (k % 8) * 128:(k % 8 + 1) * 128],
                   resid[:, k, tb * NT:(tb + 1) * NT],
                   start=(k == 0), stop=False)
            mm(pb[bi][:, :], lB[:, m * 128:(m + 1) * 128],
               axw[:, tb * NT:(tb + 1) * NT], start=False, stop=True, inc="st")
            tbm = p.n["st"]
            if tb == NTB - 1:
                wst_guard[0] = tbm
                wst_guard[1] = tbm
            p.wait("v", "st", tbm)
            p.wait("v", "sv", vcsb)
            sl = slice(tb * NT, (tb + 1) * NT)
            vtt(scrA[:, 0, :NT], pb[bi][:, :], bc[0][:, sl], AL.mult)
            vbs = vstt(ao[:, m, sl], bc[1][:, sl], csq[:, 40 + (m % 8):41 + (m % 8)],
                       AL.mult, scrA[:, 0, :NT], AL.add, inc="sv")
            bank_set(bi, "sv", vbs)

    post_ln_update()
    tap("x2", resid[:, :, :].rearrange("p k t -> p (k t)"))

    ln_stats(lambda k: resid[:, k, :], F32R, onesc[:, 0:1])
    for k in range(KD):
        vtt(scrA[:, k % 2, :], resid[:, k, :], bc[0][:, :], AL.mult)
        vx5 = vtt(xn5[:, k, :], scrA[:, k % 2, :], bc[1][:, :], AL.add, inc="sv")
    p.wait("t", "sv", vx5)
    tap("xn5", xn5[:, :, :].rearrange("p k t -> p (k t)"))

    q_l1 = ao
    wst_guard[4] = wst_guard[5] = max(wst_guard[0], wst_guard[1])
    p.wait("y", "st", wst_guard[0])
    for dst, src in ((b1c, dB1), (gsc, dGS), (usc, dUS), (dsc, dDS)):
        scl_load = p.dma("y", "dy", dst[:, :], src)
    p.wait("v", "dy", scl_load)
    p.wait("s", "dy", scl_load)
    for j in range(16):
        for part, mofs in (("q", 0), ("k", 16), ("v", 32)):
            m = mofs + j
            for h in range(2):
                bi = h
                gv = wst_guard.get(4 + bi, 0)
                p.wait("g", "st", gv)
                p.dma("g", "dg", wbl1[:, bi, :],
                      dW1.ap()[m, :, h * 8 * 128:(h + 1) * 8 * 128])
            wl = p.n["dg"]
            p.wait("t", "dg", wl)
            for tb in range(NTB):
                bi = (m * NTB + tb) % 2
                bank_wait("t", bi)
                for k in range(KD):
                    mm(pb[bi][:, :], wbl1[:, 0 if k < 8 else 1, (k % 8) * 128:(k % 8 + 1) * 128],
                       xn5[:, k, tb * NT:(tb + 1) * NT],
                       start=(k == 0), stop=(k == KD - 1),
                       inc=("st" if k == KD - 1 else None))
                tqk = p.n["st"]
                if tb == NTB - 1:
                    wst_guard[4] = tqk
                    wst_guard[5] = tqk
                sl = slice(tb * NT, (tb + 1) * NT)
                if part == "q":
                    p.wait("v", "st", tqk)
                    vq1 = vts(q_l1[:, j, sl], pb[bi][:, :], b1c[:, m:m + 1], AL.add,
                              inc="sv")
                    bank_set(bi, "sv", vq1)
                elif part == "k":
                    p.wait("s", "st", tqk)
                    sk = sact(kscr[:, sl], pb[bi][:, :], AF.Identity,
                              bias=b1c[:, m:m + 1])
                    bank_set(bi, "ss", sk)
                else:
                    p.wait("s", "st", tqk)
                    svv = sact(vscr[:, sl], pb[bi][:, :], AF.Identity,
                               bias=b1c[:, m:m + 1])
                    bank_set(bi, "ss", svv)
        p.wait("s", "sv", p.n["sv"])
        sq2 = sact(sp92[:, 0:T], q_l1[:, j, :], AF.Square, accum=acc[:, j:j + 1])
        sk2 = sact(sp92[:, 0:T], kscr[:, :], AF.Square, accum=acc[:, 16 + j:17 + j])
        p.wait("v", "ss", sk2)
        vkv = vtt(sp92[:, T:2 * T], kscr[:, :], vscr[:, :], AL.mult)
        vkr = p.run("v", lambda eng, j=j: eng.reduce_sum(
            acc[:, 32 + j:33 + j].rearrange("p (a b) -> p a b", b=1),
            sp92[:, T:2 * T].rearrange("p (a b) -> p a b", b=T), axis=AX), inc="sv")
        p.wait("s", "sv", vkr)

    p.wait("g", "sv", p.n["sv"])
    p.wait("g", "ss", p.n["ss"])
    arin = p.dma("g", "dg", dRin.ap(), acc[:, 0:48])
    p.wait("g", "dg", arin)
    p.run("g", lambda eng: eng.collective_compute(
        "AllReduce", AL.add, replica_groups=[list(range(NCORES))],
        ins=[dRin.ap().opt()], outs=[dRout.ap().opt()]), inc="cc")
    p.wait("g", "cc", p.n["cc"])
    grl = p.dma("g", "dg", acc[:, 48:96], dRout.ap())
    p.wait("v", "dg", grl)
    vnn = vtt(gst[:, 0:16], acc[:, 48:64], acc[:, 64:80], AL.mult, inc="sv")
    p.wait("s", "sv", vnn)
    snn = sact(gst[:, 16:32], gst[:, 0:16], AF.Sqrt)
    p.wait("v", "ss", snn)
    vrc2 = p.run("v", lambda eng: eng.reciprocal(gst[:, 32:48], gst[:, 16:32]),
                 inc="sv")
    p.wait("s", "sv", vrc2)
    scp = sact(gst[:, 48:64], gst[:, 32:48], AF.Copy)
    p.wait("v", "ss", scp)
    vcf = vtt(gst[:, 0:16], acc[:, 80:96], gst[:, 48:64], AL.mult, inc="sv")
    p.wait("s", "sv", vcf)
    scb = sact(cbf[:, 0:16], gst[:, 0:16], AF.Copy)
    sc2 = sact(cbf[:, 16:32], gst[:, 0:16], AF.Square)
    p.wait("t", "ss", sc2)
    p.wait("v", "ss", sc2)
    tap("qL1", q_l1[:, :, :].rearrange("p k t -> p (k t)"))
    tap("cfac", gst[:, 0:16])

    bank_wait("t", 4)
    bank_wait("t", 5)
    for tb in range(NTB):
        for k in range(KD):
            mm(pb[4][0:1, :], cbf[:, k:k + 1], q_l1[:, k, tb * NT:(tb + 1) * NT],
               start=(k == 0), stop=(k == KD - 1), inc=("st" if k == KD - 1 else None))
        p.wait("v", "st", p.n["st"])
        vsx = vcopy(rowA[0:1, tb * NT:(tb + 1) * NT], pb[4][0:1, :], inc="sv")
        p.wait("t", "sv", vsx)
    for tb in range(NTB):
        for k in range(KD):
            gvq = sq_guard[k % 2]
            p.wait("s", "st", gvq)
            sqq = sact(sp92[:, (k % 2) * NT:(k % 2 + 1) * NT],
                       q_l1[:, k, tb * NT:(tb + 1) * NT], AF.Square)
            p.wait("t", "ss", sqq)
            gv2 = mm(pb[5][0:1, :], cbf[:, 16 + k:17 + k],
                     sp92[:, (k % 2) * NT:(k % 2 + 1) * NT],
                     start=(k == 0), stop=(k == KD - 1), inc="st")
            sq_guard[k % 2] = gv2
        p.wait("v", "st", p.n["st"])
        vsq = vcopy(rowB[0:1, tb * NT:(tb + 1) * NT], pb[5][0:1, :], inc="sv")
        p.wait("t", "sv", vsq)
    vts(rowA[0:1, :], rowA[0:1, :], 1.0 / D, AL.mult)
    vts(rowB[0:1, :], rowB[0:1, :], 1.0 / D, AL.mult)
    vstt(rowC[0:1, :], rowA[0:1, :], -1.0, AL.mult, rowA[0:1, :], AL.mult)
    v1 = vtt(rowB[0:1, :], rowB[0:1, :], rowC[0:1, :], AL.add)
    p.wait("s", "sv", v1)
    s1 = sact(rowB[0:1, :], rowB[0:1, :], AF.Ln, bias=EPS)
    s2 = sact(rowB[0:1, :], rowB[0:1, :], AF.Exp, scale=-0.5)
    p.wait("v", "ss", s2)
    vz7 = vstt(rowC[0:1, :], rowA[0:1, :], -1.0, AL.mult, rowB[0:1, :], AL.mult,
               inc="sv")
    p.wait("t", "sv", vz7)
    rows = [rowB, rowC]
    for jj in range(2):
        bank_wait("t", jj)
        for tb in range(NTB):
            tm = mm(pb[jj][:, :], ones1[0:1, :], rows[jj][0:1, tb * NT:(tb + 1) * NT],
                    start=True, stop=True, inc="st")
            p.wait("v", "st", tm)
            vv = vcopy(bc[jj][:, tb * NT:(tb + 1) * NT], pb[jj][:, :], inc="sv")
            p.wait("t", "sv", vv)
        bank_set(jj, "sv", vv)
    for k in range(KD):
        vstt(scrA[:, k % 2, :], q_l1[:, k, :], gst[:, k:k + 1], AL.mult,
             bc[0][:, :], AL.mult)
        vtt(scrA[:, k % 2, :], scrA[:, k % 2, :], bc[1][:, :], AL.add)
        vup = vtt(resid[:, k, :], resid[:, k, :], scrA[:, k % 2, :], AL.add, inc="sv")
    p.wait("t", "sv", vup)
    p.wait("s", "sv", vup)
    tap("x3", resid[:, :, :].rearrange("p k t -> p (k t)"))

    ln_stats(lambda k: resid[:, k, :], F32R, onesc[:, 0:1])
    for k in range(KD):
        vtt(scrA[:, k % 2, :], resid[:, k, :], bc[0][:, :], AL.mult)
        vx6 = vtt(xn6[:, k, :], scrA[:, k % 2, :], bc[1][:, :], AL.add, inc="sv")
    p.wait("y", "sv", vx6)
    p.wait("t", "sv", vx6)
    tap("xn6", xn6[:, :, :].rearrange("p k t -> p (k t)"))
    spv = p.dma("y", "dy", dSpill.ap(), resid[:, :, :].rearrange("p k t -> p (k t)"))
    p.wait("v", "dy", spv)
    p.wait("g", "dg", quant_done)
    p.wait("g", "sv", p.n["sv"])
    p.wait("g", "ss", p.n["ss"])
    p.wait("g", "dy", spv)

    for m in range(MI):
        bi = m % 2
        gv = wst_guard.get(6 + bi, 0)
        p.wait("g", "st", gv)
        lg = p.dma("g", "dg", gwb[:, bi, :], dQ8g.ap()[m])
        lu = p.dma("g", "dg", uwb[:, bi, :], dQ8u.ap()[m])
        p.wait("t", "dg", lu)
        for tb in range(NTB):
            b0 = (m * NTB + tb) % 2
            b1_ = 2 + (m * NTB + tb) % 2
            bank_wait("t", b0)
            for k in range(KD):
                mm(pb[b0][:, :], gwb[:, bi, k * 128:(k + 1) * 128],
                   xn6[:, k, tb * NT:(tb + 1) * NT],
                   start=(k == 0), stop=(k == KD - 1), inc=("st" if k == KD - 1 else None))
            tgate = p.n["st"]
            bank_wait("t", b1_)
            for k in range(KD):
                mm(pb[b1_][:, :], uwb[:, bi, k * 128:(k + 1) * 128],
                   xn6[:, k, tb * NT:(tb + 1) * NT],
                   start=(k == 0), stop=(k == KD - 1), inc=("st" if k == KD - 1 else None))
            tup = p.n["st"]
            if tb == NTB - 1:
                wst_guard[6 + bi] = tup
            sl = slice(tb * NT, (tb + 1) * NT)
            p.wait("s", "st", tgate)
            sg_ = sact(sgs[:, :], pb[b0][:, :], AF.Silu, scale=gsc[:, m:m + 1])
            bank_set(b0, "ss", sg_)
            p.wait("v", "st", tup)
            p.wait("v", "ss", sg_)
            vfu = vstt(fuse[:, m, sl], pb[b1_][:, :], usc[:, m:m + 1], AL.mult,
                       sgs[:, :], AL.mult, inc="sv")
            bank_set(b1_, "sv", vfu)
            p.wait("s", "sv", vfu)
    tap("fuse", fuse[:, 0:4, :].rearrange("p a b -> p (a b)"))

    p.wait("g", "sv", p.n["sv"])
    p.wait("g", "ss", p.n["ss"])
    wst_guard[6] = wst_guard[7] = max(wst_guard[6], wst_guard[7])
    for mo in range(16):
        bi = mo % 2
        gv = wst_guard.get(6 + bi, 0) if mo < 2 else dwb_guard.get(bi, 0)
        p.wait("g", "st", gv)
        ld = p.dma("g", "dg", dwb[:, bi, :], dQ8d.ap()[mo])
        p.wait("t", "dg", ld)
        for tb in range(NTB):
            b0 = (mo * NTB + tb) % 4
            bank_wait("t", b0)
            for ki in range(MI):
                mm(pb[b0][:, :], dwb[:, bi, ki * 128:(ki + 1) * 128],
                   fuse[:, ki, tb * NT:(tb + 1) * NT],
                   start=(ki == 0), stop=(ki == MI - 1),
                   inc=("st" if ki == MI - 1 else None))
            tdn = p.n["st"]
            if tb == NTB - 1:
                dwb_guard[bi] = tdn
            p.wait("s", "st", tdn)
            sdn = sact(ao[:, mo, tb * NT:(tb + 1) * NT], pb[b0][:, :], AF.Copy,
                       scale=dsc[:, mo:mo + 1])
            bank_set(b0, "ss", sdn)
    p.wait("v", "ss", p.n["ss"])
    tap("ffo", ao[:, :, :].rearrange("p k t -> p (k t)"))
    last_dn = p.n["st"]
    p.wait("y", "st", last_dn)
    p.wait("s", "st", last_dn)
    rlv = p.dma("y", "dy", resid[:, :, :].rearrange("p k t -> p (k t)"), dSpill.ap())
    p.wait("v", "dy", rlv)
    p.wait("t", "dy", rlv)

    post_ln_update()

    p.wait("y", "sv", p.n["sv"])
    ov = p.dma("y", "dy", dout.ap(), resid[:, :, :].rearrange("p k t -> p (k t)"))
    p.wait("y", "dy", ov)
    p.wait("g", "dg", p.n["dg"])

    lp = nc.allow_low_precision(
        reason="f32r intermediates (fp22 reads) are well inside the "
               "validated tolerance; accumulations stay fp32 in PSUM")
    lp.__enter__()
    with nc.Block() as block:
        @block.tensor
        def _(eng):
            for f in p.ops["t"]:
                f(eng)

        @block.vector
        def _(eng):
            for f in p.ops["v"]:
                f(eng)

        @block.scalar
        def _(eng):
            for f in p.ops["s"]:
                f(eng)

        @block.gpsimd
        def _(eng):
            for f in p.ops["g"]:
                f(eng)

        @block.sync
        def _(eng):
            for f in p.ops["y"]:
                f(eng)

    lp.__exit__(None, None, None)
    stack.close()
    return nc


def _sub(dram, i):
    class W:
        def ap(self):
            return dram.ap()[i]
    return W()


def host_inputs(inputs):
    x = np.asarray(inputs["L0_attn_x"])[0]
    Wq = _retile(np.asarray(inputs["L0_attn_Wq"]))
    Wk = np.stack([_retile(np.asarray(inputs["L0_attn_Wk"])[i]) for i in range(NH)])
    Wv = np.stack([_retile(np.asarray(inputs["L0_attn_Wv"])[i]) for i in range(NH)])
    Wo = _retile(np.asarray(inputs["L0_attn_Wo"]))
    Wg0 = np.asarray(inputs["L0_attn_Wg"]).reshape(KD, 128, 3)
    Wg = np.zeros((128, KD * 4), dtype=np.float32)
    Wg.reshape(128, KD, 4)[:, :, 0:3] = Wg0.transpose(1, 0, 2)
    dl = np.zeros((1, 4), dtype=np.float32)
    dl[0, 0:3] = np.asarray(inputs["L0_attn_decay_logits"])
    osc_ = np.asarray(inputs["L0_attn_out_scale"]).reshape(1, 1)
    Wb = _retile(np.ascontiguousarray(np.asarray(inputs["L0_ffn_weight"]).T))
    A = np.asarray(inputs["L0_ffn_lora_A"])
    Ap = np.ascontiguousarray(_retile(np.ascontiguousarray(A.reshape(128, D).T))
                              .reshape(128, D))
    B = np.asarray(inputs["L0_ffn_lora_B"])
    Bp = np.ascontiguousarray(
        _retile(np.ascontiguousarray(B.transpose(0, 2, 1).reshape(128, D)))
        .transpose(1, 0, 2).reshape(128, 16 * 128))
    Rt = np.ascontiguousarray(np.asarray(inputs["L0_ffn_router_w"]).T
                              .reshape(KD, 128, 8).transpose(1, 0, 2)
                              .reshape(128, KD * 8))
    W1 = _retile(np.asarray(inputs["L1_attn_W"]))
    b1 = _cols(np.asarray(inputs["L1_attn_b"]))
    GW = _retile(np.asarray(inputs["L1_ffn_gate_w"]))
    UW = _retile(np.asarray(inputs["L1_ffn_up_w"]))
    DW = _retile(np.asarray(inputs["L1_ffn_down_w"]))
    GS = _cols(np.asarray(inputs["L1_ffn_gate_scale"]))
    US = _cols(np.asarray(inputs["L1_ffn_up_scale"]))
    DS = _cols(np.asarray(inputs["L1_ffn_down_scale"]))

    import ml_dtypes
    identb = np.eye(128, dtype=ml_dtypes.bfloat16)
    identr = np.eye(128, dtype=np.float32)
    s_i = np.arange(128)
    expnT = np.maximum(s_i[:, None] - s_i[None, :] - 1, 0).astype(np.float32)
    causT = (s_i[:, None] > s_i[None, :]).astype(np.float32)
    tpos = np.arange(T)
    e1 = (127 - (tpos % 128)).astype(np.float32).reshape(1, T)
    e2 = (1023 - tpos).astype(np.float32).reshape(1, T)
    iotac = np.stack([np.arange(128, dtype=np.float32),
                      np.full(128, 128.0, dtype=np.float32)], axis=1)
    onesc = np.ones((128, 2), dtype=np.float32)
    ones1 = np.ones((1, 128), dtype=np.float32)
    ones1b = np.ones((1, 128), dtype=ml_dtypes.bfloat16)
    pm = (np.arange(128)[None, :] // 16 == np.arange(8)[:, None]).astype(ml_dtypes.bfloat16)
    sh = np.zeros((3, NH * 128), dtype=ml_dtypes.bfloat16)
    for hh in range(NH):
        sh[hh, hh * 128:(hh + 1) * 128] = 1.0
    cn = np.tile(np.array([[EPS, -0.5]], dtype=np.float32), (128, 1))

    common = dict(Wq=Wq, Wk=Wk, Wv=Wv, Wo=Wo, Wg=Wg, decay_logits=dl, out_scale=osc_,
                  Wbase=Wb, loraA=Ap, loraB=Bp, router=Rt, W1=W1, b1=b1,
                  gatew=GW, upw=UW, downw=DW, gates=GS, ups=US, downs=DS,
                  identb=identb, identr=identr, expnT=expnT, causT=causT,
                  e1row=e1, e2row=e2, iotac=iotac, onesc=onesc, ones1=ones1,
                  ones1b=ones1b, pmat=pm, selh=sh, cnsts=cn)
    in_maps = []
    for r in range(NCORES):
        xs = np.ascontiguousarray(
            x[r * T:(r + 1) * T, :].T.reshape(KD, 128, T)
            .transpose(1, 0, 2).reshape(128, KD * T))
        bexpo = np.array([[max(1024.0 * (rp - r - 1), 0.0) for rp in range(8)]],
                         dtype=np.float32)
        bmask = np.array([[1.0 if rp > r else 0.0 for rp in range(8)]],
                         dtype=np.float32)
        m = dict(common)
        m.update(x=xs, bexpo=bexpo, bmask=bmask)
        in_maps.append(m)
    return in_maps


def kernel(**inputs):
    in_maps = host_inputs(inputs)
    nc = build_program()
    res = run_bass_kernel_spmd(nc, in_maps, core_ids=list(range(NCORES)))
    outs = []
    for r in range(NCORES):
        o = np.asarray(res.results[r]["out"])
        outs.append(o.reshape(128, KD, T).transpose(2, 1, 0).reshape(T, D))
    return np.concatenate(outs, axis=0)[None, :, :].astype(np.float32)


# version 11
# speedup vs baseline: 1.0613x; 1.0126x over previous
import numpy as np
import concourse.bass as bass
import concourse.mybir as mybir
from concourse.bass_utils import run_bass_kernel_spmd

F32 = mybir.dt.float32
F32R = mybir.dt.float32r
BF16 = mybir.dt.bfloat16
FP8 = mybir.dt.float8e4
AF = mybir.ActivationFunctionType
AL = mybir.AluOpType
AX = mybir.AxisListType.X

NCORES = 8
T = 1024
D = 2048
KD = 16
NT = 512
NTB = T // NT
DKH = 512
NH = 3
CH = 128
NCHK = T // CH
DI = 6144
MI = DI // 128
EPS = 1e-5
BIG = 1.0e30

KB = 1024


def _retile(w):
    din, dout = w.shape
    K, M = din // 128, dout // 128
    return np.ascontiguousarray(
        w.reshape(K, 128, M, 128).transpose(2, 1, 0, 3)).reshape(M, 128, K * 128)


def _cols(v):
    return np.ascontiguousarray(v.reshape(-1, 128).T)



class PB:
    ENGS = ("t", "v", "s", "g", "y")
    DQ = {"dg": ("dg", "dg2"), "dy": ("dy", "dy2")}

    def __init__(self):
        self.ops = {e: [] for e in self.ENGS}
        self.sem = {}
        self.n = {"dg": 0, "dy": 0, "st": 0, "sv": 0, "ss": 0, "sg": 0, "cc": 0}
        self.di = {"dg": 0, "dy": 0}

    def raw(self, e, fn):
        self.ops[e].append(fn)

    def _wait1(self, e, q, v):
        if v <= 0:
            return
        self.raw(e, lambda eng, q=q, v=v: eng.wait_ge(self.sem[q], v))

    def wait(self, e, q, v):
        if v <= 0:
            return
        if q in self.DQ:
            i = v // 16
            a, b = self.DQ[q]
            self._wait1(e, a, 16 * ((i + 1) // 2))
            self._wait1(e, b, 16 * (i // 2))
        else:
            self._wait1(e, q, v)

    def run(self, e, fn, inc=None):
        if inc is None:
            self.raw(e, fn)
            return self.n.get(None, 0)
        self.n[inc] += 1
        v = self.n[inc]
        self.raw(e, lambda eng, fn=fn, q=inc: fn(eng).then_inc(self.sem[q], 1))
        return v

    def dma(self, e, q, out, in_):
        self.di[q] += 1
        i = self.di[q]
        a, b = self.DQ[q]
        sem = a if i % 2 == 1 else b
        self._wait1(e, sem, 16 * ((i + 1) // 2 - 1))
        self.raw(e, lambda eng, out=out, in_=in_, s=sem:
                 eng.dma_start(out=out, in_=in_).then_inc(self.sem[s], 16))
        self.n[q] = 16 * i
        return 16 * i


def build_program(dbg=()):
    nc = bass.Bass("TRN2", target_bir_lowering=False)
    p = PB()

    dx = nc.dram_tensor("x", [128, KD * T], F32R, kind="ExternalInput").ap()
    dout = nc.dram_tensor("out", [128, KD * T], F32R, kind="ExternalOutput")

    dWq = nc.dram_tensor("Wq", [4, 128, D], F32R, kind="ExternalInput")
    dWk = nc.dram_tensor("Wk", [NH, 4, 128, D], F32R, kind="ExternalInput")
    dWv = nc.dram_tensor("Wv", [NH, 4, 128, D], F32R, kind="ExternalInput")
    dWo = nc.dram_tensor("Wo", [16, 128, DKH], F32, kind="ExternalInput")
    dWg = nc.dram_tensor("Wg", [128, KD * 4], F32R, kind="ExternalInput")
    dDL = nc.dram_tensor("decay_logits", [1, 4], F32, kind="ExternalInput").ap()
    dOS = nc.dram_tensor("out_scale", [1, 1], F32R, kind="ExternalInput").ap()
    dWb = nc.dram_tensor("Wbase", [16, 128, D], F32R, kind="ExternalInput")
    dA = nc.dram_tensor("loraA", [128, D], F32R, kind="ExternalInput")
    dB = nc.dram_tensor("loraB", [128, 16 * 128], F32R, kind="ExternalInput")
    dRt = nc.dram_tensor("router", [128, KD * 8], F32R, kind="ExternalInput")
    dW1 = nc.dram_tensor("W1", [MI, 128, D], F32, kind="ExternalInput")
    dB1 = nc.dram_tensor("b1", [128, MI], F32, kind="ExternalInput").ap()
    dGW = nc.dram_tensor("gatew", [MI, 128, D], F32, kind="ExternalInput")
    dUW = nc.dram_tensor("upw", [MI, 128, D], F32, kind="ExternalInput")
    dDW = nc.dram_tensor("downw", [16, 128, DI], F32, kind="ExternalInput")
    dGS = nc.dram_tensor("gates", [128, MI], F32, kind="ExternalInput").ap()
    dUS = nc.dram_tensor("ups", [128, MI], F32, kind="ExternalInput").ap()
    dDS = nc.dram_tensor("downs", [128, 16], F32, kind="ExternalInput").ap()

    dID = nc.dram_tensor("identb", [128, 128], BF16, kind="ExternalInput").ap()
    dIDr = nc.dram_tensor("identr", [128, 128], F32R, kind="ExternalInput").ap()
    dEXP = nc.dram_tensor("expnT", [128, 128], F32, kind="ExternalInput").ap()
    dCAU = nc.dram_tensor("causT", [128, 128], F32, kind="ExternalInput").ap()
    dE1 = nc.dram_tensor("e1row", [1, T], F32R, kind="ExternalInput").ap()
    dE2 = nc.dram_tensor("e2row", [1, T], F32R, kind="ExternalInput").ap()
    dIOT = nc.dram_tensor("iotac", [128, 2], F32, kind="ExternalInput").ap()
    dONE = nc.dram_tensor("onesc", [128, 2], F32R, kind="ExternalInput").ap()
    dONE1 = nc.dram_tensor("ones1", [1, 128], F32R, kind="ExternalInput").ap()
    dONE1b = nc.dram_tensor("ones1b", [1, 128], BF16, kind="ExternalInput").ap()
    dPM = nc.dram_tensor("pmat", [8, 128], BF16, kind="ExternalInput").ap()
    dSH = nc.dram_tensor("selh", [3, NH * 128], BF16, kind="ExternalInput").ap()
    dCN = nc.dram_tensor("cnsts", [128, 2], F32, kind="ExternalInput").ap()
    dBEX = nc.dram_tensor("bexpo", [1, 8], F32, kind="ExternalInput").ap()
    dBMK = nc.dram_tensor("bmask", [1, 8], F32, kind="ExternalInput").ap()

    dQ8g = nc.dram_tensor("q8g", [MI, 128, D], FP8)
    dQ8u = nc.dram_tensor("q8u", [MI, 128, D], FP8)
    dQ8d = nc.dram_tensor("q8d", [16, 128, DI], FP8)
    dSpill = nc.dram_tensor("spill", [128, KD * T], F32R)
    dBin = nc.dram_tensor("bstates_in", [NH * 128, 4 * DKH], BF16)
    dBout = nc.dram_tensor("bstates_out", [NCORES * NH * 128, 4 * DKH], BF16)
    dRin = nc.dram_tensor("rstats_in", [128, 48], F32)
    dRout = nc.dram_tensor("rstats_out", [128, 48], F32)

    dbg_t = {}
    _f32r_taps = {"ln1", "x1", "x2", "x3", "axt"}
    for name in dbg:
        shp = dict(ln1=[2, T], qT=[128, 4 * T], kT=[128, 4 * T], vTM=[128, 8 * DKH],
                   gate=[3, T], blend=[128, 4 * T], ao=[128, KD * T], x1=[128, KD * T],
                   x2=[128, KD * T], x3=[128, KD * T], wd=[8, T], axt=[128, T],
                   xn5=[128, KD * T], qL1=[128, KD * T], cfac=[128, 16],
                   xn6=[128, KD * T], fuse=[128, 4 * T], ffo=[128, KD * T],
                   Sst=[128, 4 * DKH], Ebn=[128, 4 * DKH],
                   qg=[128, 4 * T], qgd=[128, 4 * T], kdTM=[128, 8 * DKH],
                   blend0=[128, 4 * T], blendi=[128, 4 * T],
                   logd=[128, 4], djc=[128, 3], dCc=[128, 3],
                   gateb=[128, T])[name]
        dt_ = F32R if name in _f32r_taps else F32
        dbg_t[name] = nc.dram_tensor("dbg_" + name, shp, dt_, kind="ExternalOutput")

    import contextlib
    stack = contextlib.ExitStack()
    arena_guard = stack.enter_context(nc.sbuf_tensor("arena", [128, 192 * KB],
                                                     mybir.dt.uint8))

    smalls = {}
    def small(name, shape, dtype):
        h = stack.enter_context(nc.sbuf_tensor(name, shape, dtype))
        smalls[name] = h
        return h

    rowA = small("rowA", [1, T], F32R)
    rowB = small("rowB", [1, T], F32R)
    rowC = small("rowC", [1, T], F32R)
    csq = small("csq", [128, 64], F32)
    rcol = small("rcol", [128, 8], F32)
    zcol = small("zcol", [128, 8], F32)
    logd = small("logd", [128, 4], F32)
    djc = small("djc", [128, 4], F32)
    dCc = small("dCc", [128, 4], F32)
    wcs = small("wcs", [128, 8], F32)
    osc = small("oscol", [128, 1], F32)
    identb = small("identb_s", [128, 128], BF16)
    identr = small("identr_s", [128, 128], F32R)
    onesc = small("onesc_s", [128, 2], F32R)
    cnst = small("cnst_s", [128, 2], F32)
    nc.const_aps.register(F32, EPS, cnst[:, 0:1])
    nc.const_aps.register(F32, -0.5, cnst[:, 1:2])
    onescb = small("onescb", [128, 2], BF16)
    iotac = small("iotac_s", [128, 2], F32)
    ones1 = small("ones1_s", [1, 128], F32R)
    ones1b = small("ones1b_s", [1, 128], BF16)
    pmat = small("pmat_s", [8, 128], BF16)
    dlrow = small("dlrow", [1, 16], F32)
    dlogr = small("dlogr", [1, 4], F32R)
    oscs = small("oscs", [1, 2], F32R)
    bxrow = small("bxrow", [1, 16], F32)
    wgs = small("wgs", [128, KD * 4], F32R)
    rts = small("rts", [128, KD * 8], F32R)

    base_addr = nc.lookup_mloc(arena_guard).addr
    alias_n = [0]

    def alias(shape, dtype, off_kb):
        alias_n[0] += 1
        return nc.alloc_sbuf_tensor_at(
            "al%d" % alias_n[0], shape, dtype, offset=base_addr + int(off_kb * KB))

    lA = alias([128, KD * 128], F32R, 128)
    lB = alias([128, 16 * 128], F32R, 136)
    axt = alias([128, T], F32, 144)
    axw = alias([128, T], F32R, 148)
    expnT = alias([128, 128], F32, 96)
    causT = alias([128, 128], F32, 96.5)
    gtT = alias([3, T], BF16, 97)
    maskb = alias([128, NH * 128], BF16, 99)
    selh = alias([3, NH * 128], BF16, 99.75)
    wdt = alias([8, T], BF16, 84)
    glog = alias([128, 8 * 4], F32, 86)
    gmx = alias([128, 8 * 4], F32, 86.25)
    glog2 = alias([128, 8 * 4], F32, 87)
    rlog2 = alias([128, 8 * 8], F32, 87.25)
    rlog = alias([128, 8 * 8], F32, 86.5)
    rtop = alias([128, 8 * 4], F32, 86.75)
    acc = alias([128, 96], F32, 178)
    gst = alias([128, 64], F32, 178.5)
    cbf = alias([128, 32], BF16, 178.75)
    b1c = alias([128, MI], F32, 179)
    gsc = alias([128, MI], F32, 179.25)
    usc = alias([128, MI], F32, 179.5)
    dsc = alias([128, 16], F32, 179.75)

    resid = alias([128, KD, T], F32R, 0)
    fuse = alias([128, MI, T], BF16, 0)
    scrA = alias([128, 2, T], F32R, 64)
    vFM = alias([128, 4, T], BF16, 136)
    bc = [alias([128, T], F32R, 72 + 4 * i) for i in range(3)]
    Sbf = alias([128, 4, DKH], BF16, 84)
    bstmp = alias([128, 4 * DKH], BF16, 84)
    Ebn = alias([128, 4, DKH], BF16, 88)
    sp92 = alias([128, 2 * T], BF16, 92)
    ao = alias([128, KD, T], BF16, 96)
    qT = alias([128, 4, T], BF16, 128)
    qg = alias([128, 4, T], BF16, 136)
    qgd = alias([128, 4, T], BF16, 144)
    kT = alias([128, 4, T], BF16, 152)
    kdTM = alias([128, NCHK, DKH], BF16, 160)
    vTM = alias([128, NCHK, DKH], BF16, 168)
    blendT = alias([128, 4, T], BF16, 176)
    wst = alias([128, 2, 8 * 128], F32R, 184)
    xn5 = alias([128, KD, T], BF16, 128)
    kscr = alias([128, T], BF16, 160)
    vscr = alias([128, T], BF16, 164)
    wbl1 = alias([128, 2, 8 * 128], BF16, 184)
    xn6 = alias([128, KD, T], BF16, 128)
    gwb = alias([128, 2, D], BF16, 160)
    uwb = alias([128, 2, D], BF16, 168)
    sgs = alias([128, NT], F32, 176)
    dwb = alias([128, 2, MI * 128], BF16, 128)

    pb = [stack.enter_context(nc.psum_tensor("pb%d" % i, [128, NT], F32))
          for i in range(6)]
    ptb = stack.enter_context(nc.psum_tensor("ptb", [128, 4 * 128], BF16))
    ptr = stack.enter_context(nc.psum_tensor("ptr", [128, 128], F32R))

    sems = {}
    for q in ("dg", "dg2", "dy", "dy2", "st", "sv", "ss", "sg", "cc"):
        sems[q] = stack.enter_context(nc.semaphore(q))
    p.sem = {q: sems[q] for q in sems}

    bankuser = {}

    def bank_wait(e, i):
        u = bankuser.get(i)
        if u:
            p.wait(e, u[0], u[1])

    def bank_set(i, q, v):
        bankuser[i] = (q, v)

    def vcopy(out, in_, inc="sv", e="v"):
        return p.run(e, lambda eng, o=out, x=in_: eng.tensor_copy(o, x), inc=inc)

    def vtt(out, a, b, op, inc="sv", e="v"):
        return p.run(e, lambda eng, o=out, a=a, b=b, op=op:
                     eng.tensor_tensor(o, a, b, op), inc=inc)

    def vts(out, a, s1, op, s2=None, op2=AL.bypass, inc="sv", e="v"):
        if s2 is None:
            return p.run(e, lambda eng, o=out, a=a, s1=s1, op=op:
                         eng.tensor_single_scalar(o, a, s1, op), inc=inc)
        return p.run(e, lambda eng, o=out, a=a, s1=s1, s2=s2, op=op, op2=op2:
                     eng.tensor_scalar(o, a, s1, s2, op, op2), inc=inc)

    def vstt(out, in0, sc, op0, in1, op1, inc="sv", e="v"):
        return p.run(e, lambda eng, o=out, a=in0, s=sc, b=in1, op0=op0, op1=op1:
                     eng.scalar_tensor_tensor(o, a, s, b, op0, op1), inc=inc)

    def vbounce():
        v = p.n["sv"]
        p.wait("s", "sv", v)
        s0 = p.run("s", lambda eng: eng.nop(), inc="ss")
        p.wait("v", "ss", s0)

    def sbounce():
        s = p.n["ss"]
        p.wait("v", "ss", s)
        v0 = p.run("v", lambda eng: eng.nop(), inc="sv")
        p.wait("s", "sv", v0)

    def sact(out, in_, func, bias=0.0, scale=1.0, accum=None, inc="ss"):
        if func not in (AF.Copy, AF.Reciprocal):
            if isinstance(bias, float):
                bias = np.float32(bias)
            if isinstance(scale, float):
                scale = np.float32(scale)
        return p.run("s", lambda eng, o=out, x=in_, f=func, b=bias, sc=scale, ac=accum:
                     eng.activation(o, x, f, bias=b, scale=sc, accum_out=ac), inc=inc)

    def mm(out, lhsT, rhs, start, stop, inc=None):
        return p.run("t", lambda eng, o=out, l=lhsT, r=rhs, a=start, b=stop:
                     eng.matmul(o, l, r, start=a, stop=b), inc=inc)

    def ttrans(out, in_, ident, inc=None):
        return p.run("t", lambda eng, o=out, x=in_, i=ident:
                     eng.transpose(o, x, i), inc=inc)

    def ln_stats(src_tile, src_dtype, rsq_ones, sq_in_tile=None):
        p.wait("t", "sv", p.n["sv"])
        p.wait("t", "ss", p.n["ss"])
        p.wait("s", "sv", p.n["sv"])
        bank_wait("t", 4)
        bank_wait("t", 5)
        for tb in range(NTB):
            for k in range(KD):
                mm(pb[4][0:1, :], rsq_ones, src_tile(k)[:, tb * NT:(tb + 1) * NT],
                   start=(k == 0), stop=(k == KD - 1),
                   inc=("st" if k == KD - 1 else None))
            p.wait("v", "st", p.n["st"])
            vsx = vcopy(rowA[0:1, tb * NT:(tb + 1) * NT], pb[4][0:1, :], inc="sv")
            p.wait("t", "sv", vsx)
        for tb in range(NTB):
            for k in range(KD):
                sq_src = (sq_in_tile or src_tile)(k)[:, tb * NT:(tb + 1) * NT]
                if k >= 2 or tb > 0:
                    p.wait("s", "st", sq_guard[k % 2])
                ssq = sact(scrA[:, k % 2, :NT], sq_src, AF.Square)
                p.wait("t", "ss", ssq)
                gv = mm(pb[5][0:1, :], onesc[:, 0:1], scrA[:, k % 2, :NT],
                        start=(k == 0), stop=(k == KD - 1),
                        inc="st")
                sq_guard[k % 2] = gv
            p.wait("v", "st", p.n["st"])
            vq = vcopy(rowB[0:1, tb * NT:(tb + 1) * NT], pb[5][0:1, :], inc="sv")
            p.wait("t", "sv", vq)
        vts(rowA[0:1, :], rowA[0:1, :], 1.0 / D, AL.mult)
        vts(rowB[0:1, :], rowB[0:1, :], 1.0 / D, AL.mult)
        vstt(rowC[0:1, :], rowA[0:1, :], -1.0, AL.mult, rowA[0:1, :], AL.mult)
        v1 = vtt(rowB[0:1, :], rowB[0:1, :], rowC[0:1, :], AL.add)
        p.wait("s", "sv", v1)
        s1 = sact(rowB[0:1, :], rowB[0:1, :], AF.Ln, bias=EPS)
        s2 = sact(rowB[0:1, :], rowB[0:1, :], AF.Exp, scale=-0.5)
        p.wait("v", "ss", s2)
        v2 = vstt(rowC[0:1, :], rowA[0:1, :], -1.0, AL.mult, rowB[0:1, :], AL.mult,
                  inc="sv")
        p.wait("t", "sv", v2)
        rows = [rowB, rowC]
        for j, dst in enumerate(bc[:2]):
            bank_wait("t", j)
            for tb in range(NTB):
                tm = mm(pb[j][:, :], ones1[0:1, :],
                        rows[j][0:1, tb * NT:(tb + 1) * NT],
                        start=True, stop=True, inc="st")
                p.wait("v", "st", tm)
                vv = vcopy(dst[:, tb * NT:(tb + 1) * NT], pb[j][:, :], inc="sv")
                p.wait("t", "sv", vv)
            bank_set(j, "sv", vv)
        return

    sq_guard = {0: 0, 1: 0}

    def rz_cols():
        for j, (src, dst) in enumerate(((bc[0], rcol), (bc[1], zcol))):
            for tg in range(8):
                tv = ttrans(ptr[:, :], src[:, tg * 128:(tg + 1) * 128],
                            identr[:, :], inc="st")
                p.wait("v", "st", tv)
                vv = vcopy(dst[:, tg:tg + 1], ptr[:, 0:1], inc="sv")
                p.wait("t", "sv", vv)

    def proj_f32r(wdram, nM, cs_slot, dst, rhs_tile=None, e1="v"):
        rhs_tile = rhs_tile or (lambda k, tb: resid[:, k, tb * NT:(tb + 1) * NT])
        for m in range(nM):
            p.wait("y", "st", max(wst_guard.get(0, 0), wst_guard.get(1, 0)))
            wld = p.dma("y", "dy", wst[:, :, :].rearrange("p h f -> p (h f)"),
                        wdram.ap()[m])
            p.wait("t", "dy", wld)
            bank_wait("t", 2)
            for k in range(KD):
                mm(pb[2][:, 0:2], wst[:, 0 if k < 8 else 1,
                                      (k % 8) * 128:(k % 8 + 1) * 128],
                   onesc[:, 0:2], start=(k == 0), stop=(k == KD - 1),
                   inc=("st" if k == KD - 1 else None))
            tv = p.n["st"]
            p.wait("v", "st", tv)
            vv = vcopy(csq[:, cs_slot + m:cs_slot + m + 1], pb[2][:, 0:1], inc="sv")
            bank_set(2, "sv", vv)
            for tb in range(NTB):
                bi = m % 2
                bank_wait("t", bi)
                for k in range(KD):
                    mm(pb[bi][:, :], wst[:, 0 if k < 8 else 1, (k % 8) * 128:(k % 8 + 1) * 128],
                       rhs_tile(k, tb), start=(k == 0), stop=(k == KD - 1),
                       inc=("st" if k == KD - 1 else None))
                tvv = p.n["st"]
                if tb == NTB - 1:
                    wst_guard[0] = tvv
                    wst_guard[1] = tvv
                p.wait("v", "st", tvv)
                sl = slice(tb * NT, (tb + 1) * NT)
                vtt(scrA[:, 0, :NT], pb[bi][:, :], bc[0][:, sl], AL.mult)
                vv = vstt(dst(m)[:, sl], bc[1][:, sl],
                          csq[:, cs_slot + m:cs_slot + m + 1], AL.mult,
                          scrA[:, 0, :NT], AL.add, inc="sv")
                bank_set(bi, "sv", vv)

    wst_guard = {0: 0, 1: 0}
    dwb_guard = {0: 0, 1: 0}

    p.dma("y", "dy", resid[:, :, :].rearrange("p k t -> p (k t)"), dx)
    xload = p.n["dy"]
    for dst, src in ((identb, dID), (identr, dIDr), (expnT, dEXP), (causT, dCAU),
                     (onesc, dONE), (iotac, dIOT), (ones1, dONE1), (ones1b, dONE1b),
                     (pmat, dPM), (selh, dSH), (cnst, dCN),
                     (wgs, dWg.ap()), (rts, dRt.ap()),
                     ):
        p.dma("y", "dy", dst[:, :] if not hasattr(dst, "ndim") else dst, src)
    p.dma("g", "dg", dlrow[0:1, 0:4], dDL)
    p.dma("g", "dg", bxrow[0:1, 0:8], dBEX)
    p.dma("g", "dg", bxrow[0:1, 8:16], dBMK)
    p.dma("g", "dg", oscs[0:1, 0:1], dOS)
    p.dma("g", "dg", oscs[0:1, 1:2], dOS)
    consts_g = p.n["dg"]
    consts_y = p.n["dy"]

    p.wait("g", "dy", consts_y)
    for (src, dst) in ((dGW, dQ8g), (dUW, dQ8u), (dDW, dQ8d)):
        n0 = src.ap().shape[0]
        step = n0 // 4
        for i in range(0, n0, step):
            p.dma("g", "dg", dst.ap()[i:i + step], src.ap()[i:i + step])
    quant_done = p.n["dg"]

    p.wait("v", "dy", consts_y)
    p.wait("v", "dg", consts_g)
    vcopy(onescb[:, :], onesc[:, :])
    p.wait("t", "dg", consts_g)
    p.wait("t", "dy", consts_y)
    tosc = mm(pb[3][:, 0:2], ones1[0:1, :], oscs[0:1, 0:2],
              start=True, stop=True, inc="st")
    p.wait("v", "st", tosc)
    vosc = vcopy(osc[:, 0:1], pb[3][:, 0:1], inc="sv")

    p.wait("s", "dg", consts_g)
    p.wait("s", "dy", consts_y)
    s0 = sact(dlrow[0:1, 4:8], dlrow[0:1, 0:4], AF.Sigmoid)
    sbounce()
    s1 = sact(dlrow[0:1, 8:12], dlrow[0:1, 4:8], AF.Ln)
    sbounce()
    s1r = sact(dlogr[0:1, 0:4], dlrow[0:1, 8:12], AF.Identity)
    p.wait("t", "ss", s1r)
    p.wait("t", "sv", vosc)
    tld = mm(pb[3][:, 2:6], ones1[0:1, :], dlogr[0:1, 0:4],
             start=True, stop=True, inc="st")
    p.wait("v", "st", tld)
    vld = vcopy(logd[:, 0:4], pb[3][:, 2:6], inc="sv")
    bank_set(3, "sv", vld)
    p.wait("t", "sv", vld)
    p.wait("s", "sv", vld)
    vm_hist = {}
    for i in range(NH):
        sact(djc[:, i:i + 1], iotac[:, 0:1], AF.Exp, scale=logd[:, i:i + 1])
        sact(dCc[:, i:i + 1], iotac[:, 1:2], AF.Exp, scale=logd[:, i:i + 1])
        if i >= 2:
            p.wait("s", "sv", vm_hist[i % 2])
        sm = sact(scrA[:, i % 2, 0:128], expnT[:, :], AF.Exp,
                  scale=logd[:, i:i + 1])
        p.wait("v", "ss", sm)
        vm_hist[i % 2] = vtt(maskb[:, i * 128:(i + 1) * 128], scrA[:, i % 2, 0:128],
                             causT[:, :], AL.mult, inc="sv")

    p.wait("t", "dy", xload)

    tap_guard = []

    def tap(name, src_ap, rows=128):
        if name not in dbg_t:
            return
        p.wait("g", "sv", p.n["sv"])
        p.wait("g", "ss", p.n["ss"])
        p.wait("g", "st", p.n["st"])
        tv = p.dma("g", "dg", dbg_t[name].ap()[0:rows], src_ap)
        tap_guard.append(tv)
        for _e in ("t", "v", "s", "y", "g"):
            p.wait(_e, "dg", tv)

    tap("logd", logd[:, 0:4])
    tap("djc", djc[:, 0:3])
    tap("dCc", dCc[:, 0:3])

    ln_stats(lambda k: resid[:, k, :], F32R, onesc[:, 0:1])
    rz_cols()
    if "ln1" in dbg_t:
        p.wait("g", "sv", p.n["sv"])
        tap_guard.append(p.dma("g", "dg", dbg_t["ln1"].ap()[0:1], rowB[0:1, :]))
        tap_guard.append(p.dma("g", "dg", dbg_t["ln1"].ap()[1:2], rowC[0:1, :]))

    GP = 4
    bank_wait("t", 3)
    for tg in range(8):
        for k in range(KD):
            mm(pb[3][:, tg * GP:(tg + 1) * GP], resid[:, k, tg * 128:(tg + 1) * 128],
               wgs[:, k * GP:(k + 1) * GP], start=(k == 0), stop=(k == KD - 1),
               inc=("st" if k == KD - 1 else None))
    tgl = p.n["st"]
    p.wait("s", "st", tgl)
    p.wait("s", "sv", p.n["sv"])
    sgl = 0
    for tg in range(8):
        sgl = sact(glog[:, tg * GP:(tg + 1) * GP], pb[3][:, tg * GP:(tg + 1) * GP],
                   AF.Copy, scale=rcol[:, tg:tg + 1])
    p.wait("v", "ss", sgl)
    bank_wait("t", 4)
    for k in range(KD):
        mm(pb[4][0:1, 0:GP], onesc[:, 0:1], wgs[:, k * GP:(k + 1) * GP],
           start=(k == 0), stop=(k == KD - 1), inc=("st" if k == KD - 1 else None))
    tcg = p.n["st"]
    p.wait("v", "st", tcg)
    vcg = vcopy(rowA[0:1, 0:GP], pb[4][0:1, 0:GP], inc="sv")
    p.wait("t", "sv", vcg)
    tcb = mm(pb[4][:, 0:GP], ones1[0:1, :], rowA[0:1, 0:GP],
             start=True, stop=True, inc="st")
    p.wait("s", "st", tcb)
    scs = sact(scrA[:, 0, 0:GP], pb[4][:, 0:GP], AF.Copy)
    p.wait("v", "ss", scs)
    for tg in range(8):
        vstt(glog2[:, tg * GP:(tg + 1) * GP], scrA[:, 0, 0:GP], zcol[:, tg:tg + 1],
             AL.mult, glog[:, tg * GP:(tg + 1) * GP], AL.add)
    vg1 = p.n["sv"]
    bank_set(3, "sv", vg1)
    bank_set(4, "ss", scs)
    p.wait("s", "sv", vg1)
    sex = 0
    for tg in range(8):
        sex = sact(glog[:, tg * GP:(tg + 1) * GP], glog2[:, tg * GP:(tg + 1) * GP],
                   AF.Exp)
    p.wait("v", "ss", sex)
    vsum = p.run("v", lambda eng: eng.reduce_sum(
        gmx[:, 0:8].rearrange("p (a b) -> p a b", b=1),
        glog[:, :].rearrange("p (a b) -> p a b", b=GP), axis=AX), inc="sv")
    p.wait("s", "sv", vsum)
    sm1 = sact(gmx[:, 8:16], gmx[:, 0:8], AF.Identity, bias=-1.0)
    p.wait("v", "ss", sm1)
    p.run("v", lambda eng: eng.reciprocal(gmx[:, 16:24], gmx[:, 8:16]), inc="sv")
    vrc = p.n["sv"]
    p.wait("s", "sv", vrc)
    spk = 0
    for tg in range(8):
        spk = sact(sp92[:, tg * 3:(tg + 1) * 3], glog[:, tg * GP:tg * GP + 3],
                   AF.Copy, scale=gmx[:, 16 + tg:17 + tg])
    p.wait("t", "ss", spk)
    for tg in range(8):
        tv = ttrans(ptb[0:3, 0:128], sp92[:, tg * 3:(tg + 1) * 3], identb[:, :],
                    inc="st")
        p.wait("v", "st", tv)
        vv = vcopy(gtT[0:3, tg * 128:(tg + 1) * 128], ptb[0:3, 0:128], inc="sv")
        p.wait("t", "sv", vv)
    tap("gate", gtT[0:3, :], rows=3)

    proj_f32r(dWq, 4, 0, lambda m: qT[:, m, :])
    tap("qT", qT[:, :, :].rearrange("p m t -> p (m t)"))

    bstore = []
    for hI in range(NH):
        proj_f32r(_sub(dWk, hI), 4, 8, lambda m: kT[:, m, :])
        proj_f32r(_sub(dWv, hI), 4, 16, lambda m: vFM[:, m, :])
        for tg in range(NCHK):
            for dk4 in range(4):
                tv = ttrans(ptb[:, dk4 * 128:(dk4 + 1) * 128],
                            kT[:, dk4, tg * 128:(tg + 1) * 128], identb[:, :],
                            inc=("st" if dk4 == 3 else None))
            p.wait("s", "st", tv)
            sv_ = sact(kdTM[:, tg, :], ptb[:, 0:512], AF.Copy, scale=djc[:, hI:hI + 1])
            p.wait("t", "ss", sv_)
            for dk4 in range(4):
                tv = ttrans(ptb[:, dk4 * 128:(dk4 + 1) * 128],
                            vFM[:, dk4, tg * 128:(tg + 1) * 128], identb[:, :],
                            inc=("st" if dk4 == 3 else None))
            p.wait("s", "st", tv)
            sv_ = sact(vTM[:, tg, :], ptb[:, 0:512], AF.Copy)
            p.wait("t", "ss", sv_)
        vfm_done = tv
        if hI == 0:
            tap("kT", kT[:, :, :].rearrange("p m t -> p (m t)"))
            tap("vTM", vTM[:, :, :].rearrange("p a b -> p (a b)"))
            tap("kdTM", kdTM[:, :, :].rearrange("p a b -> p (a b)"))
        p.wait("y", "ss", p.n["ss"])
        ld = p.dma("y", "dy", rowA[0:1, :], dE1)
        p.wait("s", "dy", ld)
        se1 = sact(rowC[0:1, :], rowA[0:1, :], AF.Exp, scale=dlrow[0:1, 8 + hI:9 + hI])
        gb = 0
        bank_wait("t", 0)
        for tb in range(NTB):
            tvb = mm(pb[0][:, :], selh[:, hI * 128:(hI + 1) * 128],
                     gtT[0:3, tb * NT:(tb + 1) * NT],
                     start=True, stop=True, inc="st")
            p.wait("v", "st", tvb)
            gb = vcopy(bc[2][:, tb * NT:(tb + 1) * NT], pb[0][:, :], inc="sv")
            p.wait("t", "sv", gb)
        p.wait("v", "st", vfm_done)
        if hI == 0 and "gateb" in dbg_t:
            tap("gateb", bc[2][:, :])
            p.wait("v", "dg", p.n["dg"])
        for k4 in range(4):
            vtt(qg[:, k4, :], qT[:, k4, :], bc[2][:, :], AL.mult)
        p.wait("t", "ss", se1)
        for tb in range(NTB):
            tvb = mm(pb[0][:, :], ones1[0:1, :], rowC[0:1, tb * NT:(tb + 1) * NT],
                     start=True, stop=True, inc="st")
            p.wait("v", "st", tvb)
            vcb = vcopy(bc[2][:, tb * NT:(tb + 1) * NT], pb[0][:, :], inc="sv")
            p.wait("t", "sv", vcb)
        bank_set(0, "sv", vcb)
        for k4 in range(4):
            vqg = vtt(qgd[:, k4, :], qg[:, k4, :], bc[2][:, :], AL.mult, inc="sv")
        p.wait("t", "sv", vqg)
        if hI == 0:
            tap("qg", qg[:, :, :].rearrange("p m t -> p (m t)"))
            tap("qgd", qgd[:, :, :].rearrange("p m t -> p (m t)"))

        for ci, c in enumerate(reversed(range(NCHK))):
            csl = slice(c * CH, (c + 1) * CH)
            bank_wait("t", 0)
            for dk4 in range(4):
                mm(pb[0][:, 0:128], kT[:, dk4, csl], qg[:, dk4, csl],
                   start=(dk4 == 0), stop=(dk4 == 3), inc=("st" if dk4 == 3 else None))
            tsc = p.n["st"]
            p.wait("v", "st", tsc)
            vsc = vtt(sp92[:, 0:128], pb[0][:, 0:128], maskb[:, hI * 128:(hI + 1) * 128],
                      AL.mult, inc="sv")
            bank_set(0, "sv", vsc)
            p.wait("t", "sv", vsc)
            bank_wait("t", 1)
            for dv4 in range(4):
                mm(pb[1][:, dv4 * 128:(dv4 + 1) * 128],
                   vTM[:, c, dv4 * 128:(dv4 + 1) * 128], sp92[:, 0:128],
                   start=True, stop=(ci == 0),
                   inc=("st" if (ci == 0 and dv4 == 3) else None))
                if ci > 0:
                    for dk4 in range(4):
                        mm(pb[1][:, dv4 * 128:(dv4 + 1) * 128],
                           Sbf[:, dk4, dv4 * 128:(dv4 + 1) * 128], qgd[:, dk4, csl],
                           start=False, stop=(dk4 == 3),
                           inc=("st" if (dk4 == 3 and dv4 == 3) else None))
            tre = p.n["st"]
            p.wait("v", "st", tre)
            if hI == 0:
                for dv4 in range(4):
                    vr = vcopy(blendT[:, dv4, csl], pb[1][:, dv4 * 128:(dv4 + 1) * 128],
                               inc=("sv" if dv4 == 3 else None))
            else:
                for dv4 in range(4):
                    vr = vtt(blendT[:, dv4, csl], blendT[:, dv4, csl],
                             pb[1][:, dv4 * 128:(dv4 + 1) * 128], AL.add,
                             inc=("sv" if dv4 == 3 else None))
            bank_set(1, "sv", vr)
            for dk4 in range(4):
                bank_wait("t", 2 + dk4)
                mm(pb[2 + dk4][:, :], kdTM[:, c, dk4 * 128:(dk4 + 1) * 128],
                   vTM[:, c, :], start=True, stop=True,
                   inc="st")
                tsu = p.n["st"]
                p.wait("v", "st", tsu)
                if ci == 0:
                    vu = vcopy(Sbf[:, dk4, :], pb[2 + dk4][:, :], inc="sv")
                else:
                    vu = vstt(Sbf[:, dk4, :], Sbf[:, dk4, :], dCc[:, hI:hI + 1],
                              AL.mult, pb[2 + dk4][:, :], AL.add, inc="sv")
                bank_set(2 + dk4, "sv", vu)
                p.wait("t", "sv", vu)
        p.wait("g", "sv", p.n["sv"])
        bstore.append(p.dma("g", "dg", dBin.ap()[hI * 128:(hI + 1) * 128, :],
                            Sbf[:, :, :].rearrange("p a b -> p (a b)")))
        if hI == 0:
            tap("Sst", Sbf[:, :, :].rearrange("p a b -> p (a b)"))
            tap("blend0", blendT[:, :, :].rearrange("p a b -> p (a b)"))

    p.wait("g", "dg", bstore[-1])
    p.run("g", lambda eng: eng.collective_compute(
        "AllGather", AL.bypass, replica_groups=[list(range(NCORES))],
        ins=[dBin.ap().opt()], outs=[dBout.ap().opt()]), inc="cc")
    p.wait("g", "cc", p.n["cc"])
    ebn_guard = 0
    for hI in range(NH):
        p.wait("s", "sv", p.n["sv"])
        swr = sact(rowA[0:1, 0:8], bxrow[0:1, 0:8], AF.Exp,
                   scale=dlrow[0:1, 8 + hI:9 + hI])
        p.wait("v", "ss", swr)
        vw = vtt(rowA[0:1, 8:16], rowA[0:1, 0:8], bxrow[0:1, 8:16], AL.mult,
                 inc="sv")
        p.wait("t", "sv", vw)
        bank_wait("t", 0)
        twb = mm(pb[0][:, 0:8], ones1[0:1, :], rowA[0:1, 8:16], start=True, stop=True,
                 inc="st")
        p.wait("v", "st", twb)
        vwc = vcopy(wcs[:, 0:8], pb[0][:, 0:8], inc="sv")
        bank_set(0, "sv", vwc)
        for r in range(NCORES):
            lb = p.dma("g", "dg", bstmp[:, :],
                       dBout.ap()[(r * NH + hI) * 128:(r * NH + hI + 1) * 128, :])
            p.wait("v", "dg", lb)
            p.wait("v", "sv", vwc)
            if r == 0:
                p.wait("v", "st", ebn_guard)
                vE = vts(Ebn[:, :, :].rearrange("p a b -> p (a b)"), bstmp[:, :],
                         wcs[:, 0:1], AL.mult, inc="sv")
            else:
                vE = vstt(Ebn[:, :, :].rearrange("p a b -> p (a b)"), bstmp[:, :],
                          wcs[:, r:r + 1], AL.mult,
                          Ebn[:, :, :].rearrange("p a b -> p (a b)"), AL.add, inc="sv")
            p.wait("g", "sv", vE)
        if hI == 0:
            tap("Ebn", Ebn[:, :, :].rearrange("p a b -> p (a b)"))
        p.wait("y", "ss", p.n["ss"])
        ld = p.dma("y", "dy", rowB[0:1, :], dE2)
        p.wait("s", "dy", ld)
        se2 = sact(rowC[0:1, :], rowB[0:1, :], AF.Exp, scale=dlrow[0:1, 8 + hI:9 + hI])
        p.wait("t", "ss", se2)
        bank_wait("t", 1)
        for tb in range(NTB):
            tvb = mm(pb[1][:, :], ones1[0:1, :], rowC[0:1, tb * NT:(tb + 1) * NT],
                     start=True, stop=True, inc="st")
            p.wait("v", "st", tvb)
            ve2 = vcopy(bc[2][:, tb * NT:(tb + 1) * NT], pb[1][:, :], inc="sv")
            p.wait("t", "sv", ve2)
        for tb in range(NTB):
            tvb = mm(pb[1][:, :], selh[:, hI * 128:(hI + 1) * 128],
                     gtT[0:3, tb * NT:(tb + 1) * NT],
                     start=True, stop=True, inc="st")
            p.wait("v", "st", tvb)
            vgb = vstt(bc[2][:, tb * NT:(tb + 1) * NT], bc[2][:, tb * NT:(tb + 1) * NT],
                       1.0, AL.mult, pb[1][:, :], AL.mult, inc="sv")
            p.wait("t", "sv", vgb)
        bank_set(1, "sv", vgb)
        p.wait("v", "st", ebn_guard)
        for k4 in range(4):
            vq = vtt(qgd[:, k4, :], qT[:, k4, :], bc[2][:, :], AL.mult, inc="sv")
        p.wait("t", "sv", vq)
        for dv4 in range(4):
            for tb in range(NTB):
                bi = (dv4 * NTB + tb) % 2
                bank_wait("t", bi)
                for dk4 in range(4):
                    mm(pb[bi][:, :], Ebn[:, dk4, dv4 * 128:(dv4 + 1) * 128],
                       qgd[:, dk4, tb * NT:(tb + 1) * NT],
                       start=(dk4 == 0), stop=(dk4 == 3),
                       inc=("st" if dk4 == 3 else None))
                tb_ = p.n["st"]
                p.wait("v", "st", tb_)
                vb = vtt(blendT[:, dv4, tb * NT:(tb + 1) * NT],
                         blendT[:, dv4, tb * NT:(tb + 1) * NT], pb[bi][:, :],
                         AL.add, inc="sv")
                bank_set(bi, "sv", vb)
        ebn_guard = p.n["st"]
    tap("blend", blendT[:, :, :].rearrange("p a b -> p (a b)"))

    p.wait("y", "sv", p.n["sv"])
    p.wait("y", "st", p.n["st"])
    p.dma("y", "dy", lA[:, :], dA.ap())
    lab_load = p.dma("y", "dy", lB[:, :], dB.ap())

    for k4 in range(4):
        vos = vts(blendT[:, k4, :], blendT[:, k4, :], osc[:, 0:1], AL.mult,
                  inc=("sv" if k4 == 3 else None))
    p.wait("t", "sv", vos)

    p.wait("g", "st", max(wst_guard[0], wst_guard[1]))
    for m in range(16):
        gv = wst_guard.get(4 + m % 2, 0)
        p.wait("g", "st", gv)
        wl = p.dma("g", "dg", wbl1[:, m % 2, 0:512], dWo.ap()[m])
        p.wait("t", "dg", wl)
        for tb in range(NTB):
            bi = m % 2
            bank_wait("t", bi)
            for k4 in range(4):
                mm(pb[bi][:, :], wbl1[:, m % 2, k4 * 128:(k4 + 1) * 128],
                   blendT[:, k4, tb * NT:(tb + 1) * NT],
                   start=(k4 == 0), stop=(k4 == 3), inc=("st" if k4 == 3 else None))
            tww = p.n["st"]
            if tb == NTB - 1:
                wst_guard[4 + m % 2] = tww
            p.wait("v", "st", tww)
            vw = vcopy(ao[:, m, tb * NT:(tb + 1) * NT], pb[bi][:, :], inc="sv")
            bank_set(bi, "sv", vw)
    wst_guard[0] = wst_guard[1] = max(wst_guard[4], wst_guard[5])
    tap("ao", ao[:, :, :].rearrange("p k t -> p (k t)"))

    def post_ln_update():
        ln_stats(lambda k: ao[:, k, :], BF16, onescb[:, 0:1],
                 sq_in_tile=lambda k: ao[:, k, :])
        for k in range(KD):
            v1 = vtt(scrA[:, k % 2, :], ao[:, k, :], bc[0][:, :], AL.mult)
            v2 = vtt(scrA[:, k % 2, :], scrA[:, k % 2, :], bc[1][:, :], AL.add)
            v3 = vtt(resid[:, k, :], resid[:, k, :], scrA[:, k % 2, :], AL.add,
                     inc="sv")
        p.wait("t", "sv", p.n["sv"])
        p.wait("s", "sv", p.n["sv"])

    post_ln_update()
    tap("x1", resid[:, :, :].rearrange("p k t -> p (k t)"))

    ln_stats(lambda k: resid[:, k, :], F32R, onesc[:, 0:1])
    rz_cols()
    vrr = p.run("v", lambda eng: eng.reciprocal(rowA[0:1, :], rowB[0:1, :]), inc="sv")
    p.wait("t", "sv", vrr)
    bank_wait("t", 0)
    for tb in range(NTB):
        tvb = mm(pb[0][:, :], ones1[0:1, :], rowA[0:1, tb * NT:(tb + 1) * NT],
                 start=True, stop=True, inc="st")
        p.wait("v", "st", tvb)
        vv = vcopy(bc[2][:, tb * NT:(tb + 1) * NT], pb[0][:, :], inc="sv")
        p.wait("t", "sv", vv)
        bank_set(0, "sv", vv)

    bank_wait("t", 3)
    for tg in range(8):
        for k in range(KD):
            mm(pb[3][:, tg * 8:(tg + 1) * 8], resid[:, k, tg * 128:(tg + 1) * 128],
               rts[:, k * 8:(k + 1) * 8], start=(k == 0), stop=(k == KD - 1),
               inc=("st" if k == KD - 1 else None))
    trl = p.n["st"]
    bank_wait("t", 4)
    for k in range(KD):
        mm(pb[4][0:1, 0:8], onesc[:, 0:1], rts[:, k * 8:(k + 1) * 8],
           start=(k == 0), stop=(k == KD - 1), inc=("st" if k == KD - 1 else None))
    tcr = p.n["st"]
    p.wait("v", "st", tcr)
    vcr = vcopy(rowA[0:1, 0:8], pb[4][0:1, 0:8], inc="sv")
    p.wait("t", "sv", vcr)
    tcb2 = mm(pb[4][:, 0:8], ones1[0:1, :], rowA[0:1, 0:8], start=True, stop=True,
              inc="st")
    p.wait("v", "st", tcb2)
    vcs = vcopy(scrA[:, 0, 0:8], pb[4][:, 0:8], inc="sv")
    bank_set(4, "sv", vcs)
    p.wait("s", "st", trl)
    p.wait("s", "sv", vcs)
    srl = 0
    for tg in range(8):
        srl = sact(rlog[:, tg * 8:(tg + 1) * 8], pb[3][:, tg * 8:(tg + 1) * 8],
                   AF.Copy, scale=rcol[:, tg:tg + 1])
    p.wait("v", "ss", srl)
    for tg in range(8):
        vstt(rlog2[:, tg * 8:(tg + 1) * 8], scrA[:, 0, 0:8], zcol[:, tg:tg + 1],
             AL.mult, rlog[:, tg * 8:(tg + 1) * 8], AL.add)
    vrl = p.n["sv"]
    bank_set(3, "sv", vrl)
    vbounce()
    p.run("v", lambda eng: eng.reduce_max(
        rtop[:, 0:8].rearrange("p (a b) -> p a b", b=1),
        rlog2[:, :].rearrange("p (a b) -> p a b", b=8), axis=AX))
    vbounce()
    for tg in range(8):
        vts(sp92[:, tg * 8:(tg + 1) * 8],
            rlog2[:, tg * 8:(tg + 1) * 8], rtop[:, tg:tg + 1], AL.is_equal)
    vbounce()
    for tg in range(8):
        vstt(scrA[:, 0, tg * 8:(tg + 1) * 8], sp92[:, tg * 8:(tg + 1) * 8], -BIG,
             AL.mult, rlog2[:, tg * 8:(tg + 1) * 8], AL.add)
    vbounce()
    p.run("v", lambda eng: eng.reduce_max(
        rtop[:, 8:16].rearrange("p (a b) -> p a b", b=1),
        scrA[:, 0, 0:64].rearrange("p (a b) -> p a b", b=8), axis=AX))
    vbounce()
    v12 = vtt(rtop[:, 16:24], rtop[:, 0:8], rtop[:, 8:16], AL.subtract, inc="sv")
    p.wait("s", "sv", v12)
    sw1 = sact(rtop[:, 24:32], rtop[:, 16:24], AF.Sigmoid)
    p.wait("v", "ss", sw1)
    vts(gmx[:, 0:8], rtop[:, 24:32], 2.0, AL.mult)
    vts(gmx[:, 8:16], rtop[:, 24:32], -2.0, AL.mult, 2.0, AL.add)
    for tg in range(8):
        vts(scrA[:, 1, tg * 8:(tg + 1) * 8], scrA[:, 0, tg * 8:(tg + 1) * 8],
            rtop[:, 8 + tg:9 + tg], AL.is_equal)
    vbounce()
    for tg in range(8):
        vts(rlog[:, tg * 8:(tg + 1) * 8], sp92[:, tg * 8:(tg + 1) * 8],
            gmx[:, tg:tg + 1], AL.mult)
    vbounce()
    for tg in range(8):
        sl8 = slice(tg * 8, (tg + 1) * 8)
        vwd = vstt(sp92[:, sl8], scrA[:, 1, sl8], gmx[:, 8 + tg:9 + tg],
                   AL.mult, rlog[:, sl8], AL.add, inc="sv")
    p.wait("t", "sv", p.n["sv"])
    for tg in range(8):
        tv = ttrans(ptb[0:8, 0:128], sp92[:, tg * 8:(tg + 1) * 8], identb[:, :],
                    inc="st")
        p.wait("v", "st", tv)
        vv = vcopy(wdt[0:8, tg * 128:(tg + 1) * 128], ptb[0:8, 0:128], inc="sv")
        p.wait("t", "sv", vv)
    tap("wd", wdt[0:8, :], rows=8)

    p.wait("t", "dy", lab_load)
    for tb in range(NTB):
        bi = tb % 2
        bank_wait("t", bi)
        for k in range(KD):
            mm(pb[bi][:, :], lA[:, k * 128:(k + 1) * 128],
               resid[:, k, tb * NT:(tb + 1) * NT],
               start=(k == 0), stop=(k == KD - 1), inc=("st" if k == KD - 1 else None))
        tax = p.n["st"]
        if tb == 0:
            bank_wait("t", 2)
            for k in range(KD):
                mm(pb[2][:, 0:2], lA[:, k * 128:(k + 1) * 128], onesc[:, 0:2],
                   start=(k == 0), stop=(k == KD - 1),
                   inc=("st" if k == KD - 1 else None))
            tca = p.n["st"]
            p.wait("v", "st", tca)
            vca = vcopy(csq[:, 32:33], pb[2][:, 0:1], inc="sv")
            bank_set(2, "sv", vca)
        p.wait("v", "st", tax)
        sl = slice(tb * NT, (tb + 1) * NT)
        vtt(scrA[:, 0, :NT], pb[bi][:, :], bc[0][:, sl], AL.mult)
        vax = vstt(axt[:, sl], bc[1][:, sl], csq[:, 32:33], AL.mult,
                   scrA[:, 0, :NT], AL.add, inc="sv")
        bank_set(bi, "sv", vax)
    tap("axt", axt[:, :])
    for tb in range(NTB):
        bank_wait("t", 0)
        twd = mm(pb[0][:, :], pmat[:, :], wdt[0:8, tb * NT:(tb + 1) * NT],
                 start=True, stop=True, inc="st")
        p.wait("v", "st", twd)
        sl = slice(tb * NT, (tb + 1) * NT)
        vtt(scrA[:, 0, :NT], pb[0][:, :], bc[2][:, sl], AL.mult)
        vwx = vtt(axw[:, sl], axt[:, sl], scrA[:, 0, :NT], AL.mult, inc="sv")
        bank_set(0, "sv", vwx)
    p.wait("t", "sv", p.n["sv"])

    for m in range(16):
        for h in range(2):
            bi = h
            gv = wst_guard.get(bi, 0)
            p.wait("y", "st", gv)
            p.dma("y", "dy", wst[:, bi, :], dWb.ap()[m, :, h * 8 * 128:(h + 1) * 8 * 128])
        wl = p.n["dy"]
        p.wait("t", "dy", wl)
        bank_wait("t", 2)
        for k in range(KD):
            mm(pb[2][:, 0:2], wst[:, 0 if k < 8 else 1, (k % 8) * 128:(k % 8 + 1) * 128],
               onesc[:, 0:2], start=(k == 0), stop=(k == KD - 1),
               inc=("st" if k == KD - 1 else None))
        tcs = p.n["st"]
        p.wait("v", "st", tcs)
        vcsb = vcopy(csq[:, 40 + (m % 8):41 + (m % 8)], pb[2][:, 0:1], inc="sv")
        bank_set(2, "sv", vcsb)
        for tb in range(NTB):
            bi = m % 2
            bank_wait("t", bi)
            for k in range(KD):
                mm(pb[bi][:, :], wst[:, 0 if k < 8 else 1, (k % 8) * 128:(k % 8 + 1) * 128],
                   resid[:, k, tb * NT:(tb + 1) * NT],
                   start=(k == 0), stop=False)
            mm(pb[bi][:, :], lB[:, m * 128:(m + 1) * 128],
               axw[:, tb * NT:(tb + 1) * NT], start=False, stop=True, inc="st")
            tbm = p.n["st"]
            if tb == NTB - 1:
                wst_guard[0] = tbm
                wst_guard[1] = tbm
            p.wait("v", "st", tbm)
            p.wait("v", "sv", vcsb)
            sl = slice(tb * NT, (tb + 1) * NT)
            vtt(scrA[:, 0, :NT], pb[bi][:, :], bc[0][:, sl], AL.mult)
            vbs = vstt(ao[:, m, sl], bc[1][:, sl], csq[:, 40 + (m % 8):41 + (m % 8)],
                       AL.mult, scrA[:, 0, :NT], AL.add, inc="sv")
            bank_set(bi, "sv", vbs)

    post_ln_update()
    tap("x2", resid[:, :, :].rearrange("p k t -> p (k t)"))

    ln_stats(lambda k: resid[:, k, :], F32R, onesc[:, 0:1])
    for k in range(KD):
        vtt(scrA[:, k % 2, :], resid[:, k, :], bc[0][:, :], AL.mult)
        vx5 = vtt(xn5[:, k, :], scrA[:, k % 2, :], bc[1][:, :], AL.add, inc="sv")
    p.wait("t", "sv", vx5)
    tap("xn5", xn5[:, :, :].rearrange("p k t -> p (k t)"))

    q_l1 = ao
    wst_guard[4] = wst_guard[5] = max(wst_guard[0], wst_guard[1])
    p.wait("y", "st", wst_guard[0])
    for dst, src in ((b1c, dB1), (gsc, dGS), (usc, dUS), (dsc, dDS)):
        scl_load = p.dma("y", "dy", dst[:, :], src)
    p.wait("v", "dy", scl_load)
    p.wait("s", "dy", scl_load)
    for j in range(16):
        for part, mofs in (("q", 0), ("k", 16), ("v", 32)):
            m = mofs + j
            p.wait("g", "st", max(wst_guard.get(4, 0), wst_guard.get(5, 0)))
            wl = p.dma("g", "dg", wbl1[:, :, :].rearrange("p h f -> p (h f)"),
                       dW1.ap()[m])
            p.wait("t", "dg", wl)
            for tb in range(NTB):
                bi = (m * NTB + tb) % 2
                bank_wait("t", bi)
                for k in range(KD):
                    mm(pb[bi][:, :], wbl1[:, 0 if k < 8 else 1, (k % 8) * 128:(k % 8 + 1) * 128],
                       xn5[:, k, tb * NT:(tb + 1) * NT],
                       start=(k == 0), stop=(k == KD - 1),
                       inc=("st" if k == KD - 1 else None))
                tqk = p.n["st"]
                if tb == NTB - 1:
                    wst_guard[4] = tqk
                    wst_guard[5] = tqk
                sl = slice(tb * NT, (tb + 1) * NT)
                if part == "q":
                    p.wait("v", "st", tqk)
                    vq1 = vts(q_l1[:, j, sl], pb[bi][:, :], b1c[:, m:m + 1], AL.add,
                              inc="sv")
                    bank_set(bi, "sv", vq1)
                elif part == "k":
                    p.wait("s", "st", tqk)
                    sk = sact(kscr[:, sl], pb[bi][:, :], AF.Identity,
                              bias=b1c[:, m:m + 1])
                    bank_set(bi, "ss", sk)
                else:
                    p.wait("s", "st", tqk)
                    svv = sact(vscr[:, sl], pb[bi][:, :], AF.Identity,
                               bias=b1c[:, m:m + 1])
                    bank_set(bi, "ss", svv)
        p.wait("s", "sv", p.n["sv"])
        sq2 = sact(sp92[:, 0:T], q_l1[:, j, :], AF.Square, accum=acc[:, j:j + 1])
        sk2 = sact(sp92[:, 0:T], kscr[:, :], AF.Square, accum=acc[:, 16 + j:17 + j])
        p.wait("v", "ss", sk2)
        vkv = vtt(sp92[:, T:2 * T], kscr[:, :], vscr[:, :], AL.mult)
        vkr = p.run("v", lambda eng, j=j: eng.reduce_sum(
            acc[:, 32 + j:33 + j].rearrange("p (a b) -> p a b", b=1),
            sp92[:, T:2 * T].rearrange("p (a b) -> p a b", b=T), axis=AX), inc="sv")
        p.wait("s", "sv", vkr)

    p.wait("g", "sv", p.n["sv"])
    p.wait("g", "ss", p.n["ss"])
    arin = p.dma("g", "dg", dRin.ap(), acc[:, 0:48])
    p.wait("g", "dg", arin)
    p.run("g", lambda eng: eng.collective_compute(
        "AllReduce", AL.add, replica_groups=[list(range(NCORES))],
        ins=[dRin.ap().opt()], outs=[dRout.ap().opt()]), inc="cc")
    p.wait("g", "cc", p.n["cc"])
    grl = p.dma("g", "dg", acc[:, 48:96], dRout.ap())
    p.wait("v", "dg", grl)
    vnn = vtt(gst[:, 0:16], acc[:, 48:64], acc[:, 64:80], AL.mult, inc="sv")
    p.wait("s", "sv", vnn)
    snn = sact(gst[:, 16:32], gst[:, 0:16], AF.Sqrt)
    p.wait("v", "ss", snn)
    vrc2 = p.run("v", lambda eng: eng.reciprocal(gst[:, 32:48], gst[:, 16:32]),
                 inc="sv")
    p.wait("s", "sv", vrc2)
    scp = sact(gst[:, 48:64], gst[:, 32:48], AF.Copy)
    p.wait("v", "ss", scp)
    vcf = vtt(gst[:, 0:16], acc[:, 80:96], gst[:, 48:64], AL.mult, inc="sv")
    p.wait("s", "sv", vcf)
    scb = sact(cbf[:, 0:16], gst[:, 0:16], AF.Copy)
    sc2 = sact(cbf[:, 16:32], gst[:, 0:16], AF.Square)
    p.wait("t", "ss", sc2)
    p.wait("v", "ss", sc2)
    tap("qL1", q_l1[:, :, :].rearrange("p k t -> p (k t)"))
    tap("cfac", gst[:, 0:16])

    bank_wait("t", 4)
    bank_wait("t", 5)
    for tb in range(NTB):
        for k in range(KD):
            mm(pb[4][0:1, :], cbf[:, k:k + 1], q_l1[:, k, tb * NT:(tb + 1) * NT],
               start=(k == 0), stop=(k == KD - 1), inc=("st" if k == KD - 1 else None))
        p.wait("v", "st", p.n["st"])
        vsx = vcopy(rowA[0:1, tb * NT:(tb + 1) * NT], pb[4][0:1, :], inc="sv")
        p.wait("t", "sv", vsx)
    for tb in range(NTB):
        for k in range(KD):
            gvq = sq_guard[k % 2]
            p.wait("s", "st", gvq)
            sqq = sact(sp92[:, (k % 2) * NT:(k % 2 + 1) * NT],
                       q_l1[:, k, tb * NT:(tb + 1) * NT], AF.Square)
            p.wait("t", "ss", sqq)
            gv2 = mm(pb[5][0:1, :], cbf[:, 16 + k:17 + k],
                     sp92[:, (k % 2) * NT:(k % 2 + 1) * NT],
                     start=(k == 0), stop=(k == KD - 1), inc="st")
            sq_guard[k % 2] = gv2
        p.wait("v", "st", p.n["st"])
        vsq = vcopy(rowB[0:1, tb * NT:(tb + 1) * NT], pb[5][0:1, :], inc="sv")
        p.wait("t", "sv", vsq)
    vts(rowA[0:1, :], rowA[0:1, :], 1.0 / D, AL.mult)
    vts(rowB[0:1, :], rowB[0:1, :], 1.0 / D, AL.mult)
    vstt(rowC[0:1, :], rowA[0:1, :], -1.0, AL.mult, rowA[0:1, :], AL.mult)
    v1 = vtt(rowB[0:1, :], rowB[0:1, :], rowC[0:1, :], AL.add)
    p.wait("s", "sv", v1)
    s1 = sact(rowB[0:1, :], rowB[0:1, :], AF.Ln, bias=EPS)
    s2 = sact(rowB[0:1, :], rowB[0:1, :], AF.Exp, scale=-0.5)
    p.wait("v", "ss", s2)
    vz7 = vstt(rowC[0:1, :], rowA[0:1, :], -1.0, AL.mult, rowB[0:1, :], AL.mult,
               inc="sv")
    p.wait("t", "sv", vz7)
    rows = [rowB, rowC]
    for jj in range(2):
        bank_wait("t", jj)
        for tb in range(NTB):
            tm = mm(pb[jj][:, :], ones1[0:1, :], rows[jj][0:1, tb * NT:(tb + 1) * NT],
                    start=True, stop=True, inc="st")
            p.wait("v", "st", tm)
            vv = vcopy(bc[jj][:, tb * NT:(tb + 1) * NT], pb[jj][:, :], inc="sv")
            p.wait("t", "sv", vv)
        bank_set(jj, "sv", vv)
    for k in range(KD):
        vstt(scrA[:, k % 2, :], q_l1[:, k, :], gst[:, k:k + 1], AL.mult,
             bc[0][:, :], AL.mult)
        vtt(scrA[:, k % 2, :], scrA[:, k % 2, :], bc[1][:, :], AL.add)
        vup = vtt(resid[:, k, :], resid[:, k, :], scrA[:, k % 2, :], AL.add, inc="sv")
    p.wait("t", "sv", vup)
    p.wait("s", "sv", vup)
    tap("x3", resid[:, :, :].rearrange("p k t -> p (k t)"))

    ln_stats(lambda k: resid[:, k, :], F32R, onesc[:, 0:1])
    for k in range(KD):
        vtt(scrA[:, k % 2, :], resid[:, k, :], bc[0][:, :], AL.mult)
        vx6 = vtt(xn6[:, k, :], scrA[:, k % 2, :], bc[1][:, :], AL.add, inc="sv")
    p.wait("y", "sv", vx6)
    p.wait("t", "sv", vx6)
    tap("xn6", xn6[:, :, :].rearrange("p k t -> p (k t)"))
    spv = p.dma("y", "dy", dSpill.ap(), resid[:, :, :].rearrange("p k t -> p (k t)"))
    p.wait("v", "dy", spv)
    p.wait("g", "dg", quant_done)
    p.wait("g", "sv", p.n["sv"])
    p.wait("g", "ss", p.n["ss"])
    p.wait("g", "dy", spv)

    for m in range(MI):
        bi = m % 2
        gv = wst_guard.get(6 + bi, 0)
        p.wait("g", "st", gv)
        lg = p.dma("g", "dg", gwb[:, bi, :], dQ8g.ap()[m])
        lu = p.dma("g", "dg", uwb[:, bi, :], dQ8u.ap()[m])
        p.wait("t", "dg", lu)
        for tb in range(NTB):
            b0 = (m * NTB + tb) % 2
            b1_ = 2 + (m * NTB + tb) % 2
            bank_wait("t", b0)
            for k in range(KD):
                mm(pb[b0][:, :], gwb[:, bi, k * 128:(k + 1) * 128],
                   xn6[:, k, tb * NT:(tb + 1) * NT],
                   start=(k == 0), stop=(k == KD - 1), inc=("st" if k == KD - 1 else None))
            tgate = p.n["st"]
            bank_wait("t", b1_)
            for k in range(KD):
                mm(pb[b1_][:, :], uwb[:, bi, k * 128:(k + 1) * 128],
                   xn6[:, k, tb * NT:(tb + 1) * NT],
                   start=(k == 0), stop=(k == KD - 1), inc=("st" if k == KD - 1 else None))
            tup = p.n["st"]
            if tb == NTB - 1:
                wst_guard[6 + bi] = tup
            sl = slice(tb * NT, (tb + 1) * NT)
            p.wait("s", "st", tgate)
            sg_ = sact(sgs[:, :], pb[b0][:, :], AF.Silu, scale=gsc[:, m:m + 1])
            bank_set(b0, "ss", sg_)
            p.wait("v", "st", tup)
            p.wait("v", "ss", sg_)
            vfu = vstt(fuse[:, m, sl], pb[b1_][:, :], usc[:, m:m + 1], AL.mult,
                       sgs[:, :], AL.mult, inc="sv")
            bank_set(b1_, "sv", vfu)
            p.wait("s", "sv", vfu)
    tap("fuse", fuse[:, 0:4, :].rearrange("p a b -> p (a b)"))

    p.wait("g", "sv", p.n["sv"])
    p.wait("g", "ss", p.n["ss"])
    wst_guard[6] = wst_guard[7] = max(wst_guard[6], wst_guard[7])
    for mo in range(16):
        bi = mo % 2
        gv = wst_guard.get(6 + bi, 0) if mo < 2 else dwb_guard.get(bi, 0)
        p.wait("g", "st", gv)
        ld = p.dma("g", "dg", dwb[:, bi, :], dQ8d.ap()[mo])
        p.wait("t", "dg", ld)
        for tb in range(NTB):
            b0 = (mo * NTB + tb) % 4
            bank_wait("t", b0)
            for ki in range(MI):
                mm(pb[b0][:, :], dwb[:, bi, ki * 128:(ki + 1) * 128],
                   fuse[:, ki, tb * NT:(tb + 1) * NT],
                   start=(ki == 0), stop=(ki == MI - 1),
                   inc=("st" if ki == MI - 1 else None))
            tdn = p.n["st"]
            if tb == NTB - 1:
                dwb_guard[bi] = tdn
            p.wait("s", "st", tdn)
            sdn = sact(ao[:, mo, tb * NT:(tb + 1) * NT], pb[b0][:, :], AF.Copy,
                       scale=dsc[:, mo:mo + 1])
            bank_set(b0, "ss", sdn)
    p.wait("v", "ss", p.n["ss"])
    tap("ffo", ao[:, :, :].rearrange("p k t -> p (k t)"))
    last_dn = p.n["st"]
    p.wait("y", "st", last_dn)
    p.wait("s", "st", last_dn)
    rlv = p.dma("y", "dy", resid[:, :, :].rearrange("p k t -> p (k t)"), dSpill.ap())
    p.wait("v", "dy", rlv)
    p.wait("t", "dy", rlv)

    post_ln_update()

    p.wait("y", "sv", p.n["sv"])
    ov = p.dma("y", "dy", dout.ap(), resid[:, :, :].rearrange("p k t -> p (k t)"))
    p.wait("y", "dy", ov)
    p.wait("g", "dg", p.n["dg"])

    lp = nc.allow_low_precision(
        reason="f32r intermediates (fp22 reads) are well inside the "
               "validated tolerance; accumulations stay fp32 in PSUM")
    lp.__enter__()
    with nc.Block() as block:
        @block.tensor
        def _(eng):
            for f in p.ops["t"]:
                f(eng)

        @block.vector
        def _(eng):
            for f in p.ops["v"]:
                f(eng)

        @block.scalar
        def _(eng):
            for f in p.ops["s"]:
                f(eng)

        @block.gpsimd
        def _(eng):
            for f in p.ops["g"]:
                f(eng)

        @block.sync
        def _(eng):
            for f in p.ops["y"]:
                f(eng)

    lp.__exit__(None, None, None)
    stack.close()
    return nc


def _sub(dram, i):
    class W:
        def ap(self):
            return dram.ap()[i]
    return W()


def host_inputs(inputs):
    x = np.asarray(inputs["L0_attn_x"])[0]
    Wq = _retile(np.asarray(inputs["L0_attn_Wq"]))
    Wk = np.stack([_retile(np.asarray(inputs["L0_attn_Wk"])[i]) for i in range(NH)])
    Wv = np.stack([_retile(np.asarray(inputs["L0_attn_Wv"])[i]) for i in range(NH)])
    Wo = _retile(np.asarray(inputs["L0_attn_Wo"]))
    Wg0 = np.asarray(inputs["L0_attn_Wg"]).reshape(KD, 128, 3)
    Wg = np.zeros((128, KD * 4), dtype=np.float32)
    Wg.reshape(128, KD, 4)[:, :, 0:3] = Wg0.transpose(1, 0, 2)
    dl = np.zeros((1, 4), dtype=np.float32)
    dl[0, 0:3] = np.asarray(inputs["L0_attn_decay_logits"])
    osc_ = np.asarray(inputs["L0_attn_out_scale"]).reshape(1, 1)
    Wb = _retile(np.ascontiguousarray(np.asarray(inputs["L0_ffn_weight"]).T))
    A = np.asarray(inputs["L0_ffn_lora_A"])
    Ap = np.ascontiguousarray(_retile(np.ascontiguousarray(A.reshape(128, D).T))
                              .reshape(128, D))
    B = np.asarray(inputs["L0_ffn_lora_B"])
    Bp = np.ascontiguousarray(
        _retile(np.ascontiguousarray(B.transpose(0, 2, 1).reshape(128, D)))
        .transpose(1, 0, 2).reshape(128, 16 * 128))
    Rt = np.ascontiguousarray(np.asarray(inputs["L0_ffn_router_w"]).T
                              .reshape(KD, 128, 8).transpose(1, 0, 2)
                              .reshape(128, KD * 8))
    W1 = _retile(np.asarray(inputs["L1_attn_W"]))
    b1 = _cols(np.asarray(inputs["L1_attn_b"]))
    GW = _retile(np.asarray(inputs["L1_ffn_gate_w"]))
    UW = _retile(np.asarray(inputs["L1_ffn_up_w"]))
    DW = _retile(np.asarray(inputs["L1_ffn_down_w"]))
    GS = _cols(np.asarray(inputs["L1_ffn_gate_scale"]))
    US = _cols(np.asarray(inputs["L1_ffn_up_scale"]))
    DS = _cols(np.asarray(inputs["L1_ffn_down_scale"]))

    import ml_dtypes
    identb = np.eye(128, dtype=ml_dtypes.bfloat16)
    identr = np.eye(128, dtype=np.float32)
    s_i = np.arange(128)
    expnT = np.maximum(s_i[:, None] - s_i[None, :] - 1, 0).astype(np.float32)
    causT = (s_i[:, None] > s_i[None, :]).astype(np.float32)
    tpos = np.arange(T)
    e1 = (127 - (tpos % 128)).astype(np.float32).reshape(1, T)
    e2 = (1023 - tpos).astype(np.float32).reshape(1, T)
    iotac = np.stack([np.arange(128, dtype=np.float32),
                      np.full(128, 128.0, dtype=np.float32)], axis=1)
    onesc = np.ones((128, 2), dtype=np.float32)
    ones1 = np.ones((1, 128), dtype=np.float32)
    ones1b = np.ones((1, 128), dtype=ml_dtypes.bfloat16)
    pm = (np.arange(128)[None, :] // 16 == np.arange(8)[:, None]).astype(ml_dtypes.bfloat16)
    sh = np.zeros((3, NH * 128), dtype=ml_dtypes.bfloat16)
    for hh in range(NH):
        sh[hh, hh * 128:(hh + 1) * 128] = 1.0
    cn = np.tile(np.array([[EPS, -0.5]], dtype=np.float32), (128, 1))

    common = dict(Wq=Wq, Wk=Wk, Wv=Wv, Wo=Wo, Wg=Wg, decay_logits=dl, out_scale=osc_,
                  Wbase=Wb, loraA=Ap, loraB=Bp, router=Rt, W1=W1, b1=b1,
                  gatew=GW, upw=UW, downw=DW, gates=GS, ups=US, downs=DS,
                  identb=identb, identr=identr, expnT=expnT, causT=causT,
                  e1row=e1, e2row=e2, iotac=iotac, onesc=onesc, ones1=ones1,
                  ones1b=ones1b, pmat=pm, selh=sh, cnsts=cn)
    in_maps = []
    for r in range(NCORES):
        xs = np.ascontiguousarray(
            x[r * T:(r + 1) * T, :].T.reshape(KD, 128, T)
            .transpose(1, 0, 2).reshape(128, KD * T))
        bexpo = np.array([[max(1024.0 * (rp - r - 1), 0.0) for rp in range(8)]],
                         dtype=np.float32)
        bmask = np.array([[1.0 if rp > r else 0.0 for rp in range(8)]],
                         dtype=np.float32)
        m = dict(common)
        m.update(x=xs, bexpo=bexpo, bmask=bmask)
        in_maps.append(m)
    return in_maps


def kernel(**inputs):
    in_maps = host_inputs(inputs)
    nc = build_program()
    res = run_bass_kernel_spmd(nc, in_maps, core_ids=list(range(NCORES)))
    outs = []
    for r in range(NCORES):
        o = np.asarray(res.results[r]["out"])
        outs.append(o.reshape(128, KD, T).transpose(2, 1, 0).reshape(T, D))
    return np.concatenate(outs, axis=0)[None, :, :].astype(np.float32)


# version 12 (final)
# speedup vs baseline: 1.0816x; 1.0192x over previous
import numpy as np
import concourse.bass as bass
import concourse.mybir as mybir
from concourse.bass_utils import run_bass_kernel_spmd

F32 = mybir.dt.float32
F32R = mybir.dt.float32r
BF16 = mybir.dt.bfloat16
FP8 = mybir.dt.float8e4
AF = mybir.ActivationFunctionType
AL = mybir.AluOpType
AX = mybir.AxisListType.X

NCORES = 8
T = 1024
D = 2048
KD = 16
NT = 512
NTB = T // NT
DKH = 512
NH = 3
CH = 128
NCHK = T // CH
DI = 6144
MI = DI // 128
EPS = 1e-5
BIG = 1.0e30

KB = 1024


def _retile(w):
    din, dout = w.shape
    K, M = din // 128, dout // 128
    return np.ascontiguousarray(
        w.reshape(K, 128, M, 128).transpose(2, 1, 0, 3)).reshape(M, 128, K * 128)


def _cols(v):
    return np.ascontiguousarray(v.reshape(-1, 128).T)



class PB:
    ENGS = ("t", "v", "s", "g", "y")
    DQ = {"dg": ("dg", "dg2"), "dy": ("dy", "dy2")}

    def __init__(self):
        self.ops = {e: [] for e in self.ENGS}
        self.sem = {}
        self.n = {"dg": 0, "dy": 0, "st": 0, "sv": 0, "ss": 0, "sg": 0, "cc": 0}
        self.di = {"dg": 0, "dy": 0}

    def raw(self, e, fn):
        self.ops[e].append(fn)

    def _wait1(self, e, q, v):
        if v <= 0:
            return
        self.raw(e, lambda eng, q=q, v=v: eng.wait_ge(self.sem[q], v))

    def wait(self, e, q, v):
        if v <= 0:
            return
        if q in self.DQ:
            i = v // 16
            a, b = self.DQ[q]
            self._wait1(e, a, 16 * ((i + 1) // 2))
            self._wait1(e, b, 16 * (i // 2))
        else:
            self._wait1(e, q, v)

    def run(self, e, fn, inc=None):
        if inc is None:
            self.raw(e, fn)
            return self.n.get(None, 0)
        self.n[inc] += 1
        v = self.n[inc]
        self.raw(e, lambda eng, fn=fn, q=inc: fn(eng).then_inc(self.sem[q], 1))
        return v

    def dma(self, e, q, out, in_):
        self.di[q] += 1
        i = self.di[q]
        a, b = self.DQ[q]
        sem = a if i % 2 == 1 else b
        self._wait1(e, sem, 16 * ((i + 1) // 2 - 1))
        self.raw(e, lambda eng, out=out, in_=in_, s=sem:
                 eng.dma_start(out=out, in_=in_).then_inc(self.sem[s], 16))
        self.n[q] = 16 * i
        return 16 * i


def build_program(dbg=()):
    nc = bass.Bass("TRN2", target_bir_lowering=False)
    p = PB()

    dx = nc.dram_tensor("x", [128, KD * T], F32R, kind="ExternalInput").ap()
    dout = nc.dram_tensor("out", [128, KD * T], F32R, kind="ExternalOutput")

    dWq = nc.dram_tensor("Wq", [4, 128, D], F32R, kind="ExternalInput")
    dWk = nc.dram_tensor("Wk", [NH, 4, 128, D], F32R, kind="ExternalInput")
    dWv = nc.dram_tensor("Wv", [NH, 4, 128, D], F32R, kind="ExternalInput")
    dWo = nc.dram_tensor("Wo", [16, 128, DKH], F32, kind="ExternalInput")
    dWg = nc.dram_tensor("Wg", [128, KD * 4], F32R, kind="ExternalInput")
    dDL = nc.dram_tensor("decay_logits", [1, 4], F32, kind="ExternalInput").ap()
    dOS = nc.dram_tensor("out_scale", [1, 1], F32R, kind="ExternalInput").ap()
    dWb = nc.dram_tensor("Wbase", [16, 128, D], F32R, kind="ExternalInput")
    dA = nc.dram_tensor("loraA", [128, D], F32R, kind="ExternalInput")
    dB = nc.dram_tensor("loraB", [128, 16 * 128], F32R, kind="ExternalInput")
    dRt = nc.dram_tensor("router", [128, KD * 8], F32R, kind="ExternalInput")
    dW1 = nc.dram_tensor("W1", [MI, 128, D], F32, kind="ExternalInput")
    dB1 = nc.dram_tensor("b1", [128, MI], F32, kind="ExternalInput").ap()
    dGW = nc.dram_tensor("gatew", [MI, 128, D], F32, kind="ExternalInput")
    dUW = nc.dram_tensor("upw", [MI, 128, D], F32, kind="ExternalInput")
    dDW = nc.dram_tensor("downw", [16, 128, DI], F32, kind="ExternalInput")
    dGS = nc.dram_tensor("gates", [128, MI], F32, kind="ExternalInput").ap()
    dUS = nc.dram_tensor("ups", [128, MI], F32, kind="ExternalInput").ap()
    dDS = nc.dram_tensor("downs", [128, 16], F32, kind="ExternalInput").ap()

    dID = nc.dram_tensor("identb", [128, 128], BF16, kind="ExternalInput").ap()
    dIDr = nc.dram_tensor("identr", [128, 128], F32R, kind="ExternalInput").ap()
    dEXP = nc.dram_tensor("expnT", [128, 128], F32, kind="ExternalInput").ap()
    dCAU = nc.dram_tensor("causT", [128, 128], F32, kind="ExternalInput").ap()
    dE1 = nc.dram_tensor("e1row", [1, T], F32R, kind="ExternalInput").ap()
    dE2 = nc.dram_tensor("e2row", [1, T], F32R, kind="ExternalInput").ap()
    dIOT = nc.dram_tensor("iotac", [128, 2], F32, kind="ExternalInput").ap()
    dONE = nc.dram_tensor("onesc", [128, 2], F32R, kind="ExternalInput").ap()
    dONE1 = nc.dram_tensor("ones1", [1, 128], F32R, kind="ExternalInput").ap()
    dONE1b = nc.dram_tensor("ones1b", [1, 128], BF16, kind="ExternalInput").ap()
    dPM = nc.dram_tensor("pmat", [8, 128], BF16, kind="ExternalInput").ap()
    dSH = nc.dram_tensor("selh", [3, NH * 128], BF16, kind="ExternalInput").ap()
    dCN = nc.dram_tensor("cnsts", [128, 2], F32, kind="ExternalInput").ap()
    dBEX = nc.dram_tensor("bexpo", [1, 8], F32, kind="ExternalInput").ap()
    dBMK = nc.dram_tensor("bmask", [1, 8], F32, kind="ExternalInput").ap()

    dQ8g = nc.dram_tensor("q8g", [MI, 128, D], FP8)
    dQ8u = nc.dram_tensor("q8u", [MI, 128, D], FP8)
    dQ8d = nc.dram_tensor("q8d", [16, 128, DI], FP8)
    dSpill = nc.dram_tensor("spill", [128, KD * T], F32R)
    dBin = nc.dram_tensor("bstates_in", [NH * 128, 4 * DKH], BF16)
    dBout = nc.dram_tensor("bstates_out", [NCORES * NH * 128, 4 * DKH], BF16)
    dRin = nc.dram_tensor("rstats_in", [128, 48], F32)
    dRout = nc.dram_tensor("rstats_out", [128, 48], F32)

    dbg_t = {}
    _f32r_taps = {"ln1", "x1", "x2", "x3", "axt"}
    for name in dbg:
        shp = dict(ln1=[2, T], qT=[128, 4 * T], kT=[128, 4 * T], vTM=[128, 8 * DKH],
                   gate=[3, T], blend=[128, 4 * T], ao=[128, KD * T], x1=[128, KD * T],
                   x2=[128, KD * T], x3=[128, KD * T], wd=[8, T], axt=[128, T],
                   xn5=[128, KD * T], qL1=[128, KD * T], cfac=[128, 16],
                   xn6=[128, KD * T], fuse=[128, 4 * T], ffo=[128, KD * T],
                   Sst=[128, 4 * DKH], Ebn=[128, 4 * DKH],
                   qg=[128, 4 * T], qgd=[128, 4 * T], kdTM=[128, 8 * DKH],
                   blend0=[128, 4 * T], blendi=[128, 4 * T],
                   logd=[128, 4], djc=[128, 3], dCc=[128, 3],
                   gateb=[128, T])[name]
        dt_ = F32R if name in _f32r_taps else F32
        dbg_t[name] = nc.dram_tensor("dbg_" + name, shp, dt_, kind="ExternalOutput")

    import contextlib
    stack = contextlib.ExitStack()
    arena_guard = stack.enter_context(nc.sbuf_tensor("arena", [128, 192 * KB],
                                                     mybir.dt.uint8))

    smalls = {}
    def small(name, shape, dtype):
        h = stack.enter_context(nc.sbuf_tensor(name, shape, dtype))
        smalls[name] = h
        return h

    rowA = small("rowA", [1, T], F32R)
    rowB = small("rowB", [1, T], F32R)
    rowC = small("rowC", [1, T], F32R)
    csq = small("csq", [128, 64], F32)
    rcol = small("rcol", [128, 8], F32)
    zcol = small("zcol", [128, 8], F32)
    logd = small("logd", [128, 4], F32)
    djc = small("djc", [128, 4], F32)
    dCc = small("dCc", [128, 4], F32)
    wcs = small("wcs", [128, 8], F32)
    osc = small("oscol", [128, 1], F32)
    identb = small("identb_s", [128, 128], BF16)
    identr = small("identr_s", [128, 128], F32R)
    onesc = small("onesc_s", [128, 2], F32R)
    cnst = small("cnst_s", [128, 2], F32)
    nc.const_aps.register(F32, EPS, cnst[:, 0:1])
    nc.const_aps.register(F32, -0.5, cnst[:, 1:2])
    onescb = small("onescb", [128, 2], BF16)
    iotac = small("iotac_s", [128, 2], F32)
    ones1 = small("ones1_s", [1, 128], F32R)
    ones1b = small("ones1b_s", [1, 128], BF16)
    pmat = small("pmat_s", [8, 128], BF16)
    dlrow = small("dlrow", [1, 16], F32)
    dlogr = small("dlogr", [1, 4], F32R)
    oscs = small("oscs", [1, 2], F32R)
    bxrow = small("bxrow", [1, 16], F32)
    wgs = small("wgs", [128, KD * 4], F32R)
    rts = small("rts", [128, KD * 8], F32R)

    base_addr = nc.lookup_mloc(arena_guard).addr
    alias_n = [0]

    def alias(shape, dtype, off_kb):
        alias_n[0] += 1
        return nc.alloc_sbuf_tensor_at(
            "al%d" % alias_n[0], shape, dtype, offset=base_addr + int(off_kb * KB))

    lA = alias([128, KD * 128], F32R, 128)
    lB = alias([128, 16 * 128], F32R, 136)
    axt = alias([128, T], F32, 144)
    axw = alias([128, T], F32R, 148)
    expnT = alias([128, 128], F32, 96)
    causT = alias([128, 128], F32, 96.5)
    gtT = alias([3, T], BF16, 97)
    maskb = alias([128, NH * 128], BF16, 99)
    selh = alias([3, NH * 128], BF16, 99.75)
    wdt = alias([8, T], BF16, 84)
    glog = alias([128, 8 * 4], F32, 86)
    gmx = alias([128, 8 * 4], F32, 86.25)
    glog2 = alias([128, 8 * 4], F32, 87)
    rlog2 = alias([128, 8 * 8], F32, 87.25)
    rlog = alias([128, 8 * 8], F32, 86.5)
    rtop = alias([128, 8 * 4], F32, 86.75)
    acc = alias([128, 96], F32, 178)
    gst = alias([128, 64], F32, 178.5)
    cbf = alias([128, 32], BF16, 178.75)
    b1c = alias([128, MI], F32, 179)
    gsc = alias([128, MI], F32, 179.25)
    usc = alias([128, MI], F32, 179.5)
    dsc = alias([128, 16], F32, 179.75)

    resid = alias([128, KD, T], F32R, 0)
    fuse = alias([128, MI, T], BF16, 0)
    scrA = alias([128, 2, T], F32R, 64)
    vFM = alias([128, 4, T], BF16, 136)
    bc = [alias([128, T], F32R, 72 + 4 * i) for i in range(3)]
    Sbf = alias([128, 4, DKH], BF16, 84)
    bstmp = alias([128, 4 * DKH], BF16, 84)
    Ebn = alias([128, 4, DKH], BF16, 88)
    sp92 = alias([128, 2 * T], BF16, 92)
    ao = alias([128, KD, T], BF16, 96)
    qT = alias([128, 4, T], BF16, 128)
    qg = alias([128, 4, T], BF16, 136)
    qgd = alias([128, 4, T], BF16, 144)
    kT = alias([128, 4, T], BF16, 152)
    kdTM = alias([128, NCHK, DKH], BF16, 160)
    vTM = alias([128, NCHK, DKH], BF16, 168)
    blendT = alias([128, 4, T], BF16, 176)
    wst = alias([128, 2, 8 * 128], F32R, 184)
    xn5 = alias([128, KD, T], BF16, 128)
    kscr = alias([128, T], BF16, 160)
    vscr = alias([128, T], BF16, 164)
    wbl1 = alias([128, 2, 8 * 128], BF16, 184)
    xn6 = alias([128, KD, T], BF16, 128)
    gwb = alias([128, 2, D], BF16, 160)
    uwb = alias([128, 2, D], BF16, 168)
    sgs = alias([128, NT], F32, 176)
    dwb = alias([128, 2, MI * 128], BF16, 128)

    pb = [stack.enter_context(nc.psum_tensor("pb%d" % i, [128, NT], F32))
          for i in range(6)]
    ptb = stack.enter_context(nc.psum_tensor("ptb", [128, 4 * 128], BF16))
    ptr = stack.enter_context(nc.psum_tensor("ptr", [128, 128], F32R))

    sems = {}
    for q in ("dg", "dg2", "dy", "dy2", "st", "sv", "ss", "sg", "cc"):
        sems[q] = stack.enter_context(nc.semaphore(q))
    p.sem = {q: sems[q] for q in sems}

    bankuser = {}

    def bank_wait(e, i):
        u = bankuser.get(i)
        if u:
            p.wait(e, u[0], u[1])

    def bank_set(i, q, v):
        bankuser[i] = (q, v)

    def vcopy(out, in_, inc="sv", e="v"):
        return p.run(e, lambda eng, o=out, x=in_: eng.tensor_copy(o, x), inc=inc)

    def vtt(out, a, b, op, inc="sv", e="v"):
        return p.run(e, lambda eng, o=out, a=a, b=b, op=op:
                     eng.tensor_tensor(o, a, b, op), inc=inc)

    def vts(out, a, s1, op, s2=None, op2=AL.bypass, inc="sv", e="v"):
        if s2 is None:
            return p.run(e, lambda eng, o=out, a=a, s1=s1, op=op:
                         eng.tensor_single_scalar(o, a, s1, op), inc=inc)
        return p.run(e, lambda eng, o=out, a=a, s1=s1, s2=s2, op=op, op2=op2:
                     eng.tensor_scalar(o, a, s1, s2, op, op2), inc=inc)

    def vstt(out, in0, sc, op0, in1, op1, inc="sv", e="v"):
        return p.run(e, lambda eng, o=out, a=in0, s=sc, b=in1, op0=op0, op1=op1:
                     eng.scalar_tensor_tensor(o, a, s, b, op0, op1), inc=inc)

    def vbounce():
        v = p.n["sv"]
        p.wait("s", "sv", v)
        s0 = p.run("s", lambda eng: eng.nop(), inc="ss")
        p.wait("v", "ss", s0)

    def sbounce():
        s = p.n["ss"]
        p.wait("v", "ss", s)
        v0 = p.run("v", lambda eng: eng.nop(), inc="sv")
        p.wait("s", "sv", v0)

    def sact(out, in_, func, bias=0.0, scale=1.0, accum=None, inc="ss"):
        if func not in (AF.Copy, AF.Reciprocal):
            if isinstance(bias, float):
                bias = np.float32(bias)
            if isinstance(scale, float):
                scale = np.float32(scale)
        return p.run("s", lambda eng, o=out, x=in_, f=func, b=bias, sc=scale, ac=accum:
                     eng.activation(o, x, f, bias=b, scale=sc, accum_out=ac), inc=inc)

    def mm(out, lhsT, rhs, start, stop, inc=None):
        return p.run("t", lambda eng, o=out, l=lhsT, r=rhs, a=start, b=stop:
                     eng.matmul(o, l, r, start=a, stop=b), inc=inc)

    def ttrans(out, in_, ident, inc=None):
        return p.run("t", lambda eng, o=out, x=in_, i=ident:
                     eng.transpose(o, x, i), inc=inc)

    def ln_stats(src_tile, src_dtype, rsq_ones, sq_in_tile=None):
        p.wait("t", "sv", p.n["sv"])
        p.wait("t", "ss", p.n["ss"])
        p.wait("s", "sv", p.n["sv"])
        bank_wait("t", 4)
        bank_wait("t", 5)
        for tb in range(NTB):
            for k in range(KD):
                mm(pb[4][0:1, :], rsq_ones, src_tile(k)[:, tb * NT:(tb + 1) * NT],
                   start=(k == 0), stop=(k == KD - 1),
                   inc=("st" if k == KD - 1 else None))
            p.wait("v", "st", p.n["st"])
            vsx = vcopy(rowA[0:1, tb * NT:(tb + 1) * NT], pb[4][0:1, :], inc="sv")
            p.wait("t", "sv", vsx)
        for tb in range(NTB):
            for k in range(KD):
                sq_src = (sq_in_tile or src_tile)(k)[:, tb * NT:(tb + 1) * NT]
                if k >= 2 or tb > 0:
                    p.wait("s", "st", sq_guard[k % 2])
                ssq = sact(scrA[:, k % 2, :NT], sq_src, AF.Square)
                p.wait("t", "ss", ssq)
                gv = mm(pb[5][0:1, :], onesc[:, 0:1], scrA[:, k % 2, :NT],
                        start=(k == 0), stop=(k == KD - 1),
                        inc="st")
                sq_guard[k % 2] = gv
            p.wait("v", "st", p.n["st"])
            vq = vcopy(rowB[0:1, tb * NT:(tb + 1) * NT], pb[5][0:1, :], inc="sv")
            p.wait("t", "sv", vq)
        vts(rowA[0:1, :], rowA[0:1, :], 1.0 / D, AL.mult)
        vts(rowB[0:1, :], rowB[0:1, :], 1.0 / D, AL.mult)
        vstt(rowC[0:1, :], rowA[0:1, :], -1.0, AL.mult, rowA[0:1, :], AL.mult)
        v1 = vtt(rowB[0:1, :], rowB[0:1, :], rowC[0:1, :], AL.add)
        p.wait("s", "sv", v1)
        s1 = sact(rowB[0:1, :], rowB[0:1, :], AF.Ln, bias=EPS)
        s2 = sact(rowB[0:1, :], rowB[0:1, :], AF.Exp, scale=-0.5)
        p.wait("v", "ss", s2)
        v2 = vstt(rowC[0:1, :], rowA[0:1, :], -1.0, AL.mult, rowB[0:1, :], AL.mult,
                  inc="sv")
        p.wait("t", "sv", v2)
        rows = [rowB, rowC]
        for j, dst in enumerate(bc[:2]):
            bank_wait("t", j)
            for tb in range(NTB):
                tm = mm(pb[j][:, :], ones1[0:1, :],
                        rows[j][0:1, tb * NT:(tb + 1) * NT],
                        start=True, stop=True, inc="st")
                p.wait("v", "st", tm)
                vv = vcopy(dst[:, tb * NT:(tb + 1) * NT], pb[j][:, :], inc="sv")
                p.wait("t", "sv", vv)
            bank_set(j, "sv", vv)
        return

    sq_guard = {0: 0, 1: 0}

    def rz_cols():
        for j, (src, dst) in enumerate(((bc[0], rcol), (bc[1], zcol))):
            for tg in range(8):
                tv = ttrans(ptr[:, :], src[:, tg * 128:(tg + 1) * 128],
                            identr[:, :], inc="st")
                p.wait("v", "st", tv)
                vv = vcopy(dst[:, tg:tg + 1], ptr[:, 0:1], inc="sv")
                p.wait("t", "sv", vv)

    def proj_f32r(wdram, nM, cs_slot, dst, rhs_tile=None, e1="v"):
        rhs_tile = rhs_tile or (lambda k, tb: resid[:, k, tb * NT:(tb + 1) * NT])
        for m in range(nM):
            p.wait("y", "st", max(wst_guard.get(0, 0), wst_guard.get(1, 0)))
            wld = p.dma("y", "dy", wst[:, :, :].rearrange("p h f -> p (h f)"),
                        wdram.ap()[m])
            p.wait("t", "dy", wld)
            bank_wait("t", 2)
            for k in range(KD):
                mm(pb[2][:, 0:2], wst[:, 0 if k < 8 else 1,
                                      (k % 8) * 128:(k % 8 + 1) * 128],
                   onesc[:, 0:2], start=(k == 0), stop=(k == KD - 1),
                   inc=("st" if k == KD - 1 else None))
            tv = p.n["st"]
            p.wait("v", "st", tv)
            vv = vcopy(csq[:, cs_slot + m:cs_slot + m + 1], pb[2][:, 0:1], inc="sv")
            bank_set(2, "sv", vv)
            for tb in range(NTB):
                bi = m % 2
                bank_wait("t", bi)
                for k in range(KD):
                    mm(pb[bi][:, :], wst[:, 0 if k < 8 else 1, (k % 8) * 128:(k % 8 + 1) * 128],
                       rhs_tile(k, tb), start=(k == 0), stop=(k == KD - 1),
                       inc=("st" if k == KD - 1 else None))
                tvv = p.n["st"]
                if tb == NTB - 1:
                    wst_guard[0] = tvv
                    wst_guard[1] = tvv
                p.wait("v", "st", tvv)
                sl = slice(tb * NT, (tb + 1) * NT)
                vtt(scrA[:, 0, :NT], pb[bi][:, :], bc[0][:, sl], AL.mult)
                vv = vstt(dst(m)[:, sl], bc[1][:, sl],
                          csq[:, cs_slot + m:cs_slot + m + 1], AL.mult,
                          scrA[:, 0, :NT], AL.add, inc="sv")
                bank_set(bi, "sv", vv)

    wst_guard = {0: 0, 1: 0}
    dwb_guard = {0: 0, 1: 0}

    p.dma("y", "dy", resid[:, :, :].rearrange("p k t -> p (k t)"), dx)
    xload = p.n["dy"]
    for dst, src in ((identb, dID), (identr, dIDr), (expnT, dEXP), (causT, dCAU),
                     (onesc, dONE), (iotac, dIOT), (ones1, dONE1), (ones1b, dONE1b),
                     (pmat, dPM), (selh, dSH), (cnst, dCN),
                     (wgs, dWg.ap()), (rts, dRt.ap()),
                     ):
        p.dma("y", "dy", dst[:, :] if not hasattr(dst, "ndim") else dst, src)
    p.dma("g", "dg", dlrow[0:1, 0:4], dDL)
    p.dma("g", "dg", bxrow[0:1, 0:8], dBEX)
    p.dma("g", "dg", bxrow[0:1, 8:16], dBMK)
    p.dma("g", "dg", oscs[0:1, 0:1], dOS)
    p.dma("g", "dg", oscs[0:1, 1:2], dOS)
    consts_g = p.n["dg"]
    consts_y = p.n["dy"]

    p.wait("g", "dy", consts_y)
    qmile_gu, qmile_d = [], []
    stepg = MI // 4
    for i in range(0, MI, stepg):
        p.dma("g", "dg", dQ8g.ap()[i:i + stepg], dGW.ap()[i:i + stepg])
        qmile_gu.append(p.dma("g", "dg", dQ8u.ap()[i:i + stepg],
                              dUW.ap()[i:i + stepg]))
    for i in range(0, 16, 4):
        qmile_d.append(p.dma("g", "dg", dQ8d.ap()[i:i + 4], dDW.ap()[i:i + 4]))
    quant_done = p.n["dg"]

    p.wait("v", "dy", consts_y)
    p.wait("v", "dg", consts_g)
    vcopy(onescb[:, :], onesc[:, :])
    p.wait("t", "dg", consts_g)
    p.wait("t", "dy", consts_y)
    tosc = mm(pb[3][:, 0:2], ones1[0:1, :], oscs[0:1, 0:2],
              start=True, stop=True, inc="st")
    p.wait("v", "st", tosc)
    vosc = vcopy(osc[:, 0:1], pb[3][:, 0:1], inc="sv")

    p.wait("s", "dg", consts_g)
    p.wait("s", "dy", consts_y)
    s0 = sact(dlrow[0:1, 4:8], dlrow[0:1, 0:4], AF.Sigmoid)
    sbounce()
    s1 = sact(dlrow[0:1, 8:12], dlrow[0:1, 4:8], AF.Ln)
    sbounce()
    s1r = sact(dlogr[0:1, 0:4], dlrow[0:1, 8:12], AF.Identity)
    p.wait("t", "ss", s1r)
    p.wait("t", "sv", vosc)
    tld = mm(pb[3][:, 2:6], ones1[0:1, :], dlogr[0:1, 0:4],
             start=True, stop=True, inc="st")
    p.wait("v", "st", tld)
    vld = vcopy(logd[:, 0:4], pb[3][:, 2:6], inc="sv")
    bank_set(3, "sv", vld)
    p.wait("t", "sv", vld)
    p.wait("s", "sv", vld)
    vm_hist = {}
    for i in range(NH):
        sact(djc[:, i:i + 1], iotac[:, 0:1], AF.Exp, scale=logd[:, i:i + 1])
        sact(dCc[:, i:i + 1], iotac[:, 1:2], AF.Exp, scale=logd[:, i:i + 1])
        if i >= 2:
            p.wait("s", "sv", vm_hist[i % 2])
        sm = sact(scrA[:, i % 2, 0:128], expnT[:, :], AF.Exp,
                  scale=logd[:, i:i + 1])
        p.wait("v", "ss", sm)
        vm_hist[i % 2] = vtt(maskb[:, i * 128:(i + 1) * 128], scrA[:, i % 2, 0:128],
                             causT[:, :], AL.mult, inc="sv")

    p.wait("t", "dy", xload)

    tap_guard = []

    def tap(name, src_ap, rows=128):
        if name not in dbg_t:
            return
        p.wait("g", "sv", p.n["sv"])
        p.wait("g", "ss", p.n["ss"])
        p.wait("g", "st", p.n["st"])
        tv = p.dma("g", "dg", dbg_t[name].ap()[0:rows], src_ap)
        tap_guard.append(tv)
        for _e in ("t", "v", "s", "y", "g"):
            p.wait(_e, "dg", tv)

    tap("logd", logd[:, 0:4])
    tap("djc", djc[:, 0:3])
    tap("dCc", dCc[:, 0:3])

    ln_stats(lambda k: resid[:, k, :], F32R, onesc[:, 0:1])
    rz_cols()
    if "ln1" in dbg_t:
        p.wait("g", "sv", p.n["sv"])
        tap_guard.append(p.dma("g", "dg", dbg_t["ln1"].ap()[0:1], rowB[0:1, :]))
        tap_guard.append(p.dma("g", "dg", dbg_t["ln1"].ap()[1:2], rowC[0:1, :]))

    GP = 4
    bank_wait("t", 3)
    for tg in range(8):
        for k in range(KD):
            mm(pb[3][:, tg * GP:(tg + 1) * GP], resid[:, k, tg * 128:(tg + 1) * 128],
               wgs[:, k * GP:(k + 1) * GP], start=(k == 0), stop=(k == KD - 1),
               inc=("st" if k == KD - 1 else None))
    tgl = p.n["st"]
    p.wait("s", "st", tgl)
    p.wait("s", "sv", p.n["sv"])
    sgl = 0
    for tg in range(8):
        sgl = sact(glog[:, tg * GP:(tg + 1) * GP], pb[3][:, tg * GP:(tg + 1) * GP],
                   AF.Copy, scale=rcol[:, tg:tg + 1])
    p.wait("v", "ss", sgl)
    bank_wait("t", 4)
    for k in range(KD):
        mm(pb[4][0:1, 0:GP], onesc[:, 0:1], wgs[:, k * GP:(k + 1) * GP],
           start=(k == 0), stop=(k == KD - 1), inc=("st" if k == KD - 1 else None))
    tcg = p.n["st"]
    p.wait("v", "st", tcg)
    vcg = vcopy(rowA[0:1, 0:GP], pb[4][0:1, 0:GP], inc="sv")
    p.wait("t", "sv", vcg)
    tcb = mm(pb[4][:, 0:GP], ones1[0:1, :], rowA[0:1, 0:GP],
             start=True, stop=True, inc="st")
    p.wait("s", "st", tcb)
    scs = sact(scrA[:, 0, 0:GP], pb[4][:, 0:GP], AF.Copy)
    p.wait("v", "ss", scs)
    for tg in range(8):
        vstt(glog2[:, tg * GP:(tg + 1) * GP], scrA[:, 0, 0:GP], zcol[:, tg:tg + 1],
             AL.mult, glog[:, tg * GP:(tg + 1) * GP], AL.add)
    vg1 = p.n["sv"]
    bank_set(3, "sv", vg1)
    bank_set(4, "ss", scs)
    p.wait("s", "sv", vg1)
    sex = 0
    for tg in range(8):
        sex = sact(glog[:, tg * GP:(tg + 1) * GP], glog2[:, tg * GP:(tg + 1) * GP],
                   AF.Exp)
    p.wait("v", "ss", sex)
    vsum = p.run("v", lambda eng: eng.reduce_sum(
        gmx[:, 0:8].rearrange("p (a b) -> p a b", b=1),
        glog[:, :].rearrange("p (a b) -> p a b", b=GP), axis=AX), inc="sv")
    p.wait("s", "sv", vsum)
    sm1 = sact(gmx[:, 8:16], gmx[:, 0:8], AF.Identity, bias=-1.0)
    p.wait("v", "ss", sm1)
    p.run("v", lambda eng: eng.reciprocal(gmx[:, 16:24], gmx[:, 8:16]), inc="sv")
    vrc = p.n["sv"]
    p.wait("s", "sv", vrc)
    spk = 0
    for tg in range(8):
        spk = sact(sp92[:, tg * 3:(tg + 1) * 3], glog[:, tg * GP:tg * GP + 3],
                   AF.Copy, scale=gmx[:, 16 + tg:17 + tg])
    p.wait("t", "ss", spk)
    for tg in range(8):
        tv = ttrans(ptb[0:3, 0:128], sp92[:, tg * 3:(tg + 1) * 3], identb[:, :],
                    inc="st")
        p.wait("v", "st", tv)
        vv = vcopy(gtT[0:3, tg * 128:(tg + 1) * 128], ptb[0:3, 0:128], inc="sv")
        p.wait("t", "sv", vv)
    tap("gate", gtT[0:3, :], rows=3)

    proj_f32r(dWq, 4, 0, lambda m: qT[:, m, :])
    tap("qT", qT[:, :, :].rearrange("p m t -> p (m t)"))

    bstore = []
    for hI in range(NH):
        proj_f32r(_sub(dWk, hI), 4, 8, lambda m: kT[:, m, :])
        proj_f32r(_sub(dWv, hI), 4, 16, lambda m: vFM[:, m, :])
        for tg in range(NCHK):
            for dk4 in range(4):
                tv = ttrans(ptb[:, dk4 * 128:(dk4 + 1) * 128],
                            kT[:, dk4, tg * 128:(tg + 1) * 128], identb[:, :],
                            inc=("st" if dk4 == 3 else None))
            p.wait("s", "st", tv)
            sv_ = sact(kdTM[:, tg, :], ptb[:, 0:512], AF.Copy, scale=djc[:, hI:hI + 1])
            p.wait("t", "ss", sv_)
            for dk4 in range(4):
                tv = ttrans(ptb[:, dk4 * 128:(dk4 + 1) * 128],
                            vFM[:, dk4, tg * 128:(tg + 1) * 128], identb[:, :],
                            inc=("st" if dk4 == 3 else None))
            p.wait("s", "st", tv)
            sv_ = sact(vTM[:, tg, :], ptb[:, 0:512], AF.Copy)
            p.wait("t", "ss", sv_)
        vfm_done = tv
        if hI == 0:
            tap("kT", kT[:, :, :].rearrange("p m t -> p (m t)"))
            tap("vTM", vTM[:, :, :].rearrange("p a b -> p (a b)"))
            tap("kdTM", kdTM[:, :, :].rearrange("p a b -> p (a b)"))
        p.wait("y", "ss", p.n["ss"])
        ld = p.dma("y", "dy", rowA[0:1, :], dE1)
        p.wait("s", "dy", ld)
        se1 = sact(rowC[0:1, :], rowA[0:1, :], AF.Exp, scale=dlrow[0:1, 8 + hI:9 + hI])
        gb = 0
        bank_wait("t", 0)
        for tb in range(NTB):
            tvb = mm(pb[0][:, :], selh[:, hI * 128:(hI + 1) * 128],
                     gtT[0:3, tb * NT:(tb + 1) * NT],
                     start=True, stop=True, inc="st")
            p.wait("v", "st", tvb)
            gb = vcopy(bc[2][:, tb * NT:(tb + 1) * NT], pb[0][:, :], inc="sv")
            p.wait("t", "sv", gb)
        p.wait("v", "st", vfm_done)
        if hI == 0 and "gateb" in dbg_t:
            tap("gateb", bc[2][:, :])
            p.wait("v", "dg", p.n["dg"])
        for k4 in range(4):
            vtt(qg[:, k4, :], qT[:, k4, :], bc[2][:, :], AL.mult)
        p.wait("t", "ss", se1)
        for tb in range(NTB):
            tvb = mm(pb[0][:, :], ones1[0:1, :], rowC[0:1, tb * NT:(tb + 1) * NT],
                     start=True, stop=True, inc="st")
            p.wait("v", "st", tvb)
            vcb = vcopy(bc[2][:, tb * NT:(tb + 1) * NT], pb[0][:, :], inc="sv")
            p.wait("t", "sv", vcb)
        bank_set(0, "sv", vcb)
        for k4 in range(4):
            vqg = vtt(qgd[:, k4, :], qg[:, k4, :], bc[2][:, :], AL.mult, inc="sv")
        p.wait("t", "sv", vqg)
        if hI == 0:
            tap("qg", qg[:, :, :].rearrange("p m t -> p (m t)"))
            tap("qgd", qgd[:, :, :].rearrange("p m t -> p (m t)"))

        for ci, c in enumerate(reversed(range(NCHK))):
            csl = slice(c * CH, (c + 1) * CH)
            bank_wait("t", 0)
            for dk4 in range(4):
                mm(pb[0][:, 0:128], kT[:, dk4, csl], qg[:, dk4, csl],
                   start=(dk4 == 0), stop=(dk4 == 3), inc=("st" if dk4 == 3 else None))
            tsc = p.n["st"]
            p.wait("v", "st", tsc)
            vsc = vtt(sp92[:, 0:128], pb[0][:, 0:128], maskb[:, hI * 128:(hI + 1) * 128],
                      AL.mult, inc="sv")
            bank_set(0, "sv", vsc)
            p.wait("t", "sv", vsc)
            bank_wait("t", 1)
            for dv4 in range(4):
                mm(pb[1][:, dv4 * 128:(dv4 + 1) * 128],
                   vTM[:, c, dv4 * 128:(dv4 + 1) * 128], sp92[:, 0:128],
                   start=True, stop=(ci == 0),
                   inc=("st" if (ci == 0 and dv4 == 3) else None))
                if ci > 0:
                    for dk4 in range(4):
                        mm(pb[1][:, dv4 * 128:(dv4 + 1) * 128],
                           Sbf[:, dk4, dv4 * 128:(dv4 + 1) * 128], qgd[:, dk4, csl],
                           start=False, stop=(dk4 == 3),
                           inc=("st" if (dk4 == 3 and dv4 == 3) else None))
            tre = p.n["st"]
            p.wait("v", "st", tre)
            if hI == 0:
                for dv4 in range(4):
                    vr = vcopy(blendT[:, dv4, csl], pb[1][:, dv4 * 128:(dv4 + 1) * 128],
                               inc=("sv" if dv4 == 3 else None))
            else:
                for dv4 in range(4):
                    vr = vtt(blendT[:, dv4, csl], blendT[:, dv4, csl],
                             pb[1][:, dv4 * 128:(dv4 + 1) * 128], AL.add,
                             inc=("sv" if dv4 == 3 else None))
            bank_set(1, "sv", vr)
            for dk4 in range(4):
                bank_wait("t", 2 + dk4)
                mm(pb[2 + dk4][:, :], kdTM[:, c, dk4 * 128:(dk4 + 1) * 128],
                   vTM[:, c, :], start=True, stop=True,
                   inc="st")
                tsu = p.n["st"]
                p.wait("v", "st", tsu)
                if ci == 0:
                    vu = vcopy(Sbf[:, dk4, :], pb[2 + dk4][:, :], inc="sv")
                else:
                    vu = vstt(Sbf[:, dk4, :], Sbf[:, dk4, :], dCc[:, hI:hI + 1],
                              AL.mult, pb[2 + dk4][:, :], AL.add, inc="sv")
                bank_set(2 + dk4, "sv", vu)
                p.wait("t", "sv", vu)
        p.wait("g", "sv", p.n["sv"])
        bstore.append(p.dma("g", "dg", dBin.ap()[hI * 128:(hI + 1) * 128, :],
                            Sbf[:, :, :].rearrange("p a b -> p (a b)")))
        if hI == 0:
            tap("Sst", Sbf[:, :, :].rearrange("p a b -> p (a b)"))
            tap("blend0", blendT[:, :, :].rearrange("p a b -> p (a b)"))

    p.wait("g", "dg", bstore[-1])
    p.run("g", lambda eng: eng.collective_compute(
        "AllGather", AL.bypass, replica_groups=[list(range(NCORES))],
        ins=[dBin.ap().opt()], outs=[dBout.ap().opt()]), inc="cc")
    p.wait("g", "cc", p.n["cc"])
    ebn_guard = 0
    for hI in range(NH):
        p.wait("s", "sv", p.n["sv"])
        swr = sact(rowA[0:1, 0:8], bxrow[0:1, 0:8], AF.Exp,
                   scale=dlrow[0:1, 8 + hI:9 + hI])
        p.wait("v", "ss", swr)
        vw = vtt(rowA[0:1, 8:16], rowA[0:1, 0:8], bxrow[0:1, 8:16], AL.mult,
                 inc="sv")
        p.wait("t", "sv", vw)
        bank_wait("t", 0)
        twb = mm(pb[0][:, 0:8], ones1[0:1, :], rowA[0:1, 8:16], start=True, stop=True,
                 inc="st")
        p.wait("v", "st", twb)
        vwc = vcopy(wcs[:, 0:8], pb[0][:, 0:8], inc="sv")
        bank_set(0, "sv", vwc)
        for r in range(NCORES):
            lb = p.dma("g", "dg", bstmp[:, :],
                       dBout.ap()[(r * NH + hI) * 128:(r * NH + hI + 1) * 128, :])
            p.wait("v", "dg", lb)
            p.wait("v", "sv", vwc)
            if r == 0:
                p.wait("v", "st", ebn_guard)
                vE = vts(Ebn[:, :, :].rearrange("p a b -> p (a b)"), bstmp[:, :],
                         wcs[:, 0:1], AL.mult, inc="sv")
            else:
                vE = vstt(Ebn[:, :, :].rearrange("p a b -> p (a b)"), bstmp[:, :],
                          wcs[:, r:r + 1], AL.mult,
                          Ebn[:, :, :].rearrange("p a b -> p (a b)"), AL.add, inc="sv")
            p.wait("g", "sv", vE)
        if hI == 0:
            tap("Ebn", Ebn[:, :, :].rearrange("p a b -> p (a b)"))
        p.wait("y", "ss", p.n["ss"])
        ld = p.dma("y", "dy", rowB[0:1, :], dE2)
        p.wait("s", "dy", ld)
        se2 = sact(rowC[0:1, :], rowB[0:1, :], AF.Exp, scale=dlrow[0:1, 8 + hI:9 + hI])
        p.wait("t", "ss", se2)
        bank_wait("t", 1)
        for tb in range(NTB):
            tvb = mm(pb[1][:, :], ones1[0:1, :], rowC[0:1, tb * NT:(tb + 1) * NT],
                     start=True, stop=True, inc="st")
            p.wait("v", "st", tvb)
            ve2 = vcopy(bc[2][:, tb * NT:(tb + 1) * NT], pb[1][:, :], inc="sv")
            p.wait("t", "sv", ve2)
        for tb in range(NTB):
            tvb = mm(pb[1][:, :], selh[:, hI * 128:(hI + 1) * 128],
                     gtT[0:3, tb * NT:(tb + 1) * NT],
                     start=True, stop=True, inc="st")
            p.wait("v", "st", tvb)
            vgb = vstt(bc[2][:, tb * NT:(tb + 1) * NT], bc[2][:, tb * NT:(tb + 1) * NT],
                       1.0, AL.mult, pb[1][:, :], AL.mult, inc="sv")
            p.wait("t", "sv", vgb)
        bank_set(1, "sv", vgb)
        p.wait("v", "st", ebn_guard)
        for k4 in range(4):
            vq = vtt(qgd[:, k4, :], qT[:, k4, :], bc[2][:, :], AL.mult, inc="sv")
        p.wait("t", "sv", vq)
        for dv4 in range(4):
            for tb in range(NTB):
                bi = (dv4 * NTB + tb) % 2
                bank_wait("t", bi)
                for dk4 in range(4):
                    mm(pb[bi][:, :], Ebn[:, dk4, dv4 * 128:(dv4 + 1) * 128],
                       qgd[:, dk4, tb * NT:(tb + 1) * NT],
                       start=(dk4 == 0), stop=(dk4 == 3),
                       inc=("st" if dk4 == 3 else None))
                tb_ = p.n["st"]
                p.wait("v", "st", tb_)
                vb = vtt(blendT[:, dv4, tb * NT:(tb + 1) * NT],
                         blendT[:, dv4, tb * NT:(tb + 1) * NT], pb[bi][:, :],
                         AL.add, inc="sv")
                bank_set(bi, "sv", vb)
        ebn_guard = p.n["st"]
    tap("blend", blendT[:, :, :].rearrange("p a b -> p (a b)"))

    p.wait("y", "sv", p.n["sv"])
    p.wait("y", "st", p.n["st"])
    p.dma("y", "dy", lA[:, :], dA.ap())
    lab_load = p.dma("y", "dy", lB[:, :], dB.ap())

    for k4 in range(4):
        vos = vts(blendT[:, k4, :], blendT[:, k4, :], osc[:, 0:1], AL.mult,
                  inc=("sv" if k4 == 3 else None))
    p.wait("t", "sv", vos)

    p.wait("g", "st", max(wst_guard[0], wst_guard[1]))
    for m in range(16):
        gv = wst_guard.get(4 + m % 2, 0)
        p.wait("g", "st", gv)
        wl = p.dma("g", "dg", wbl1[:, m % 2, 0:512], dWo.ap()[m])
        p.wait("t", "dg", wl)
        for tb in range(NTB):
            bi = m % 2
            bank_wait("t", bi)
            for k4 in range(4):
                mm(pb[bi][:, :], wbl1[:, m % 2, k4 * 128:(k4 + 1) * 128],
                   blendT[:, k4, tb * NT:(tb + 1) * NT],
                   start=(k4 == 0), stop=(k4 == 3), inc=("st" if k4 == 3 else None))
            tww = p.n["st"]
            if tb == NTB - 1:
                wst_guard[4 + m % 2] = tww
            p.wait("v", "st", tww)
            vw = vcopy(ao[:, m, tb * NT:(tb + 1) * NT], pb[bi][:, :], inc="sv")
            bank_set(bi, "sv", vw)
    wst_guard[0] = wst_guard[1] = max(wst_guard[4], wst_guard[5])
    tap("ao", ao[:, :, :].rearrange("p k t -> p (k t)"))

    def post_ln_update():
        ln_stats(lambda k: ao[:, k, :], BF16, onescb[:, 0:1],
                 sq_in_tile=lambda k: ao[:, k, :])
        for k in range(KD):
            v1 = vtt(scrA[:, k % 2, :], ao[:, k, :], bc[0][:, :], AL.mult)
            v2 = vtt(scrA[:, k % 2, :], scrA[:, k % 2, :], bc[1][:, :], AL.add)
            v3 = vtt(resid[:, k, :], resid[:, k, :], scrA[:, k % 2, :], AL.add,
                     inc="sv")
        p.wait("t", "sv", p.n["sv"])
        p.wait("s", "sv", p.n["sv"])

    post_ln_update()
    tap("x1", resid[:, :, :].rearrange("p k t -> p (k t)"))

    ln_stats(lambda k: resid[:, k, :], F32R, onesc[:, 0:1])
    rz_cols()
    vrr = p.run("v", lambda eng: eng.reciprocal(rowA[0:1, :], rowB[0:1, :]), inc="sv")
    p.wait("t", "sv", vrr)
    bank_wait("t", 0)
    for tb in range(NTB):
        tvb = mm(pb[0][:, :], ones1[0:1, :], rowA[0:1, tb * NT:(tb + 1) * NT],
                 start=True, stop=True, inc="st")
        p.wait("v", "st", tvb)
        vv = vcopy(bc[2][:, tb * NT:(tb + 1) * NT], pb[0][:, :], inc="sv")
        p.wait("t", "sv", vv)
        bank_set(0, "sv", vv)

    bank_wait("t", 3)
    for tg in range(8):
        for k in range(KD):
            mm(pb[3][:, tg * 8:(tg + 1) * 8], resid[:, k, tg * 128:(tg + 1) * 128],
               rts[:, k * 8:(k + 1) * 8], start=(k == 0), stop=(k == KD - 1),
               inc=("st" if k == KD - 1 else None))
    trl = p.n["st"]
    bank_wait("t", 4)
    for k in range(KD):
        mm(pb[4][0:1, 0:8], onesc[:, 0:1], rts[:, k * 8:(k + 1) * 8],
           start=(k == 0), stop=(k == KD - 1), inc=("st" if k == KD - 1 else None))
    tcr = p.n["st"]
    p.wait("v", "st", tcr)
    vcr = vcopy(rowA[0:1, 0:8], pb[4][0:1, 0:8], inc="sv")
    p.wait("t", "sv", vcr)
    tcb2 = mm(pb[4][:, 0:8], ones1[0:1, :], rowA[0:1, 0:8], start=True, stop=True,
              inc="st")
    p.wait("v", "st", tcb2)
    vcs = vcopy(scrA[:, 0, 0:8], pb[4][:, 0:8], inc="sv")
    bank_set(4, "sv", vcs)
    p.wait("s", "st", trl)
    p.wait("s", "sv", vcs)
    srl = 0
    for tg in range(8):
        srl = sact(rlog[:, tg * 8:(tg + 1) * 8], pb[3][:, tg * 8:(tg + 1) * 8],
                   AF.Copy, scale=rcol[:, tg:tg + 1])
    p.wait("v", "ss", srl)
    for tg in range(8):
        vstt(rlog2[:, tg * 8:(tg + 1) * 8], scrA[:, 0, 0:8], zcol[:, tg:tg + 1],
             AL.mult, rlog[:, tg * 8:(tg + 1) * 8], AL.add)
    vrl = p.n["sv"]
    bank_set(3, "sv", vrl)
    vbounce()
    p.run("v", lambda eng: eng.reduce_max(
        rtop[:, 0:8].rearrange("p (a b) -> p a b", b=1),
        rlog2[:, :].rearrange("p (a b) -> p a b", b=8), axis=AX))
    vbounce()
    for tg in range(8):
        vts(sp92[:, tg * 8:(tg + 1) * 8],
            rlog2[:, tg * 8:(tg + 1) * 8], rtop[:, tg:tg + 1], AL.is_equal)
    vbounce()
    for tg in range(8):
        vstt(scrA[:, 0, tg * 8:(tg + 1) * 8], sp92[:, tg * 8:(tg + 1) * 8], -BIG,
             AL.mult, rlog2[:, tg * 8:(tg + 1) * 8], AL.add)
    vbounce()
    p.run("v", lambda eng: eng.reduce_max(
        rtop[:, 8:16].rearrange("p (a b) -> p a b", b=1),
        scrA[:, 0, 0:64].rearrange("p (a b) -> p a b", b=8), axis=AX))
    vbounce()
    v12 = vtt(rtop[:, 16:24], rtop[:, 0:8], rtop[:, 8:16], AL.subtract, inc="sv")
    p.wait("s", "sv", v12)
    sw1 = sact(rtop[:, 24:32], rtop[:, 16:24], AF.Sigmoid)
    p.wait("v", "ss", sw1)
    vts(gmx[:, 0:8], rtop[:, 24:32], 2.0, AL.mult)
    vts(gmx[:, 8:16], rtop[:, 24:32], -2.0, AL.mult, 2.0, AL.add)
    for tg in range(8):
        vts(scrA[:, 1, tg * 8:(tg + 1) * 8], scrA[:, 0, tg * 8:(tg + 1) * 8],
            rtop[:, 8 + tg:9 + tg], AL.is_equal)
    vbounce()
    for tg in range(8):
        vts(rlog[:, tg * 8:(tg + 1) * 8], sp92[:, tg * 8:(tg + 1) * 8],
            gmx[:, tg:tg + 1], AL.mult)
    vbounce()
    for tg in range(8):
        sl8 = slice(tg * 8, (tg + 1) * 8)
        vwd = vstt(sp92[:, sl8], scrA[:, 1, sl8], gmx[:, 8 + tg:9 + tg],
                   AL.mult, rlog[:, sl8], AL.add, inc="sv")
    p.wait("t", "sv", p.n["sv"])
    for tg in range(8):
        tv = ttrans(ptb[0:8, 0:128], sp92[:, tg * 8:(tg + 1) * 8], identb[:, :],
                    inc="st")
        p.wait("v", "st", tv)
        vv = vcopy(wdt[0:8, tg * 128:(tg + 1) * 128], ptb[0:8, 0:128], inc="sv")
        p.wait("t", "sv", vv)
    tap("wd", wdt[0:8, :], rows=8)

    p.wait("t", "dy", lab_load)
    for tb in range(NTB):
        bi = tb % 2
        bank_wait("t", bi)
        for k in range(KD):
            mm(pb[bi][:, :], lA[:, k * 128:(k + 1) * 128],
               resid[:, k, tb * NT:(tb + 1) * NT],
               start=(k == 0), stop=(k == KD - 1), inc=("st" if k == KD - 1 else None))
        tax = p.n["st"]
        if tb == 0:
            bank_wait("t", 2)
            for k in range(KD):
                mm(pb[2][:, 0:2], lA[:, k * 128:(k + 1) * 128], onesc[:, 0:2],
                   start=(k == 0), stop=(k == KD - 1),
                   inc=("st" if k == KD - 1 else None))
            tca = p.n["st"]
            p.wait("v", "st", tca)
            vca = vcopy(csq[:, 32:33], pb[2][:, 0:1], inc="sv")
            bank_set(2, "sv", vca)
        p.wait("v", "st", tax)
        sl = slice(tb * NT, (tb + 1) * NT)
        vtt(scrA[:, 0, :NT], pb[bi][:, :], bc[0][:, sl], AL.mult)
        vax = vstt(axt[:, sl], bc[1][:, sl], csq[:, 32:33], AL.mult,
                   scrA[:, 0, :NT], AL.add, inc="sv")
        bank_set(bi, "sv", vax)
    tap("axt", axt[:, :])
    for tb in range(NTB):
        bank_wait("t", 0)
        twd = mm(pb[0][:, :], pmat[:, :], wdt[0:8, tb * NT:(tb + 1) * NT],
                 start=True, stop=True, inc="st")
        p.wait("v", "st", twd)
        sl = slice(tb * NT, (tb + 1) * NT)
        vtt(scrA[:, 0, :NT], pb[0][:, :], bc[2][:, sl], AL.mult)
        vwx = vtt(axw[:, sl], axt[:, sl], scrA[:, 0, :NT], AL.mult, inc="sv")
        bank_set(0, "sv", vwx)
    p.wait("t", "sv", p.n["sv"])

    for m in range(16):
        for h in range(2):
            bi = h
            gv = wst_guard.get(bi, 0)
            p.wait("y", "st", gv)
            p.dma("y", "dy", wst[:, bi, :], dWb.ap()[m, :, h * 8 * 128:(h + 1) * 8 * 128])
        wl = p.n["dy"]
        p.wait("t", "dy", wl)
        bank_wait("t", 2)
        for k in range(KD):
            mm(pb[2][:, 0:2], wst[:, 0 if k < 8 else 1, (k % 8) * 128:(k % 8 + 1) * 128],
               onesc[:, 0:2], start=(k == 0), stop=(k == KD - 1),
               inc=("st" if k == KD - 1 else None))
        tcs = p.n["st"]
        p.wait("v", "st", tcs)
        vcsb = vcopy(csq[:, 40 + (m % 8):41 + (m % 8)], pb[2][:, 0:1], inc="sv")
        bank_set(2, "sv", vcsb)
        for tb in range(NTB):
            bi = m % 2
            bank_wait("t", bi)
            for k in range(KD):
                mm(pb[bi][:, :], wst[:, 0 if k < 8 else 1, (k % 8) * 128:(k % 8 + 1) * 128],
                   resid[:, k, tb * NT:(tb + 1) * NT],
                   start=(k == 0), stop=False)
            mm(pb[bi][:, :], lB[:, m * 128:(m + 1) * 128],
               axw[:, tb * NT:(tb + 1) * NT], start=False, stop=True, inc="st")
            tbm = p.n["st"]
            if tb == NTB - 1:
                wst_guard[0] = tbm
                wst_guard[1] = tbm
            p.wait("v", "st", tbm)
            p.wait("v", "sv", vcsb)
            sl = slice(tb * NT, (tb + 1) * NT)
            vtt(scrA[:, 0, :NT], pb[bi][:, :], bc[0][:, sl], AL.mult)
            vbs = vstt(ao[:, m, sl], bc[1][:, sl], csq[:, 40 + (m % 8):41 + (m % 8)],
                       AL.mult, scrA[:, 0, :NT], AL.add, inc="sv")
            bank_set(bi, "sv", vbs)

    post_ln_update()
    tap("x2", resid[:, :, :].rearrange("p k t -> p (k t)"))

    ln_stats(lambda k: resid[:, k, :], F32R, onesc[:, 0:1])
    for k in range(KD):
        vtt(scrA[:, k % 2, :], resid[:, k, :], bc[0][:, :], AL.mult)
        vx5 = vtt(xn5[:, k, :], scrA[:, k % 2, :], bc[1][:, :], AL.add, inc="sv")
    p.wait("t", "sv", vx5)
    tap("xn5", xn5[:, :, :].rearrange("p k t -> p (k t)"))

    q_l1 = ao
    wst_guard[4] = wst_guard[5] = max(wst_guard[0], wst_guard[1])
    p.wait("y", "st", wst_guard[0])
    for dst, src in ((b1c, dB1), (gsc, dGS), (usc, dUS), (dsc, dDS)):
        scl_load = p.dma("y", "dy", dst[:, :], src)
    p.wait("v", "dy", scl_load)
    p.wait("s", "dy", scl_load)
    for j in range(16):
        for part, mofs in (("q", 0), ("k", 16), ("v", 32)):
            m = mofs + j
            p.wait("g", "st", max(wst_guard.get(4, 0), wst_guard.get(5, 0)))
            wl = p.dma("g", "dg", wbl1[:, :, :].rearrange("p h f -> p (h f)"),
                       dW1.ap()[m])
            p.wait("t", "dg", wl)
            for tb in range(NTB):
                bi = (m * NTB + tb) % 2
                bank_wait("t", bi)
                for k in range(KD):
                    mm(pb[bi][:, :], wbl1[:, 0 if k < 8 else 1, (k % 8) * 128:(k % 8 + 1) * 128],
                       xn5[:, k, tb * NT:(tb + 1) * NT],
                       start=(k == 0), stop=(k == KD - 1),
                       inc=("st" if k == KD - 1 else None))
                tqk = p.n["st"]
                if tb == NTB - 1:
                    wst_guard[4] = tqk
                    wst_guard[5] = tqk
                sl = slice(tb * NT, (tb + 1) * NT)
                if part == "q":
                    p.wait("v", "st", tqk)
                    vq1 = vts(q_l1[:, j, sl], pb[bi][:, :], b1c[:, m:m + 1], AL.add,
                              inc="sv")
                    bank_set(bi, "sv", vq1)
                elif part == "k":
                    p.wait("s", "st", tqk)
                    sk = sact(kscr[:, sl], pb[bi][:, :], AF.Identity,
                              bias=b1c[:, m:m + 1])
                    bank_set(bi, "ss", sk)
                else:
                    p.wait("s", "st", tqk)
                    svv = sact(vscr[:, sl], pb[bi][:, :], AF.Identity,
                               bias=b1c[:, m:m + 1])
                    bank_set(bi, "ss", svv)
        p.wait("s", "sv", p.n["sv"])
        sq2 = sact(sp92[:, 0:T], q_l1[:, j, :], AF.Square, accum=acc[:, j:j + 1])
        sk2 = sact(sp92[:, 0:T], kscr[:, :], AF.Square, accum=acc[:, 16 + j:17 + j])
        p.wait("v", "ss", sk2)
        vkv = vtt(sp92[:, T:2 * T], kscr[:, :], vscr[:, :], AL.mult)
        vkr = p.run("v", lambda eng, j=j: eng.reduce_sum(
            acc[:, 32 + j:33 + j].rearrange("p (a b) -> p a b", b=1),
            sp92[:, T:2 * T].rearrange("p (a b) -> p a b", b=T), axis=AX), inc="sv")
        p.wait("s", "sv", vkr)

    p.wait("g", "sv", p.n["sv"])
    p.wait("g", "ss", p.n["ss"])
    arin = p.dma("g", "dg", dRin.ap(), acc[:, 0:48])
    p.wait("g", "dg", arin)
    p.run("g", lambda eng: eng.collective_compute(
        "AllReduce", AL.add, replica_groups=[list(range(NCORES))],
        ins=[dRin.ap().opt()], outs=[dRout.ap().opt()]), inc="cc")
    p.wait("g", "cc", p.n["cc"])
    grl = p.dma("g", "dg", acc[:, 48:96], dRout.ap())
    p.wait("v", "dg", grl)
    vnn = vtt(gst[:, 0:16], acc[:, 48:64], acc[:, 64:80], AL.mult, inc="sv")
    p.wait("s", "sv", vnn)
    snn = sact(gst[:, 16:32], gst[:, 0:16], AF.Sqrt)
    p.wait("v", "ss", snn)
    vrc2 = p.run("v", lambda eng: eng.reciprocal(gst[:, 32:48], gst[:, 16:32]),
                 inc="sv")
    p.wait("s", "sv", vrc2)
    scp = sact(gst[:, 48:64], gst[:, 32:48], AF.Copy)
    p.wait("v", "ss", scp)
    vcf = vtt(gst[:, 0:16], acc[:, 80:96], gst[:, 48:64], AL.mult, inc="sv")
    p.wait("s", "sv", vcf)
    scb = sact(cbf[:, 0:16], gst[:, 0:16], AF.Copy)
    sc2 = sact(cbf[:, 16:32], gst[:, 0:16], AF.Square)
    p.wait("t", "ss", sc2)
    p.wait("v", "ss", sc2)
    tap("qL1", q_l1[:, :, :].rearrange("p k t -> p (k t)"))
    tap("cfac", gst[:, 0:16])

    bank_wait("t", 4)
    bank_wait("t", 5)
    for tb in range(NTB):
        for k in range(KD):
            mm(pb[4][0:1, :], cbf[:, k:k + 1], q_l1[:, k, tb * NT:(tb + 1) * NT],
               start=(k == 0), stop=(k == KD - 1), inc=("st" if k == KD - 1 else None))
        p.wait("v", "st", p.n["st"])
        vsx = vcopy(rowA[0:1, tb * NT:(tb + 1) * NT], pb[4][0:1, :], inc="sv")
        p.wait("t", "sv", vsx)
    for tb in range(NTB):
        for k in range(KD):
            gvq = sq_guard[k % 2]
            p.wait("s", "st", gvq)
            sqq = sact(sp92[:, (k % 2) * NT:(k % 2 + 1) * NT],
                       q_l1[:, k, tb * NT:(tb + 1) * NT], AF.Square)
            p.wait("t", "ss", sqq)
            gv2 = mm(pb[5][0:1, :], cbf[:, 16 + k:17 + k],
                     sp92[:, (k % 2) * NT:(k % 2 + 1) * NT],
                     start=(k == 0), stop=(k == KD - 1), inc="st")
            sq_guard[k % 2] = gv2
        p.wait("v", "st", p.n["st"])
        vsq = vcopy(rowB[0:1, tb * NT:(tb + 1) * NT], pb[5][0:1, :], inc="sv")
        p.wait("t", "sv", vsq)
    vts(rowA[0:1, :], rowA[0:1, :], 1.0 / D, AL.mult)
    vts(rowB[0:1, :], rowB[0:1, :], 1.0 / D, AL.mult)
    vstt(rowC[0:1, :], rowA[0:1, :], -1.0, AL.mult, rowA[0:1, :], AL.mult)
    v1 = vtt(rowB[0:1, :], rowB[0:1, :], rowC[0:1, :], AL.add)
    p.wait("s", "sv", v1)
    s1 = sact(rowB[0:1, :], rowB[0:1, :], AF.Ln, bias=EPS)
    s2 = sact(rowB[0:1, :], rowB[0:1, :], AF.Exp, scale=-0.5)
    p.wait("v", "ss", s2)
    vz7 = vstt(rowC[0:1, :], rowA[0:1, :], -1.0, AL.mult, rowB[0:1, :], AL.mult,
               inc="sv")
    p.wait("t", "sv", vz7)
    rows = [rowB, rowC]
    for jj in range(2):
        bank_wait("t", jj)
        for tb in range(NTB):
            tm = mm(pb[jj][:, :], ones1[0:1, :], rows[jj][0:1, tb * NT:(tb + 1) * NT],
                    start=True, stop=True, inc="st")
            p.wait("v", "st", tm)
            vv = vcopy(bc[jj][:, tb * NT:(tb + 1) * NT], pb[jj][:, :], inc="sv")
            p.wait("t", "sv", vv)
        bank_set(jj, "sv", vv)
    for k in range(KD):
        vstt(scrA[:, k % 2, :], q_l1[:, k, :], gst[:, k:k + 1], AL.mult,
             bc[0][:, :], AL.mult)
        vtt(scrA[:, k % 2, :], scrA[:, k % 2, :], bc[1][:, :], AL.add)
        vup = vtt(resid[:, k, :], resid[:, k, :], scrA[:, k % 2, :], AL.add, inc="sv")
    p.wait("t", "sv", vup)
    p.wait("s", "sv", vup)
    tap("x3", resid[:, :, :].rearrange("p k t -> p (k t)"))

    ln_stats(lambda k: resid[:, k, :], F32R, onesc[:, 0:1])
    for k in range(KD):
        vtt(scrA[:, k % 2, :], resid[:, k, :], bc[0][:, :], AL.mult)
        vx6 = vtt(xn6[:, k, :], scrA[:, k % 2, :], bc[1][:, :], AL.add, inc="sv")
    p.wait("y", "sv", vx6)
    p.wait("t", "sv", vx6)
    tap("xn6", xn6[:, :, :].rearrange("p k t -> p (k t)"))
    spv = p.dma("y", "dy", dSpill.ap(), resid[:, :, :].rearrange("p k t -> p (k t)"))
    p.wait("v", "dy", spv)
    p.wait("g", "sv", p.n["sv"])
    p.wait("g", "ss", p.n["ss"])
    p.wait("g", "dy", spv)

    qchunk_seen = -1
    for m in range(MI):
        bi = m % 2
        gv = wst_guard.get(6 + bi, 0)
        p.wait("g", "st", gv)
        qc = min(m // stepg, 3)
        if qc > qchunk_seen:
            p.wait("g", "dg", qmile_gu[qc])
            qchunk_seen = qc
        lg = p.dma("g", "dg", gwb[:, bi, :], dQ8g.ap()[m])
        lu = p.dma("g", "dg", uwb[:, bi, :], dQ8u.ap()[m])
        p.wait("t", "dg", lu)
        for tb in range(NTB):
            b0 = (m * NTB + tb) % 2
            b1_ = 2 + (m * NTB + tb) % 2
            bank_wait("t", b0)
            for k in range(KD):
                mm(pb[b0][:, :], gwb[:, bi, k * 128:(k + 1) * 128],
                   xn6[:, k, tb * NT:(tb + 1) * NT],
                   start=(k == 0), stop=(k == KD - 1), inc=("st" if k == KD - 1 else None))
            tgate = p.n["st"]
            bank_wait("t", b1_)
            for k in range(KD):
                mm(pb[b1_][:, :], uwb[:, bi, k * 128:(k + 1) * 128],
                   xn6[:, k, tb * NT:(tb + 1) * NT],
                   start=(k == 0), stop=(k == KD - 1), inc=("st" if k == KD - 1 else None))
            tup = p.n["st"]
            if tb == NTB - 1:
                wst_guard[6 + bi] = tup
            sl = slice(tb * NT, (tb + 1) * NT)
            p.wait("s", "st", tgate)
            sg_ = sact(sgs[:, :], pb[b0][:, :], AF.Silu, scale=gsc[:, m:m + 1])
            bank_set(b0, "ss", sg_)
            p.wait("v", "st", tup)
            p.wait("v", "ss", sg_)
            vfu = vstt(fuse[:, m, sl], pb[b1_][:, :], usc[:, m:m + 1], AL.mult,
                       sgs[:, :], AL.mult, inc="sv")
            bank_set(b1_, "sv", vfu)
            p.wait("s", "sv", vfu)
    tap("fuse", fuse[:, 0:4, :].rearrange("p a b -> p (a b)"))

    p.wait("g", "sv", p.n["sv"])
    p.wait("g", "ss", p.n["ss"])
    wst_guard[6] = wst_guard[7] = max(wst_guard[6], wst_guard[7])
    dchunk_seen = -1
    for mo in range(16):
        bi = mo % 2
        gv = wst_guard.get(6 + bi, 0) if mo < 2 else dwb_guard.get(bi, 0)
        p.wait("g", "st", gv)
        if mo // 4 > dchunk_seen:
            p.wait("g", "dg", qmile_d[mo // 4])
            dchunk_seen = mo // 4
        ld = p.dma("g", "dg", dwb[:, bi, :], dQ8d.ap()[mo])
        p.wait("t", "dg", ld)
        for tb in range(NTB):
            b0 = (mo * NTB + tb) % 4
            bank_wait("t", b0)
            for ki in range(MI):
                mm(pb[b0][:, :], dwb[:, bi, ki * 128:(ki + 1) * 128],
                   fuse[:, ki, tb * NT:(tb + 1) * NT],
                   start=(ki == 0), stop=(ki == MI - 1),
                   inc=("st" if ki == MI - 1 else None))
            tdn = p.n["st"]
            if tb == NTB - 1:
                dwb_guard[bi] = tdn
            p.wait("s", "st", tdn)
            sdn = sact(ao[:, mo, tb * NT:(tb + 1) * NT], pb[b0][:, :], AF.Copy,
                       scale=dsc[:, mo:mo + 1])
            bank_set(b0, "ss", sdn)
    p.wait("v", "ss", p.n["ss"])
    tap("ffo", ao[:, :, :].rearrange("p k t -> p (k t)"))
    last_dn = p.n["st"]
    p.wait("y", "st", last_dn)
    p.wait("s", "st", last_dn)
    rlv = p.dma("y", "dy", resid[:, :, :].rearrange("p k t -> p (k t)"), dSpill.ap())
    p.wait("v", "dy", rlv)
    p.wait("t", "dy", rlv)

    post_ln_update()

    p.wait("y", "sv", p.n["sv"])
    ov = p.dma("y", "dy", dout.ap(), resid[:, :, :].rearrange("p k t -> p (k t)"))
    p.wait("y", "dy", ov)
    p.wait("g", "dg", p.n["dg"])

    lp = nc.allow_low_precision(
        reason="f32r intermediates (fp22 reads) are well inside the "
               "validated tolerance; accumulations stay fp32 in PSUM")
    lp.__enter__()
    with nc.Block() as block:
        @block.tensor
        def _(eng):
            for f in p.ops["t"]:
                f(eng)

        @block.vector
        def _(eng):
            for f in p.ops["v"]:
                f(eng)

        @block.scalar
        def _(eng):
            for f in p.ops["s"]:
                f(eng)

        @block.gpsimd
        def _(eng):
            for f in p.ops["g"]:
                f(eng)

        @block.sync
        def _(eng):
            for f in p.ops["y"]:
                f(eng)

    lp.__exit__(None, None, None)
    stack.close()
    return nc


def _sub(dram, i):
    class W:
        def ap(self):
            return dram.ap()[i]
    return W()


def host_inputs(inputs):
    x = np.asarray(inputs["L0_attn_x"])[0]
    Wq = _retile(np.asarray(inputs["L0_attn_Wq"]))
    Wk = np.stack([_retile(np.asarray(inputs["L0_attn_Wk"])[i]) for i in range(NH)])
    Wv = np.stack([_retile(np.asarray(inputs["L0_attn_Wv"])[i]) for i in range(NH)])
    Wo = _retile(np.asarray(inputs["L0_attn_Wo"]))
    Wg0 = np.asarray(inputs["L0_attn_Wg"]).reshape(KD, 128, 3)
    Wg = np.zeros((128, KD * 4), dtype=np.float32)
    Wg.reshape(128, KD, 4)[:, :, 0:3] = Wg0.transpose(1, 0, 2)
    dl = np.zeros((1, 4), dtype=np.float32)
    dl[0, 0:3] = np.asarray(inputs["L0_attn_decay_logits"])
    osc_ = np.asarray(inputs["L0_attn_out_scale"]).reshape(1, 1)
    Wb = _retile(np.ascontiguousarray(np.asarray(inputs["L0_ffn_weight"]).T))
    A = np.asarray(inputs["L0_ffn_lora_A"])
    Ap = np.ascontiguousarray(_retile(np.ascontiguousarray(A.reshape(128, D).T))
                              .reshape(128, D))
    B = np.asarray(inputs["L0_ffn_lora_B"])
    Bp = np.ascontiguousarray(
        _retile(np.ascontiguousarray(B.transpose(0, 2, 1).reshape(128, D)))
        .transpose(1, 0, 2).reshape(128, 16 * 128))
    Rt = np.ascontiguousarray(np.asarray(inputs["L0_ffn_router_w"]).T
                              .reshape(KD, 128, 8).transpose(1, 0, 2)
                              .reshape(128, KD * 8))
    W1 = _retile(np.asarray(inputs["L1_attn_W"]))
    b1 = _cols(np.asarray(inputs["L1_attn_b"]))
    GW = _retile(np.asarray(inputs["L1_ffn_gate_w"]))
    UW = _retile(np.asarray(inputs["L1_ffn_up_w"]))
    DW = _retile(np.asarray(inputs["L1_ffn_down_w"]))
    GS = _cols(np.asarray(inputs["L1_ffn_gate_scale"]))
    US = _cols(np.asarray(inputs["L1_ffn_up_scale"]))
    DS = _cols(np.asarray(inputs["L1_ffn_down_scale"]))

    import ml_dtypes
    identb = np.eye(128, dtype=ml_dtypes.bfloat16)
    identr = np.eye(128, dtype=np.float32)
    s_i = np.arange(128)
    expnT = np.maximum(s_i[:, None] - s_i[None, :] - 1, 0).astype(np.float32)
    causT = (s_i[:, None] > s_i[None, :]).astype(np.float32)
    tpos = np.arange(T)
    e1 = (127 - (tpos % 128)).astype(np.float32).reshape(1, T)
    e2 = (1023 - tpos).astype(np.float32).reshape(1, T)
    iotac = np.stack([np.arange(128, dtype=np.float32),
                      np.full(128, 128.0, dtype=np.float32)], axis=1)
    onesc = np.ones((128, 2), dtype=np.float32)
    ones1 = np.ones((1, 128), dtype=np.float32)
    ones1b = np.ones((1, 128), dtype=ml_dtypes.bfloat16)
    pm = (np.arange(128)[None, :] // 16 == np.arange(8)[:, None]).astype(ml_dtypes.bfloat16)
    sh = np.zeros((3, NH * 128), dtype=ml_dtypes.bfloat16)
    for hh in range(NH):
        sh[hh, hh * 128:(hh + 1) * 128] = 1.0
    cn = np.tile(np.array([[EPS, -0.5]], dtype=np.float32), (128, 1))

    common = dict(Wq=Wq, Wk=Wk, Wv=Wv, Wo=Wo, Wg=Wg, decay_logits=dl, out_scale=osc_,
                  Wbase=Wb, loraA=Ap, loraB=Bp, router=Rt, W1=W1, b1=b1,
                  gatew=GW, upw=UW, downw=DW, gates=GS, ups=US, downs=DS,
                  identb=identb, identr=identr, expnT=expnT, causT=causT,
                  e1row=e1, e2row=e2, iotac=iotac, onesc=onesc, ones1=ones1,
                  ones1b=ones1b, pmat=pm, selh=sh, cnsts=cn)
    in_maps = []
    for r in range(NCORES):
        xs = np.ascontiguousarray(
            x[r * T:(r + 1) * T, :].T.reshape(KD, 128, T)
            .transpose(1, 0, 2).reshape(128, KD * T))
        bexpo = np.array([[max(1024.0 * (rp - r - 1), 0.0) for rp in range(8)]],
                         dtype=np.float32)
        bmask = np.array([[1.0 if rp > r else 0.0 for rp in range(8)]],
                         dtype=np.float32)
        m = dict(common)
        m.update(x=xs, bexpo=bexpo, bmask=bmask)
        in_maps.append(m)
    return in_maps


def kernel(**inputs):
    in_maps = host_inputs(inputs)
    nc = build_program()
    res = run_bass_kernel_spmd(nc, in_maps, core_ids=list(range(NCORES)))
    outs = []
    for r in range(NCORES):
        o = np.asarray(res.results[r]["out"])
        outs.append(o.reshape(128, KD, T).transpose(2, 1, 0).reshape(T, D))
    return np.concatenate(outs, axis=0)[None, :, :].astype(np.float32)
